# version 06
# baseline (speedup 1.0000x reference)
import numpy as np
import concourse.bass as bass
import concourse.mybir as mybir
from contextlib import ExitStack

F32 = mybir.dt.float32
BF16 = mybir.dt.bfloat16
U8 = mybir.dt.uint8
AF = mybir.ActivationFunctionType
ALU = mybir.AluOpType
AXX = mybir.AxisListType.X

ENGINES = ("sync", "tensor", "vector", "scalar", "gpsimd")

D = 2048; T = 2048; TO = 1024; NDC = 16
H = 16; HD = 128; NG = 4; M = 4; CH = 128; NSC = 8
HF = 5632; NHC = 44
SCALE = float(HD) ** -0.5
LN_EPS = 1e-5
EPS_ATT = 1e-6
NFFN = float(HF * D)


class Ch:
    __slots__ = ("sem", "n", "name", "dma")
    def __init__(self, name, dma=False):
        self.sem = None; self.n = 0; self.name = name; self.dma = dma


class Prog:
    def __init__(self, nc):
        self.nc = nc
        self.streams = {e: [] for e in ENGINES}
        self.chans = []
        self.seen = {e: {} for e in ENGINES}

    def chan(self, name, dma=False):
        ch = Ch(name, dma); self.chans.append(ch); return ch

    def emit(self, eng, fn, waits=(), inc=None, inc_amt=1, fwaits=()):
        waits = [w for w in waits if w is not None]
        fwaits = [w for w in fwaits if w is not None]
        seen = self.seen[eng]
        needed = list(fwaits)
        for (ch, v) in fwaits:
            if seen.get(ch, 0) < v:
                seen[ch] = v
        for (ch, v) in waits:
            if seen.get(ch, 0) < v:
                needed.append((ch, v)); seen[ch] = v
        tok = None
        if inc is not None:
            inc.n += inc_amt
            tok = (inc, inc.n)
            if not inc.dma and seen.get(inc, 0) < inc.n:
                seen[inc] = inc.n
        self.streams[eng].append((needed, fn, (inc, inc_amt) if inc else None))
        return tok

    def dma(self, eng, out, in_, waits=(), inc=None):
        assert inc is not None
        return self.emit(eng, lambda e: e.dma_start(out=out, in_=in_),
                         waits=waits, inc=inc, inc_amt=16)

    def barrier(self, eng, waits):
        self.emit(eng, None, waits=waits)

    def finalize(self, block):
        def mk(eng_name):
            stream = self.streams[eng_name]
            def body(e):
                for needed, fn, inc in stream:
                    for (ch, v) in needed:
                        e.wait_ge(ch.sem, v)
                    if fn is None:
                        continue
                    ins = fn(e)
                    if inc is not None:
                        ch, amt = inc
                        ins.then_inc(ch.sem, amt)
            return body
        for eng_name in ENGINES:
            if self.streams[eng_name]:
                getattr(block, eng_name)(mk(eng_name))


class Arena:
    def __init__(self, ap, nbytes):
        self.ap = ap; self.nbytes = nbytes; self.off = 0

    def alloc(self, cols, dtype, name=""):
        isz = 2 if dtype == BF16 else 4
        self.off = (self.off + 3) & ~3
        nb = cols * isz
        assert self.off + nb <= self.nbytes, f"arena overflow {name}: {self.off+nb}>{self.nbytes}"
        v = self.ap[:, self.off:self.off + nb].bitcast(dtype)
        self.off += nb
        return v


class Bank:
    __slots__ = ("ap", "rd", "wr")
    def __init__(self, ap):
        self.ap = ap; self.rd = None; self.wr = None


def build(dbg_names=(), stop_after=99):
    nc = bass.Bass("TRN2", target_bir_lowering=False)

    def din(name, shape, dt=F32):
        return nc.dram_tensor(name, shape, dt, kind="ExternalInput").ap()

    xctx = din("xctx", [D * T])
    wqd = din("wq", [NG * NDC * 128 * 512])
    wkd = din("wk", [NG * NDC * 128 * 512])
    wvd = din("wv", [NG * NDC * 128 * 512])
    wod = din("wo", [16 * 128 * 16 * 128])
    w1d = din("w1", [NHC * 128 * NDC * 128])
    w2d = din("w2", [NHC * 128 * NDC * 128])
    w3d = din("w3", [16 * 128 * NHC * 128])
    memd = din("mems", [H * 128 * M * 128])
    normd = din("normst", [128, 64])
    gated = din("gater", [128, 16])
    cstfd = din("cstf", [128, 390])
    cstbd = din("cstb", [128, 386], BF16)

    outd = nc.dram_tensor("out", [NDC * 128 * TO], F32, kind="ExternalOutput").ap()

    dbg_d = {}
    def want_dbg(name, shape):
        if name in dbg_names:
            dbg_d[name] = nc.dram_tensor("dbg_" + name, shape, F32, kind="ExternalOutput").ap()
            return dbg_d[name]
        return None

    es = ExitStack()
    SB_BYTES = 206 * 1024
    ar_t = es.enter_context(nc.sbuf_tensor("arena", [128, SB_BYTES], U8))
    pb_t = [es.enter_context(nc.psum_tensor(f"pb{i}", [128, 512], F32)) for i in range(8)]
    B = [Bank(t[:]) for t in pb_t]

    A = Arena(ar_t[:], SB_BYTES)
    xn_pre = A.alloc(NDC * 1024, BF16, "xn_pre")
    xn_own = A.alloc(NDC * 1024, BF16, "xn_own")
    R1ap = ar_t[:, A.off:A.off + 91 * 1024]
    R1 = Arena(R1ap, 91 * 1024)
    A.off += 91 * 1024
    SAap = ar_t[:, A.off:A.off + 24 * 1024]
    A.off += 24 * 1024
    A128 = A.alloc(2048, BF16, "A128")
    B128 = A.alloc(2048, BF16, "B128")
    sxn = A.alloc(16, F32, "sxn")
    sxn_b = A.alloc(16, BF16, "sxn_b")
    cb = A.alloc(386, BF16, "cb")
    cf = A.alloc(390, F32, "cf")
    gsum = A.alloc(4, F32, "gsum")
    rtmp = A.alloc(4, F32, "rtmp")
    gacc = A.alloc(2, F32, "gacc")
    th128 = A.alloc(4, F32, "th128")
    thn128 = A.alloc(4, F32, "thn128")
    g128 = A.alloc(16, F32, "g128")
    gi128 = A.alloc(16, F32, "gi128")
    w128 = A.alloc(16, F32, "w128")
    sq_sb = A.alloc(4, BF16, "sq_sb")
    normstb = A.alloc(64, BF16, "normstb")
    nsb = A.alloc(4, BF16, "nsb")
    nf4 = A.alloc(4, F32, "nf4")
    relP = A.alloc(16, BF16, "relP")
    r_sc = A.alloc(4 * 32, F32, "r_sc")
    Sst = A.alloc(2 * 132, F32, "Sst")
    Sst_b = A.alloc(2 * 132, BF16, "Sst_b")
    Scur_b = A.alloc(2 * 132, BF16, "Scur_b")
    kpf = A.alloc(2 * 128, BF16, "kpf")
    scm = A.alloc(2 * 128, BF16, "scm")
    denb = A.alloc(2 * 2, F32, "denb")
    elut = A.alloc(2 * 512, F32, "elut")
    sgb = A.alloc(2 * 512, BF16, "sgb")
    memo = A.alloc(2 * 1024, BF16, "memo")
    rows = A.alloc(512, F32, "rows")
    rows2 = A.alloc(64, F32, "rows2")
    rowsb = A.alloc(64, BF16, "rowsb")
    assert A.off <= SB_BYTES, A.off

    kp = R1.alloc(16 * 512, BF16, "kp")
    vt = R1.alloc(16 * 512, BF16, "vt")
    sjq = R1.alloc(4 * 1024, BF16, "sq")
    comb = R1.alloc(8 * 2048, BF16, "comb")
    gbuf = [R1.alloc(1408, F32, "gbuf0"), R1.alloc(1408, F32, "gbuf1")]
    hidden = Arena(R1ap, R1.nbytes).alloc(NHC * 1024, BF16, "hidden")
    comb_fm = Arena(R1ap, 40 * 1024).alloc(16 * 1024, BF16, "comb_fm")
    attn_fm = xn_pre

    ones_b = cb[:, 0:1]
    id_b = cb[:, 1:129]
    onesrow_b = cb[0:1, 257:385]
    id_f = cf[:, 0:128]
    onesrow_f = cf[0:1, 128:256]
    mask_f = cf[:, 257:258]
    lneps_f = cf[0:1, 258:259]
    half_f = cf[:, 259:260]
    tril_f = cf[:, 260:388]

    P = Prog(nc)
    qS = P.chan("qS", dma=True); qG = P.chan("qG", dma=True)
    qO = [P.chan("qO0", dma=True), P.chan("qO1", dma=True)]
    qCb = P.chan("qCb", dma=True); qCf = P.chan("qCf", dma=True); qGt = P.chan("qGt", dma=True)
    qNrm = P.chan("qNrm", dma=True)
    qGm = [P.chan("qGm0", dma=True), P.chan("qGm1", dma=True)]
    qXb = [P.chan("qXb0", dma=True), P.chan("qXb1", dma=True)]
    qWk = [P.chan("qWk0", dma=True), P.chan("qWk1", dma=True)]
    qMb = [P.chan("qMb0", dma=True), P.chan("qMb1", dma=True)]
    qWo = [P.chan("qWo0", dma=True), P.chan("qWo1", dma=True)]
    qWf = [[P.chan(f"qWf{m}{i}", dma=True) for i in range(2)] for m in range(2)]
    qW3 = [P.chan("qW30", dma=True), P.chan("qW31", dma=True)]
    qXo = [P.chan("qXo0", dma=True), P.chan("qXo1", dma=True)]
    tC = P.chan("tC"); vC = P.chan("vC"); aC = P.chan("aC"); gC = P.chan("gC")

    t_cb = P.dma("sync", cb, cstbd, inc=qCb)
    t_cf = P.dma("sync", cf, cstfd, inc=qCf)
    t_nrm = P.dma("gpsimd", normstb, normd, inc=qNrm)
    t_gate = P.dma("sync", rows2[:, 0:16], gated, inc=qGt)
    t_g = P.emit("scalar", lambda s: s.activation(g128, rows2[:, 0:16], AF.Sigmoid), waits=[t_gate], inc=aC)
    t_gi = P.emit("vector", lambda v: v.tensor_scalar(gi128, g128, -1.0, 1.0, ALU.mult, ALU.add), waits=[t_g], inc=vC)
    t_gz = P.emit("gpsimd", lambda g: g.memset(gsum, 0.0), inc=gC)

    GT_PER = 64
    gamma_tiles = [(j, wd, t) for j, wd in enumerate((w1d, w2d, w3d)) for t in range(GT_PER)]
    gst = {"i": 0, "last_rd": [None, None], "last_ac": [None, None], "tok": None}

    def gamma_pump(n):
        for _ in range(n):
            i = gst["i"]
            if i >= len(gamma_tiles):
                return
            gst["i"] = i + 1
            j, wd, t = gamma_tiles[i]
            bi = i % 2
            gb = gbuf[bi]
            off = t * 128 * 1408
            t_ld = P.dma("sync", gb, wd[off:off + 128 * 1408].rearrange("(p c) -> p c", p=128),
                         waits=[gst["last_rd"][bi]], inc=qGm[bi])
            t_rd = P.emit("vector", lambda v, gb=gb, bi=bi: v.scalar_tensor_tensor(
                gb, gb, -1.0, gb, ALU.mult, ALU.max, accum_out=gacc[:, bi:bi+1]),
                          waits=[t_ld, t_gz, gst["last_ac"][bi]], inc=vC)
            t_ac = P.emit("gpsimd", lambda g, j=j, bi=bi: g.tensor_tensor(gsum[:, j:j+1], gsum[:, j:j+1], gacc[:, bi:bi+1], ALU.add),
                          waits=[t_rd], inc=gC)
            gst["last_rd"][bi] = t_rd
            gst["last_ac"][bi] = t_ac
            gst["tok"] = t_ac

    def xview(dc):
        return xctx[dc * 128 * T:(dc + 1) * 128 * T].rearrange("(p t) -> p t", p=128)

    xpark = Arena(R1ap, 64 * 1024).alloc(NDC * 2048, BF16, "xpark")
    SA = Arena(SAap, 24 * 1024)
    xb = [SA.alloc(2048, BF16, "xb0"), SA.alloc(2048, BF16, "xb1")]
    x2b = [SA.alloc(2048, BF16, "x2b0"), SA.alloc(2048, BF16, "x2b1")]
    arow_b = SA.alloc(2048, BF16, "arow_b")
    brow_b = SA.alloc(2048, BF16, "brow_b")
    assert SA.off <= 24 * 1024

    last_mm = [None, None]
    stat_last = None
    xpark_tok = {}
    for dc in range(NDC):
        bi = dc % 2
        xp = xpark[:, 2048*dc:2048*(dc+1)]
        t_x = P.dma("gpsimd", xp, xview(dc), waits=[], inc=qXb[bi])
        xpark_tok[dc] = t_x
        t_sq = P.emit("vector", lambda v, bi=bi, xp=xp: v.tensor_mul(x2b[bi], xp, xp),
                      waits=[t_x, last_mm[bi]], inc=vC)
        for tt in range(4):
            st, sp = dc == 0, dc == NDC - 1
            P.emit("tensor", lambda t, xp=xp, tt=tt, st=st, sp=sp: t.matmul(
                B[tt].ap[0:1, :], ones_b, xp[:, 512*tt:512*(tt+1)], start=st, stop=sp),
                waits=[t_x, t_cb], inc=tC)
            stat_last = P.emit("tensor", lambda t, bi=bi, tt=tt, st=st, sp=sp: t.matmul(
                B[4+tt].ap[0:1, :], ones_b, x2b[bi][:, 512*tt:512*(tt+1)], start=st, stop=sp),
                waits=[t_sq], inc=tC)
        last_mm[bi] = stat_last
        gamma_pump(1)

    t_fin = None
    for tt in range(4):
        t_mn = P.emit("vector", lambda v, tt=tt: v.tensor_scalar(
            B[tt].ap[0:1, :], B[tt].ap[0:1, :], 1.0/D, None, ALU.mult),
            waits=[stat_last], inc=vC)
        t_e2 = P.emit("vector", lambda v, tt=tt: v.tensor_scalar(
            B[4+tt].ap[0:1, :], B[4+tt].ap[0:1, :], 1.0/D, None, ALU.mult), inc=vC)
        t_ms = P.emit("scalar", lambda s, tt=tt: s.activation(
            rows[0:1, 0:512], B[tt].ap[0:1, :], AF.Square), waits=[t_mn], inc=aC)
        t_vr = P.emit("vector", lambda v, tt=tt: v.tensor_sub(
            B[4+tt].ap[0:1, :], B[4+tt].ap[0:1, :], rows[0:1, 0:512]), waits=[t_ms, t_e2], inc=vC)
        t_sq2 = P.emit("scalar", lambda s, tt=tt: s.activation(
            B[4+tt].ap[0:1, :], B[4+tt].ap[0:1, :], AF.Sqrt, bias=lneps_f),
            waits=[t_vr, t_cf], inc=aC)
        t_rs = P.emit("vector", lambda v, tt=tt: v.reciprocal(
            B[4+tt].ap[0:1, :], B[4+tt].ap[0:1, :]), waits=[t_sq2], inc=vC)
        t_a = P.emit("vector", lambda v, tt=tt: v.tensor_copy(
            arow_b[0:1, 512*tt:512*(tt+1)], B[4+tt].ap[0:1, :]), fwaits=[t_rs], inc=vC)
        t_mc = P.emit("vector", lambda v, tt=tt: v.tensor_copy(rows[0:1, 0:512], B[tt].ap[0:1, :]), inc=vC)
        t_b = P.emit("vector", lambda v, tt=tt: v.scalar_tensor_tensor(
            brow_b[0:1, 512*tt:512*(tt+1)], rows[0:1, 0:512], -1.0, B[4+tt].ap[0:1, :],
            ALU.mult, ALU.mult), waits=[t_mc], inc=vC)
        t_fin = t_b
        B[tt].rd = t_b
        B[4+tt].rd = t_b
    t_ab = t_fin
    t_bbb = t_fin
    bc_last = None
    for tt in range(4):
        t1 = P.emit("tensor", lambda t, tt=tt: t.matmul(
            B[tt].ap, onesrow_b, arow_b[0:1, 512*tt:512*(tt+1)], start=True, stop=True),
            waits=[t_ab, B[tt].rd], inc=tC)
        t2 = P.emit("tensor", lambda t, tt=tt: t.matmul(
            B[4+tt].ap, onesrow_b, brow_b[0:1, 512*tt:512*(tt+1)], start=True, stop=True),
            waits=[t_bbb, B[4+tt].rd], inc=tC)
        e1 = P.emit("vector", lambda v, tt=tt: v.tensor_copy(A128[:, 512*tt:512*(tt+1)], B[tt].ap), waits=[t1], inc=vC)
        e2 = P.emit("vector", lambda v, tt=tt: v.tensor_copy(B128[:, 512*tt:512*(tt+1)], B[4+tt].ap), waits=[t2], inc=vC)
        B[tt].rd, B[tt].wr = e1, t1
        B[4+tt].rd, B[4+tt].wr = e2, t2
        bc_last = e2
    gamma_pump(2)

    last_use = [None, None]
    sxn_toks = []
    xn_tok = {}
    for dc in range(NDC):
        bi = dc % 2
        xp = xpark[:, 2048*dc:2048*(dc+1)]
        hts = []
        for hfi, dst in ((0, xn_pre), (1, xn_own)):
            sl = slice(1024*hfi, 1024*(hfi+1))
            t_p1 = P.emit("vector", lambda v, bi=bi, sl=sl, xp=xp: v.tensor_tensor(
                x2b[bi][:, sl], xp[:, sl], A128[:, sl], ALU.mult), waits=[xpark_tok[dc], bc_last, last_use[bi]], inc=vC)
            t_p2 = P.emit("vector", lambda v, bi=bi, sl=sl, dst=dst, dc=dc: v.tensor_tensor(
                dst[:, 1024*dc:1024*(dc+1)], x2b[bi][:, sl], B128[:, sl], ALU.add),
                waits=[t_p1], inc=vC)
            hts.append(t_p2)
        xn_tok[dc] = hts
        t_r1 = P.emit("vector", lambda v, dc=dc: v.tensor_reduce(rtmp[:, 1:2], xn_pre[:, 1024*dc:1024*(dc+1)], AXX, ALU.add), waits=[hts[0]], inc=vC)
        t_r2 = P.emit("vector", lambda v, dc=dc: v.tensor_reduce(rtmp[:, 2:3], xn_own[:, 1024*dc:1024*(dc+1)], AXX, ALU.add), waits=[hts[1]], inc=vC)
        t_r3 = P.emit("vector", lambda v, dc=dc: v.tensor_tensor(sxn[:, dc:dc+1], rtmp[:, 1:2], rtmp[:, 2:3], ALU.add), fwaits=[t_r2], inc=vC)
        sxn_toks.append(t_r3)
        last_use[bi] = hts[1]
        gamma_pump(1)
    t_sxnb = P.emit("vector", lambda v: v.tensor_copy(sxn_b, sxn), waits=[sxn_toks[-1]], inc=vC)

    dbq = P.chan("dbq", dma=True)
    dbg_xn = want_dbg("xn", [NDC * 128, T])
    if dbg_xn is not None:
        for dc in range(NDC):
            P.dma("gpsimd", dbg_xn[128*dc:128*(dc+1), 0:1024], xn_pre[:, 1024*dc:1024*(dc+1)], waits=xn_tok[dc], inc=dbq)
            P.dma("gpsimd", dbg_xn[128*dc:128*(dc+1), 1024:2048], xn_own[:, 1024*dc:1024*(dc+1)], waits=xn_tok[dc], inc=dbq)
    dbg_sxn = want_dbg("sxn", [128, 16])
    if dbg_sxn is not None:
        P.dma("gpsimd", dbg_sxn, sxn, waits=sxn_toks, inc=dbq)
    dbg_a = want_dbg("a128", [128, 2048])
    if dbg_a is not None:
        P.dma("gpsimd", dbg_a, A128, waits=[bc_last], inc=dbq)
    dbg_bb = want_dbg("b128", [128, 2048])
    if dbg_bb is not None:
        P.dma("gpsimd", dbg_bb, B128, waits=[bc_last], inc=dbq)
    dbg_ar = want_dbg("arow", [1, 2048])
    if dbg_ar is not None:
        P.dma("gpsimd", dbg_ar, arow_b[0:1, :], waits=[t_fin], inc=dbq)
    dbg_br = want_dbg("brow", [1, 2048])
    if dbg_br is not None:
        P.dma("gpsimd", dbg_br, brow_b[0:1, :], waits=[t_fin], inc=dbq)
    if stop_after < 1:
        return _fin(nc, P, es, outd, dbg_d, locals())

    SC = Arena(SAap, 24 * 1024)
    wkb = [SC.alloc(4*512, BF16, f"wkb{i}") for i in range(4)]
    membuf = [SC.alloc(512, BF16, "mem0"), SC.alloc(512, BF16, "mem1")]
    assert SC.off <= 24 * 1024
    wkb_tok = [None, None, None, None]
    qWk4 = [P.chan(f"qWk4_{i}", dma=True) for i in range(4)]

    def wslice(wd, g, dc0, ndc):
        off = (g * NDC + dc0) * 128 * 512
        return wd[off:off + ndc * 128 * 512].rearrange("(a p c) -> p a c", a=ndc, p=128)

    def elu_evac(bk, dst, par, extra=()):
        t_min = P.emit("vector", lambda v, bk=bk, par=par: v.tensor_scalar(
            elut[:, 512*par:512*(par+1)], bk.ap, 0.0, None, ALU.min),
            waits=[bk.wr] + list(extra), inc=vC)
        t_exp = P.emit("scalar", lambda s, par=par: s.activation(
            elut[:, 512*par:512*(par+1)], elut[:, 512*par:512*(par+1)], AF.Exp),
            waits=[t_min], inc=aC)
        t_f = P.emit("vector", lambda v, bk=bk, dst=dst, par=par: v.scalar_tensor_tensor(
            dst, bk.ap, 1.0, elut[:, 512*par:512*(par+1)], ALU.add, ALU.max),
            waits=[t_exp], inc=vC)
        bk.rd = t_f
        return t_f

    kp_rdy = [None] * 16
    vt_rdy = [None] * 16
    sq_rdy = [None] * 4
    group_end = None
    t_er = t_ier = None
    e_row = rows2[0:1, 16:20]
    ie_row = rows2[0:1, 20:24]
    t_w128 = None
    t_sqs = None

    dbg_kp = want_dbg("kp", [16*128, 512])
    dbg_vt = want_dbg("vt", [16*128, 512])
    dbg_sq = want_dbg("sigq", [4*128, 1024])
    dbg_w = want_dbg("wmh", [128, 16])
    dbg_comb = want_dbg("comb", [8*128, 2048])
    dbg_st = want_dbg("sst", [4*128, 129])

    ngroups = NG if stop_after >= 3 else 1
    for g in range(ngroups):
        prev_end = group_end
        for wd, dstbuf, is_k, rdy in ((wkd, kp, True, kp_rdy), (wvd, vt, False, vt_rdy)):
            for tcb in range(4):
                banks = [B[(tcb % 2) * 4 + i] for i in range(4)]
                first = [True] * 4
                mm_last = None
                for dcb in range(4):
                    bi = dcb % 2 if False else (tcb * 4 + dcb) % 4
                    t_w = P.dma("gpsimd", wkb[bi][:].rearrange("p (a c) -> p a c", a=4),
                                wslice(wd, g, 4*dcb, 4), waits=[wkb_tok[bi]], inc=qWk4[bi])
                    for dci in range(4):
                        dc = 4*dcb + dci
                        for i4 in range(4):
                            tcn = 4*tcb + i4
                            xsrc = xn_pre if tcn < 8 else xn_own
                            xcol = 1024*dc + 128*(tcn % 8)
                            bkk = banks[i4]
                            w_ = [t_w, xn_tok[dc][0 if tcn < 8 else 1]]
                            if first[i4]:
                                w_.append(bkk.rd)
                            mm_last = P.emit("tensor", lambda t, bkk=bkk, xsrc=xsrc, xcol=xcol, bi=bi, dci=dci, st=first[i4], sp=(dc == 15): t.matmul(
                                bkk.ap, xsrc[:, xcol:xcol+128], wkb[bi][:, 512*dci:512*(dci+1)],
                                start=st, stop=sp), waits=w_, inc=tC)
                            first[i4] = False
                            bkk.wr = mm_last
                    wkb_tok[bi] = mm_last
                for i4 in range(4):
                    tcn = 4*tcb + i4
                    bkk = banks[i4]
                    dst = dstbuf[:, 512*tcn:512*(tcn+1)]
                    if is_k:
                        rdy[tcn] = elu_evac(bkk, dst, i4 % 2)
                    else:
                        rdy[tcn] = P.emit("vector", lambda v, bkk=bkk, dst=dst: v.tensor_copy(dst, bkk.ap), waits=[bkk.wr], inc=vC)
                        bkk.rd = rdy[tcn]
                gamma_pump(2)
        if g == 0 and dbg_kp is not None:
            for tcn in range(16):
                P.dma("gpsimd", dbg_kp[128*tcn:128*(tcn+1), :], kp[:, 512*tcn:512*(tcn+1)], waits=[kp_rdy[tcn]], inc=dbq)
        if g == 0 and dbg_vt is not None:
            for tcn in range(16):
                P.dma("gpsimd", dbg_vt[128*tcn:128*(tcn+1), :], vt[:, 512*tcn:512*(tcn+1)], waits=[vt_rdy[tcn]], inc=dbq)

        sqbk = B[4]
        for p2 in (0, 1):
            qbs = [B[0], B[1], B[2], B[3]] if p2 == 0 else [B[5], B[6], B[7], B[0]]
            firstq = [True] * 4
            firstsq = True
            for dcb in range(4):
                bi = (p2 * 4 + dcb) % 4
                t_w = P.dma("gpsimd", wkb[bi][:].rearrange("p (a c) -> p a c", a=4),
                            wslice(wqd, g, 4*dcb, 4), waits=[wkb_tok[bi]], inc=qWk4[bi])
                mm_last = None
                for dci in range(4):
                    dc = 4*dcb + dci
                    for ei in range(2):
                        e128 = 2*p2 + ei
                        for ct in range(2):
                            qi = 2*ei + ct
                            bkk = qbs[qi]
                            w_ = [t_w, xn_tok[dc][1]]
                            if firstq[qi]:
                                w_.append(bkk.rd)
                            ecol = 512*dci + 128*(2*p2 + ei)
                            mm_last = P.emit("tensor", lambda t, bkk=bkk, bi=bi, ecol=ecol, dc=dc, ct=ct, st=firstq[qi], sp=(dc == 15): t.matmul(
                                bkk.ap, wkb[bi][:, ecol:ecol+128],
                                xn_own[:, 1024*dc + 512*ct:1024*dc + 512*(ct+1)],
                                start=st, stop=sp), waits=w_, inc=tC)
                            firstq[qi] = False
                            bkk.wr = mm_last
                            wkb_tok[bi] = mm_last
                    if p2 == 0:
                        w_ = [t_sxnb]
                        if firstsq:
                            w_.append(sqbk.rd)
                        mm_last = P.emit("tensor", lambda t, bi=bi, dci=dci, dc=dc, st=firstsq, sp=(dc == 15): t.matmul(
                            sqbk.ap[0:1, :], sxn_b[:, dc:dc+1], wkb[bi][:, 512*dci:512*(dci+1)],
                            start=st, stop=sp), waits=w_, inc=tC)
                        firstsq = False
                        sqbk.wr = mm_last
                        wkb_tok[bi] = mm_last
            for qi in range(4):
                e128 = 2*p2 + (qi // 2)
                ct = qi % 2
                bkk = qbs[qi]
                sq_rdy[e128] = elu_evac(bkk, sjq[:, 1024*e128 + 512*ct:1024*e128 + 512*(ct+1)], qi % 2)
            if p2 == 0:
                t_cp = P.emit("vector", lambda v: v.tensor_copy(rows[0:1, 0:512], sqbk.ap[0:1, :]), waits=[sqbk.wr], inc=vC)
                sqbk.rd = t_cp
                tt_last = None
                for hh in range(4):
                    tt_last = P.emit("tensor", lambda t, hh=hh: t.transpose(
                        sqbk.ap[:, hh:hh+1], rows[0:1, 128*hh:128*(hh+1)], id_f[0:1, 0:1]),
                        waits=[t_cp, t_cf], inc=tC)
                sqbk.wr = tt_last
                t_sqs = P.emit("vector", lambda v: v.tensor_scalar(
                    sq_sb, sqbk.ap[:, 0:4], SCALE / T, None, ALU.mult), waits=[tt_last], inc=vC)
                sqbk.rd = t_sqs
        gamma_pump(4)
        if g == 0 and dbg_sq is not None:
            for hh in range(4):
                P.dma("gpsimd", dbg_sq[128*hh:128*(hh+1), :], sjq[:, 1024*hh:1024*(hh+1)], waits=[sq_rdy[hh]], inc=dbq)

        if g == 0:
            nst = []
            for m in range(M):
                nst.append(P.emit("vector", lambda v, m=m: v.tensor_reduce(
                    nf4[:, m:m+1], normstb[:, m:64:4], AXX, ALU.add), waits=[t_nrm], inc=vC))
            t_nsb = P.emit("vector", lambda v: v.tensor_copy(nsb, nf4), fwaits=[nst[-1]], inc=vC)
            t_nmm = P.emit("tensor", lambda t: t.matmul(B[1].ap[0:1, 0:4], ones_b, nsb, start=True, stop=True),
                           waits=[t_nsb, B[1].rd], inc=tC)
            B[1].wr = t_nmm
            t_er = P.emit("vector", lambda v: v.tensor_scalar(e_row, B[1].ap[0:1, 0:4], 1e-6, None, ALU.is_lt), waits=[t_nmm], inc=vC)
            B[1].rd = t_er
            t_ier = P.emit("vector", lambda v: v.tensor_scalar(ie_row, e_row, -1.0, 1.0, ALU.mult, ALU.add), waits=[t_er], inc=vC)
        t_P = P.emit("vector", lambda v, g=g: v.tensor_tensor(
            relP[:].rearrange("p (a b) -> p a b", a=4),
            normstb[:, 16*g:16*(g+1)].rearrange("p (a b) -> p a b", a=4),
            sq_sb[:].unsqueeze(2).broadcast_to([128, 4, 4]), ALU.mult), waits=[t_nrm], fwaits=[t_sqs], inc=vC)
        t_rmm = P.emit("tensor", lambda t: t.matmul(B[1].ap[0:1, 4:20], ones_b, relP, start=True, stop=True),
                       waits=[t_P, B[1].rd], inc=tC)
        B[1].wr = t_rmm
        RL = rows2[0:1, 32:48]
        RLv = RL.rearrange("p (a b) -> p a b", a=4)
        MX = rows2[0:1, 48:52]
        M2 = rows2[0:1, 52:56]
        t_rl = P.emit("vector", lambda v: v.tensor_tensor(
            RLv, B[1].ap[0:1, 4:20].rearrange("p (a b) -> p a b", a=4),
            ie_row.unsqueeze(1).broadcast_to([1, 4, 4]), ALU.mult), waits=[t_rmm, t_ier], inc=vC)
        B[1].rd = t_rl
        t_rl2 = P.emit("vector", lambda v: v.scalar_tensor_tensor(
            RLv, e_row.unsqueeze(1).broadcast_to([1, 4, 4]), -1e30, RLv, ALU.mult, ALU.add), fwaits=[t_rl], inc=vC)
        t_m1 = P.emit("vector", lambda v: v.tensor_tensor(MX, RL[:, 0:16:4], RL[:, 1:16:4], ALU.max), fwaits=[t_rl2], inc=vC)
        t_m2 = P.emit("vector", lambda v: v.tensor_tensor(M2, RL[:, 2:16:4], RL[:, 3:16:4], ALU.max), inc=vC)
        t_m3 = P.emit("vector", lambda v: v.tensor_tensor(MX, MX, M2, ALU.max), fwaits=[t_m2], inc=vC)
        t_sb = P.emit("vector", lambda v: v.tensor_tensor(
            RLv, RLv, MX.unsqueeze(2).broadcast_to([1, 4, 4]), ALU.subtract), fwaits=[t_m3], inc=vC)
        t_ex = P.emit("scalar", lambda s: s.activation(RL, RL, AF.Exp), waits=[t_sb], inc=aC)
        t_s1 = P.emit("vector", lambda v: v.tensor_tensor(MX, RL[:, 0:16:4], RL[:, 1:16:4], ALU.add), waits=[t_ex], inc=vC)
        t_s2 = P.emit("vector", lambda v: v.tensor_tensor(M2, RL[:, 2:16:4], RL[:, 3:16:4], ALU.add), inc=vC)
        t_s3 = P.emit("vector", lambda v: v.tensor_tensor(MX, MX, M2, ALU.add), fwaits=[t_s2], inc=vC)
        t_rc = P.emit("vector", lambda v: v.reciprocal(MX, MX), fwaits=[t_s3], inc=vC)
        t_wn = P.emit("vector", lambda v: v.tensor_tensor(
            RLv, RLv, MX.unsqueeze(2).broadcast_to([1, 4, 4]), ALU.mult), fwaits=[t_rc], inc=vC)
        t_wm = P.emit("vector", lambda v: v.tensor_tensor(
            rowsb[0:1, 0:16].rearrange("p (a b) -> p a b", a=4), RLv,
            ie_row.unsqueeze(1).broadcast_to([1, 4, 4]), ALU.mult), fwaits=[t_wn], inc=vC)
        t_wbc = P.emit("tensor", lambda t: t.matmul(B[1].ap[:, 32:48], onesrow_b, rowsb[0:1, 0:16], start=True, stop=True),
                       waits=[t_wm, B[1].rd], inc=tC)
        B[1].wr = t_wbc
        t_w128 = P.emit("vector", lambda v: v.tensor_copy(w128, B[1].ap[:, 32:48]), waits=[t_wbc], inc=vC)
        B[1].rd = t_w128
        if g == 0 and dbg_w is not None:
            P.dma("gpsimd", dbg_w, w128, waits=[t_w128], inc=dbq)

        memtok = [(tC, tC.n), (tC, tC.n)]
        t_sstb_all = None
        for pair in range(2):
            hpair = [2*pair, 2*pair+1]
            sst_tok = {}
            for hh in hpair:
                h = 4*g + hh
                par = hh % 2
                t_mem = P.dma("gpsimd", membuf[par], memd[h*128*512:(h+1)*128*512].rearrange("(p c) -> p c", p=128),
                              waits=[memtok[par]], inc=qMb[par])
                bS = B[6 + par]
                mmz = None
                for j in range(8):
                    kcol = 512*j + 128*hh
                    w_ = [kp_rdy[j], vt_rdy[j]]
                    if j == 0:
                        w_.append(bS.rd)
                    P.emit("tensor", lambda t, bS=bS, kcol=kcol, st=(j == 0): t.matmul(
                        bS.ap[:, 0:128], kp[:, kcol:kcol+128], vt[:, kcol:kcol+128],
                        start=st, stop=False), waits=w_, inc=tC)
                    mmz = P.emit("tensor", lambda t, bS=bS, kcol=kcol, sp=(j == 7): t.matmul(
                        bS.ap[:, 128:129], kp[:, kcol:kcol+128], ones_b,
                        start=False, stop=sp), inc=tC)
                bS.wr = mmz
                t_sst = P.emit("scalar", lambda s, bS=bS, par=par: s.activation(
                    Sst[:, 132*par:132*par+129], bS.ap[:, 0:129], AF.Copy, scale=mask_f),
                    waits=[mmz, t_cf, prev_end], inc=aC)
                bS.rd = t_sst
                sst_tok[hh] = t_sst
                t_sstb_all = P.emit("vector", lambda v, par=par: v.tensor_copy(
                    Sst_b[:, 132*par:132*par+129], Sst[:, 132*par:132*par+129]), waits=[t_sst], inc=vC)
                if g == 0 and dbg_st is not None:
                    P.dma("gpsimd", dbg_st[128*hh:128*(hh+1), :], Sst[:, 132*par:132*par+129], waits=[t_sst], inc=dbq)

                bA = B[par]
                bN = B[2]
                nmm = None
                for sc in range(NSC):
                    w_ = [sq_rdy[hh], t_nrm]
                    if sc == 0:
                        w_.append(bN.rd)
                    nmm = P.emit("tensor", lambda t, sc=sc, hh=hh, g=g: t.matmul(
                        bN.ap[:, 32*hh + 4*sc:32*hh + 4*sc + 4],
                        sjq[:, 1024*hh + 128*sc:1024*hh + 128*sc + 128],
                        normstb[:, 4*(4*g+hh):4*(4*g+hh)+4], start=True, stop=True),
                        waits=w_, inc=tC)
                bN.wr = nmm
                t_mx = P.emit("vector", lambda v, hh=hh, bN=bN: v.tensor_scalar(
                    r_sc[:, 32*hh:32*(hh+1)], bN.ap[:, 32*hh:32*(hh+1)], EPS_ATT, None, ALU.max),
                    waits=[nmm], inc=vC)
                bN.rd = t_mx
                t_rp = P.emit("vector", lambda v, hh=hh: v.reciprocal(
                    r_sc[:, 32*hh:32*(hh+1)], r_sc[:, 32*hh:32*(hh+1)]), fwaits=[t_mx], inc=vC)
                t_rw = t_rp
                for m in range(M):
                    t_rw = P.emit("vector", lambda v, hh=hh, m=m: v.tensor_scalar(
                        r_sc[:, 32*hh+m:32*(hh+1):4], r_sc[:, 32*hh+m:32*(hh+1):4],
                        w128[:, 4*hh+m:4*hh+m+1], None, ALU.mult),
                        waits=[t_w128], fwaits=[t_rw] if m == 0 else [], inc=vC)
                t_drain = t_rw
                amm = None
                for sc in range(NSC):
                    for m in range(M):
                        w_ = [t_mem, sq_rdy[hh]]
                        if m == 0:
                            w_.append(bA.rd)
                        amm = P.emit("tensor", lambda t, sc=sc, m=m, hh=hh, bA=bA, par=par: t.matmul(
                            bA.ap[:, 128*m:128*(m+1)],
                            sjq[:, 1024*hh + 128*sc:1024*hh + 128*sc + 128],
                            membuf[par][:, 128*m:128*(m+1)], start=True, stop=True),
                            waits=w_, inc=tC)
                    bA.wr = amm
                    t_c = P.emit("vector", lambda v, sc=sc, hh=hh, par=par, bA=bA: v.tensor_scalar(
                        memo[:, 1024*par + 128*sc:1024*par + 128*(sc+1)], bA.ap[:, 0:128],
                        r_sc[:, 32*hh + 4*sc:32*hh + 4*sc + 1], None, ALU.mult),
                        waits=[amm], fwaits=[t_drain] if sc == 0 else [], inc=vC)
                    for m in range(1, M):
                        t_c = P.emit("vector", lambda v, sc=sc, hh=hh, par=par, m=m, bA=bA: v.scalar_tensor_tensor(
                            memo[:, 1024*par + 128*sc:1024*par + 128*(sc+1)], bA.ap[:, 128*m:128*(m+1)],
                            r_sc[:, 32*hh + 4*sc + m:32*hh + 4*sc + m + 1],
                            memo[:, 1024*par + 128*sc:1024*par + 128*(sc+1)], ALU.mult, ALU.add),
                            inc=vC)
                    bA.rd = t_c
                memtok[par] = amm

            state_tok = {hh: None for hh in hpair}
            for j in range(NSC):
                for hh in hpair:
                    par = hh % 2
                    h = 4*g + hh
                    tcn = 8 + j
                    kcol = 512*tcn + 128*hh
                    qcol = 1024*hh + 128*j
                    bT = B[par]; bSc = B[2 + par]; bNm = B[4 + par]; bOw = B[6 + par]
                    t_t1 = P.emit("tensor", lambda t, bT=bT, kcol=kcol: t.transpose(
                        bT.ap.bitcast(BF16)[:, 0:128], kp[:, kcol:kcol+128], id_b),
                        waits=[bT.rd, kp_rdy[tcn]], inc=tC)
                    bT.wr = t_t1
                    t_kpf = P.emit("vector", lambda v, bT=bT, par=par: v.tensor_copy(
                        kpf[:, 128*par:128*(par+1)], bT.ap.bitcast(BF16)[:, 0:128]), waits=[t_t1], inc=vC)
                    bT.rd = t_kpf
                    t_t2 = P.emit("tensor", lambda t, bSc=bSc, par=par, qcol=qcol: t.matmul(
                        bSc.ap[:, 0:128], kpf[:, 128*par:128*(par+1)], sjq[:, qcol:qcol+128],
                        start=True, stop=True), waits=[t_kpf, bSc.rd, sq_rdy[hh]], inc=tC)
                    bSc.wr = t_t2
                    t_scm = P.emit("vector", lambda v, bSc=bSc, par=par: v.tensor_tensor(
                        scm[:, 128*par:128*(par+1)], bSc.ap[:, 0:128], tril_f, ALU.mult),
                        waits=[t_t2], inc=vC)
                    bSc.rd = t_scm
                    if state_tok[hh] is None:
                        Sop = Sst_b[:, 132*par:132*par+129]
                        s_tok = t_sstb_all
                    else:
                        Sop = Scur_b[:, 132*par:132*par+129]
                        s_tok = state_tok[hh]
                    t_t3 = P.emit("tensor", lambda t, bNm=bNm, qcol=qcol, Sop=Sop: t.matmul(
                        bNm.ap[:, 0:128], sjq[:, qcol:qcol+128], Sop[:, 0:128],
                        start=True, stop=False), waits=[s_tok, bNm.rd], inc=tC)
                    t_t4 = P.emit("tensor", lambda t, bNm=bNm, par=par, kcol=kcol: t.matmul(
                        bNm.ap[:, 0:128], scm[:, 128*par:128*(par+1)], vt[:, kcol:kcol+128],
                        start=False, stop=True), waits=[t_scm, vt_rdy[tcn]], inc=tC)
                    t_t5 = P.emit("tensor", lambda t, bNm=bNm, qcol=qcol, Sop=Sop: t.matmul(
                        bNm.ap[:, 128:129], sjq[:, qcol:qcol+128], Sop[:, 128:129],
                        start=False, stop=False), inc=tC)
                    t_t6 = P.emit("tensor", lambda t, bNm=bNm, par=par: t.matmul(
                        bNm.ap[:, 128:129], scm[:, 128*par:128*(par+1)], ones_b,
                        start=False, stop=True), inc=tC)
                    bNm.wr = t_t6
                    w_ = [kp_rdy[tcn], vt_rdy[tcn]]
                    if j == 0:
                        w_.append(bOw.rd)
                    t_t7 = P.emit("tensor", lambda t, bOw=bOw, kcol=kcol, st=(j == 0): t.matmul(
                        bOw.ap[:, 0:128], kp[:, kcol:kcol+128], vt[:, kcol:kcol+128],
                        start=st, stop=False), waits=w_, inc=tC)
                    t_t8 = P.emit("tensor", lambda t, bOw=bOw, kcol=kcol, sp=(j == NSC-1): t.matmul(
                        bOw.ap[:, 128:129], kp[:, kcol:kcol+128], ones_b,
                        start=False, stop=sp), inc=tC)
                    bOw.wr = t_t8
                    if j < NSC - 1:
                        t_su = P.emit("vector", lambda v, par=par, bOw=bOw: v.tensor_tensor(
                            Scur_b[:, 132*par:132*par+129], Sst[:, 132*par:132*par+129], bOw.ap[:, 0:129], ALU.add),
                            waits=[t_t8, sst_tok[hh]], inc=vC)
                        state_tok[hh] = t_su
                        bOw.rd = t_su
                    t_d1 = P.emit("vector", lambda v, bNm=bNm, par=par: v.tensor_scalar(
                        denb[:, 2*par:2*par+1], bNm.ap[:, 128:129], EPS_ATT, None, ALU.max),
                        waits=[t_t6], inc=vC)
                    t_d2 = P.emit("vector", lambda v, par=par: v.reciprocal(
                        denb[:, 2*par:2*par+1], denb[:, 2*par:2*par+1]), fwaits=[t_d1], inc=vC)
                    t_d3 = P.emit("gpsimd", lambda g, par=par, h=h: g.tensor_scalar(
                        denb[:, 2*par:2*par+1], denb[:, 2*par:2*par+1], gi128[:, h:h+1], None, ALU.mult),
                        waits=[t_gi, t_d2], inc=gC)
                    ccol = 2048*j + 128*h
                    t_c1 = P.emit("vector", lambda v, bNm=bNm, par=par, ccol=ccol: v.tensor_scalar(
                        comb[:, ccol:ccol+128], bNm.ap[:, 0:128], denb[:, 2*par:2*par+1], None, ALU.mult),
                        waits=[t_d3], inc=vC)
                    t_c2 = P.emit("vector", lambda v, par=par, j=j, h=h, ccol=ccol: v.scalar_tensor_tensor(
                        comb[:, ccol:ccol+128], memo[:, 1024*par + 128*j:1024*par + 128*(j+1)],
                        g128[:, h:h+1], comb[:, ccol:ccol+128], ALU.mult, ALU.add),
                        waits=[t_g], inc=vC)
                    bNm.rd = t_c2
                gamma_pump(1)
        group_end = P.emit("vector", lambda v: v.tensor_copy(rtmp[:, 3:4], rtmp[:, 3:4]), inc=vC)

    if dbg_comb is not None:
        for sc in range(NSC):
            P.dma("gpsimd", dbg_comb[128*sc:128*(sc+1), :], comb[:, 2048*sc:2048*(sc+1)], waits=[group_end], inc=dbq)
    if stop_after < 4:
        return _fin(nc, P, es, outd, dbg_d, locals())

    SD = Arena(SAap, 24 * 1024)
    wob = [SD.alloc(2048, BF16, "wob0"), SD.alloc(2048, BF16, "wob1")]
    for sc in range(NSC):
        for ec in range(16):
            par = (sc * 16 + ec) % 2
            bT = B[par]
            t_tp = P.emit("tensor", lambda t, bT=bT, sc=sc, ec=ec: t.transpose(
                bT.ap.bitcast(BF16)[:, 0:128], comb[:, 2048*sc + 128*ec:2048*sc + 128*(ec+1)], id_b),
                waits=[bT.rd, group_end], inc=tC)
            bT.wr = t_tp
            t_ev = P.emit("vector", lambda v, bT=bT, sc=sc, ec=ec: v.tensor_copy(
                comb_fm[:, 1024*ec + 128*sc:1024*ec + 128*(sc+1)], bT.ap.bitcast(BF16)[:, 0:128]),
                waits=[t_tp], inc=vC)
            bT.rd = t_ev
    cfm_all = P.emit("vector", lambda v: v.tensor_copy(rtmp[:, 3:4], rtmp[:, 3:4]), inc=vC)

    wo_tok = [(tC, tC.n), (tC, tC.n)]
    attn_rdy = {}
    for fc in range(16):
        bi = fc % 2
        t_w = P.dma("gpsimd", wob[bi], wod[fc*128*2048:(fc+1)*128*2048].rearrange("(p c) -> p c", p=128),
                    waits=[wo_tok[bi]], inc=qWo[bi])
        banks = [B[2 + 2*(fc % 2)], B[3 + 2*(fc % 2)]]
        mm_last = None
        for ct in range(2):
            first = True
            for ec in range(16):
                w_ = [t_w, cfm_all]
                if first:
                    w_.append(banks[ct].rd)
                mm_last = P.emit("tensor", lambda t, bkk=banks[ct], bi=bi, ec=ec, ct=ct, st=first, sp=(ec == 15): t.matmul(
                    bkk.ap, wob[bi][:, 128*ec:128*(ec+1)],
                    comb_fm[:, 1024*ec + 512*ct:1024*ec + 512*(ct+1)],
                    start=st, stop=sp), waits=w_, inc=tC)
                first = False
            banks[ct].wr = mm_last
        wo_tok[bi] = mm_last
        for ct in range(2):
            t_ev = P.emit("vector", lambda v, bkk=banks[ct], fc=fc, ct=ct: v.tensor_copy(
                attn_fm[:, 1024*fc + 512*ct:1024*fc + 512*(ct+1)], bkk.ap), waits=[banks[ct].wr], inc=vC)
            banks[ct].rd = t_ev
            attn_rdy[fc] = t_ev
    dbg_attn = want_dbg("attn", [16*128, 1024])
    if dbg_attn is not None:
        for fc in range(16):
            P.dma("gpsimd", dbg_attn[128*fc:128*(fc+1), :], attn_fm[:, 1024*fc:1024*(fc+1)], waits=[attn_rdy[fc]], inc=dbq)
    if stop_after < 5:
        return _fin(nc, P, es, outd, dbg_d, locals())

    gamma_pump(10**6)
    t_tg = P.emit("tensor", lambda t: t.transpose(B[6].ap[0:4, 0:128], gsum, id_f),
                  waits=[gst["tok"], B[6].rd, t_cf], inc=tC)
    B[6].wr = t_tg
    t_gr = P.emit("vector", lambda v: v.tensor_reduce(rows2[0:4, 24:25], B[6].ap[0:4, 0:128], AXX, ALU.add),
                  waits=[t_tg], inc=vC)
    B[6].rd = t_gr
    t_th = P.emit("vector", lambda v: v.tensor_scalar(rows2[0:4, 25:26], rows2[0:4, 24:25], 0.5/NFFN, 0.5e-8, ALU.mult, ALU.add),
                  fwaits=[t_gr], inc=vC)
    t_tb = P.emit("tensor", lambda t: t.transpose(B[6].ap[0:1, 4:8], rows2[0:4, 25:26], id_f[0:4, 0:4]),
                  waits=[t_th, B[6].rd], inc=tC)
    B[6].wr = t_tb
    t_cpr = P.emit("vector", lambda v: v.tensor_copy(rows2[0:1, 26:30], B[6].ap[0:1, 4:8]), waits=[t_tb], inc=vC)
    B[6].rd = t_cpr
    t_bcm = P.emit("tensor", lambda t: t.matmul(B[6].ap[:, 8:12], onesrow_f, rows2[0:1, 26:30], start=True, stop=True),
                   waits=[t_cpr, B[6].rd], inc=tC)
    B[6].wr = t_bcm
    t_th128 = P.emit("vector", lambda v: v.tensor_copy(th128, B[6].ap[:, 8:12]), waits=[t_bcm], inc=vC)
    B[6].rd = t_th128
    t_thn = P.emit("vector", lambda v: v.tensor_scalar(thn128, th128, -1.0, None, ALU.mult), fwaits=[t_th128], inc=vC)
    dbg_th = want_dbg("th", [128, 4])
    if dbg_th is not None:
        P.dma("gpsimd", dbg_th, th128, waits=[t_th128], inc=dbq)

    SE = Arena(SAap, 24 * 1024)
    wf = [[SE.alloc(512, F32, f"wf{m}{i}") for i in range(2)] for m in range(2)]
    qbb = [[SE.alloc(2048, BF16, f"qb{m}{i}") for i in range(2)] for m in range(2)]
    ab_q = [[memo[:, 1024*m + 512*i:1024*m + 512*(i+1)] for i in range(2)] for m in range(2)]
    ab_q2 = [[elut.bitcast(BF16)[:, 1024*m + 512*i:1024*m + 512*(i+1)] for i in range(2)] for m in range(2)]
    assert SE.off <= 24 * 1024

    wo_done = wo_tok[0] if wo_tok[0][1] > wo_tok[1][1] else wo_tok[1]
    qhz = {"lastQ": [None] * 8}
    def quantize(matj, wfb, qdst, tok_ld, abuf, abuf2, guard=None, lane=0):
        t_a = P.emit("scalar", lambda s: s.activation(abuf, wfb, AF.Sign, bias=thn128[:, matj:matj+1]),
                     waits=[tok_ld, t_thn, qhz["lastQ"][lane]], inc=aC)
        t_b = P.emit("scalar", lambda s: s.activation(abuf2, wfb, AF.Sign, bias=th128[:, matj:matj+1]),
                     waits=[], inc=aC)
        t_q = P.emit("vector", lambda v: v.tensor_tensor(qdst, abuf, abuf2, ALU.add),
                     waits=[t_b, guard], inc=vC)
        qhz["lastQ"][lane] = t_q
        return t_q
    hid_rdy = {}
    wf_tok = [[(tC, tC.n), (tC, tC.n)], [(tC, tC.n), (tC, tC.n)]]
    q_tok = [[None, None], [None, None]]
    dbg_q1 = want_dbg("q1", [256, 2048])
    qb_guard = [[(tC, tC.n), (tC, tC.n)], [(tC, tC.n), (tC, tC.n)]]
    sg_guard = [None, None]
    for hc in range(NHC):
        x1b = [B[0], B[2]]
        x2bk = [B[1], B[3]]
        qtoks = [[], []]
        qb = [qbb[0][hc % 2], qbb[1][hc % 2]]
        for matj, (wd, qbm) in enumerate(((w1d, qb[0]), (w2d, qb[1]))):
            for qt in range(4):
                bi = qt % 2
                t_ld = P.dma("sync", wf[matj][bi],
                             wd[hc*16*128*128:(hc+1)*16*128*128].rearrange("(p c) -> p c", p=128)[:, 512*qt:512*(qt+1)],
                             waits=[wf_tok[matj][bi]], inc=qWf[matj][bi])
                abp = ab_q[matj] if qt % 2 == 0 else ab_q2[matj]
                t_q = quantize(matj, wf[matj][bi], qbm[:, 512*qt:512*(qt+1)], t_ld, abp[0], abp[1], guard=qb_guard[matj][hc % 2], lane=matj * 2 + qt % 2)
                wf_tok[matj][bi] = t_q
                qtoks[matj].append(t_q)
        mm_last = None
        for ct in range(2):
            for matj in range(2):
                bkk = x1b[ct] if matj == 0 else x2bk[ct]
                first = True
                for dc in range(NDC):
                    w_ = [qtoks[matj][dc // 4], xn_tok[dc][1]]
                    if first:
                        w_.append(bkk.rd)
                    mm_last = P.emit("tensor", lambda t, bkk=bkk, qbm_=qb[matj], dc=dc, ct=ct, st=first, sp=(dc == 15): t.matmul(
                        bkk.ap, qbm_[:, 128*dc:128*(dc+1)],
                        xn_own[:, 1024*dc + 512*ct:1024*dc + 512*(ct+1)],
                        start=st, stop=sp), waits=w_, inc=tC)
                    first = False
                    bkk.wr = mm_last
        for _p in range(2):
            qb_guard[0][_p] = mm_last
            qb_guard[1][_p] = mm_last
        for ct in range(2):
            t_sg = P.emit("scalar", lambda s, ct=ct: s.activation(
                sgb[:, 512*ct:512*(ct+1)], x1b[ct].ap, AF.Sigmoid, scale=half_f), waits=[x1b[ct].wr, t_cf, sg_guard[ct]], inc=aC)
            x1b[ct].rd = t_sg
            t_h = P.emit("vector", lambda v, ct=ct, hc=hc: v.scalar_tensor_tensor(
                hidden[:, 1024*hc + 512*ct:1024*hc + 512*(ct+1)], sgb[:, 512*ct:512*(ct+1)],
                0.5, x2bk[ct].ap, ALU.mult, ALU.mult), waits=[t_sg, x2bk[ct].wr, wo_done], inc=vC)
            x2bk[ct].rd = t_h
            sg_guard[ct] = t_h
            hid_rdy[(hc, ct)] = t_h
    dbg_hid = want_dbg("hid", [44*128, 1024])
    if dbg_hid is not None:
        for hc in range(NHC):
            P.dma("gpsimd", dbg_hid[128*hc:128*(hc+1), :], hidden[:, 1024*hc:1024*(hc+1)],
                  waits=[hid_rdy[(hc, 0)], hid_rdy[(hc, 1)]], inc=dbq)
    if stop_after < 6:
        return _fin(nc, P, es, outd, dbg_d, locals())

    SF = Arena(ar_t[:, 32 * 1024:64 * 1024], 32 * 1024)
    q3b = [SF.alloc(5632, BF16, "q3a"), SF.alloc(5632, BF16, "q3b")]
    assert SF.off <= 32 * 1024
    SFO = Arena(SAap, 24 * 1024)
    ost = [SFO.alloc(1024, F32, "ost0"), SFO.alloc(1024, F32, "ost1")]
    xot = [SFO.alloc(1024, BF16, "xot0"), SFO.alloc(1024, BF16, "xot1")]
    w3f = [SFO.alloc(704, F32, "w3f0"), SFO.alloc(704, F32, "w3f1")]
    ab3p = [[SFO.alloc(704, BF16, f"ab3{i}{j}") for j in range(2)] for i in range(2)]
    assert SFO.off <= 24 * 1024
    gE = (gC, gC.n)
    vE = (vC, vC.n)

    w3_tok = [(tC, tC.n), (tC, tC.n)]
    xo_tok = [None, None]
    ost_tok = [None, None]
    last_out = None
    phaseE_done = mm_last
    q3_guard = [(tC, tC.n), (tC, tC.n)]
    for fc in range(16):
        q3 = q3b[fc % 2]
        qq_toks = []
        for qt in range(8):
            bi = qt % 2
            t_ld = P.dma("sync", w3f[bi],
                         w3d[fc*44*128*128:(fc+1)*44*128*128].rearrange("(p c) -> p c", p=128)[:, 704*qt:704*(qt+1)],
                         waits=[w3_tok[bi], phaseE_done], inc=qW3[bi])
            t_q = quantize(2, w3f[bi], q3[:, 704*qt:704*(qt+1)], t_ld, ab3p[bi][0], ab3p[bi][1], guard=q3_guard[fc % 2], lane=4 + bi)
            w3_tok[bi] = t_q
            qq_toks.append(t_q)
        banks = [B[4 + 2*(fc % 2)], B[5 + 2*(fc % 2)]]
        mm3 = None
        for ct in range(2):
            first = True
            for hcc in range(NHC):
                w_ = [qq_toks[(128 * hcc + 127) // 704], hid_rdy[(hcc, ct)]]
                if first:
                    w_.append(banks[ct].rd)
                mm3 = P.emit("tensor", lambda t, bkk=banks[ct], q3_=q3, hcc=hcc, ct=ct, st=first, sp=(hcc == NHC-1): t.matmul(
                    bkk.ap, q3_[:, 128*hcc:128*(hcc+1)],
                    hidden[:, 1024*hcc + 512*ct:1024*hcc + 512*(ct+1)],
                    start=st, stop=sp), waits=w_, inc=tC)
                first = False
            banks[ct].wr = mm3
        q3_guard[fc % 2] = mm3
        bi = fc % 2
        t_xo = P.dma("gpsimd", xot[bi], xview(fc)[:, 1024:2048], waits=[xo_tok[bi], phaseE_done, gE, vE], inc=qXo[bi])
        for ct in range(2):
            t_a1 = P.emit("vector", lambda v, bkk=banks[ct], fc=fc, ct=ct, bi=bi: v.scalar_tensor_tensor(
                ost[bi][:, 512*ct:512*(ct+1)], bkk.ap, 0.5, attn_fm[:, 1024*fc + 512*ct:1024*fc + 512*(ct+1)],
                ALU.mult, ALU.add),
                waits=[banks[ct].wr, attn_rdy[fc], ost_tok[bi], gE], inc=vC)
            banks[ct].rd = t_a1
            t_a2 = P.emit("gpsimd", lambda g, fc=fc, ct=ct, bi=bi: g.tensor_tensor(
                ost[bi][:, 512*ct:512*(ct+1)], ost[bi][:, 512*ct:512*(ct+1)],
                xot[bi][:, 512*ct:512*(ct+1)], ALU.add), waits=[t_a1, t_xo], inc=gC)
        xo_tok[bi] = t_a2
        last_out = P.dma("sync", outd[fc*128*TO:(fc+1)*128*TO].rearrange("(p c) -> p c", p=128),
                         ost[bi], waits=[t_a2], inc=qO[bi])
        ost_tok[bi] = last_out

    P.barrier("sync", [(qO[0], qO[0].n), (qO[1], qO[1].n)])
    return _fin(nc, P, es, outd, dbg_d, locals())


def _fin(nc, P, es, outd, dbg_d, lcl):
    es2 = ExitStack()
    for ch in P.chans:
        ch.sem = es2.enter_context(nc.semaphore(ch.name))
    with es2:
        with nc.Block() as block:
            P.finalize(block)
    es.close()
    return nc, dbg_d


def host_inputs(full, core):
    b, half = core // 2, core % 2
    x = np.asarray(full["L0_attn_hidden_states"])
    own = x[b, 1024*half:1024*half+1024]
    pre = x[b, 1024*(1-half):1024*(1-half)+1024]
    ctx = np.concatenate([pre, own], axis=0)
    xctx = np.ascontiguousarray(ctx.T).reshape(-1)
    def wtile(w):
        return np.ascontiguousarray(np.asarray(w).T.reshape(16, 128, 4, 512).transpose(2, 0, 1, 3)).reshape(-1)
    wo = np.asarray(full["L0_attn_w_o"]).T.reshape(16, 128, 16, 128).transpose(2, 1, 0, 3)
    w1 = np.asarray(full["L0_ffn_w1"]).T.reshape(16, 128, 44, 128).transpose(2, 1, 0, 3)
    w2 = np.asarray(full["L0_ffn_w2"]).T.reshape(16, 128, 44, 128).transpose(2, 1, 0, 3)
    w3 = np.asarray(full["L0_ffn_w3"]).T.reshape(44, 128, 16, 128).transpose(2, 1, 0, 3)
    mems = np.asarray(full["L0_attn_memories"]).transpose(1, 2, 0, 3)
    normst = np.ascontiguousarray(np.asarray(full["L0_attn_memory_norms"]).transpose(2, 1, 0).reshape(128, 64))
    nt = np.asarray(full["L0_attn_memory_norms"]).transpose(2, 1, 0)
    normst = np.ascontiguousarray(nt.reshape(128, 64))
    gater = np.tile(np.asarray(full["L0_attn_gate"]).reshape(1, 16), (128, 1)).astype(np.float32)
    cstf = np.zeros((128, 390), np.float32)
    cstf[:, 0:128] = np.eye(128)
    cstf[:, 128:256] = 1.0
    cstf[:, 257] = float(half)
    cstf[0, 258] = LN_EPS
    cstf[:, 259] = 0.5
    jj, ii = np.mgrid[0:128, 0:128]
    cstf[:, 260:388] = (jj <= ii).astype(np.float32)
    cstb = np.zeros((128, 386), np.float32)
    cstb[:, 0] = 1.0
    cstb[:, 1:129] = np.eye(128)
    cstb[:, 129:257] = (jj <= ii)
    cstb[:, 257:385] = 1.0
    import ml_dtypes
    return {
        "xctx": xctx.astype(np.float32),
        "wq": wtile(full["L0_attn_w_q"]).astype(np.float32),
        "wk": wtile(full["L0_attn_w_k"]).astype(np.float32),
        "wv": wtile(full["L0_attn_w_v"]).astype(np.float32),
        "wo": np.ascontiguousarray(wo).reshape(-1).astype(np.float32),
        "w1": np.ascontiguousarray(w1).reshape(-1).astype(np.float32),
        "w2": np.ascontiguousarray(w2).reshape(-1).astype(np.float32),
        "w3": np.ascontiguousarray(w3).reshape(-1).astype(np.float32),
        "mems": np.ascontiguousarray(mems).reshape(-1).astype(np.float32),
        "normst": normst.astype(np.float32),
        "gater": gater,
        "cstf": cstf,
        "cstb": cstb.astype(ml_dtypes.bfloat16),
    }


def kernel(**inputs):
    from concourse.bass_utils import run_bass_kernel_spmd
    nc, _ = build(dbg_names=(), stop_after=99)
    in_maps = [host_inputs(inputs, c) for c in range(8)]
    res = run_bass_kernel_spmd(nc, in_maps, core_ids=list(range(8)))
    out = np.zeros((4, 2048, 2048), dtype=np.float32)
    for c in range(8):
        b, half = c // 2, c % 2
        o = np.asarray(res.results[c]["out"]).reshape(2048, 1024)
        out[b, 1024*half:1024*(half+1), :] = o.T
    return out


# version 08
# speedup vs baseline: 1.0675x; 1.0675x over previous
import numpy as np
import concourse.bass as bass
import concourse.mybir as mybir
from contextlib import ExitStack

F32 = mybir.dt.float32
BF16 = mybir.dt.bfloat16
U8 = mybir.dt.uint8
AF = mybir.ActivationFunctionType
ALU = mybir.AluOpType
AXX = mybir.AxisListType.X

ENGINES = ("sync", "tensor", "vector", "scalar", "gpsimd")

D = 2048; T = 2048; TO = 1024; NDC = 16
H = 16; HD = 128; NG = 4; M = 4; CH = 128; NSC = 8
HF = 5632; NHC = 44
SCALE = float(HD) ** -0.5
LN_EPS = 1e-5
EPS_ATT = 1e-6
NFFN = float(HF * D)


class Ch:
    __slots__ = ("sem", "n", "name", "dma")
    def __init__(self, name, dma=False):
        self.sem = None; self.n = 0; self.name = name; self.dma = dma


class Prog:
    def __init__(self, nc):
        self.nc = nc
        self.streams = {e: [] for e in ENGINES}
        self.chans = []
        self.seen = {e: {} for e in ENGINES}

    def chan(self, name, dma=False):
        ch = Ch(name, dma); self.chans.append(ch); return ch

    def emit(self, eng, fn, waits=(), inc=None, inc_amt=1, fwaits=()):
        waits = [w for w in waits if w is not None]
        fwaits = [w for w in fwaits if w is not None]
        seen = self.seen[eng]
        needed = list(fwaits)
        for (ch, v) in fwaits:
            if seen.get(ch, 0) < v:
                seen[ch] = v
        for (ch, v) in waits:
            if seen.get(ch, 0) < v:
                needed.append((ch, v)); seen[ch] = v
        tok = None
        if inc is not None:
            inc.n += inc_amt
            tok = (inc, inc.n)
        self.streams[eng].append((needed, fn, (inc, inc_amt) if inc else None))
        return tok

    def dma(self, eng, out, in_, waits=(), inc=None):
        assert inc is not None
        w = list(waits) + ([(inc, inc.n)] if inc.n else [])
        return self.emit(eng, lambda e: e.dma_start(out=out, in_=in_),
                         waits=w, inc=inc, inc_amt=16)

    def barrier(self, eng, waits):
        self.emit(eng, None, waits=waits)

    def finalize(self, block):
        def mk(eng_name):
            stream = self.streams[eng_name]
            def body(e):
                for needed, fn, inc in stream:
                    for (ch, v) in needed:
                        e.wait_ge(ch.sem, v)
                    if fn is None:
                        continue
                    ins = fn(e)
                    if inc is not None:
                        ch, amt = inc
                        ins.then_inc(ch.sem, amt)
            return body
        for eng_name in ENGINES:
            if self.streams[eng_name]:
                getattr(block, eng_name)(mk(eng_name))


class Arena:
    def __init__(self, ap, nbytes):
        self.ap = ap; self.nbytes = nbytes; self.off = 0

    def alloc(self, cols, dtype, name=""):
        isz = 2 if dtype == BF16 else 4
        self.off = (self.off + 3) & ~3
        nb = cols * isz
        assert self.off + nb <= self.nbytes, f"arena overflow {name}: {self.off+nb}>{self.nbytes}"
        v = self.ap[:, self.off:self.off + nb].bitcast(dtype)
        self.off += nb
        return v


class Bank:
    __slots__ = ("ap", "rd", "wr")
    def __init__(self, ap):
        self.ap = ap; self.rd = None; self.wr = None


def build(dbg_names=(), stop_after=99):
    nc = bass.Bass("TRN2", target_bir_lowering=False)

    def din(name, shape, dt=F32):
        return nc.dram_tensor(name, shape, dt, kind="ExternalInput").ap()

    xctx = din("xctx", [D * T])
    wqd = din("wq", [NG * NDC * 128 * 512])
    wkd = din("wk", [NG * NDC * 128 * 512])
    wvd = din("wv", [NG * NDC * 128 * 512])
    wod = din("wo", [16 * 128 * 16 * 128])
    w1d = din("w1", [NHC * 128 * NDC * 128])
    w2d = din("w2", [NHC * 128 * NDC * 128])
    w3d = din("w3", [16 * 128 * NHC * 128])
    memd = din("mems", [H * 128 * M * 128])
    normd = din("normst", [128, 64])
    gated = din("gater", [128, 16])
    cstfd = din("cstf", [128, 390])
    cstbd = din("cstb", [128, 386], BF16)

    outd = nc.dram_tensor("out", [NDC * 128 * TO], F32, kind="ExternalOutput").ap()

    dbg_d = {}
    def want_dbg(name, shape):
        if name in dbg_names:
            dbg_d[name] = nc.dram_tensor("dbg_" + name, shape, F32, kind="ExternalOutput").ap()
            return dbg_d[name]
        return None

    es = ExitStack()
    SB_BYTES = 206 * 1024
    ar_t = es.enter_context(nc.sbuf_tensor("arena", [128, SB_BYTES], U8))
    pb_t = [es.enter_context(nc.psum_tensor(f"pb{i}", [128, 512], F32)) for i in range(8)]
    B = [Bank(t[:]) for t in pb_t]

    A = Arena(ar_t[:], SB_BYTES)
    xn_pre = A.alloc(NDC * 1024, BF16, "xn_pre")
    xn_own = A.alloc(NDC * 1024, BF16, "xn_own")
    R1ap = ar_t[:, A.off:A.off + 91 * 1024]
    R1 = Arena(R1ap, 91 * 1024)
    A.off += 91 * 1024
    SAap = ar_t[:, A.off:A.off + 24 * 1024]
    A.off += 24 * 1024
    A128 = A.alloc(2048, BF16, "A128")
    B128 = A.alloc(2048, BF16, "B128")
    sxn = A.alloc(16, F32, "sxn")
    sxn_b = A.alloc(16, BF16, "sxn_b")
    cb = A.alloc(386, BF16, "cb")
    cf = A.alloc(390, F32, "cf")
    gsum = A.alloc(4, F32, "gsum")
    rtmp = A.alloc(4, F32, "rtmp")
    gacc = A.alloc(2, F32, "gacc")
    th128 = A.alloc(4, F32, "th128")
    thn128 = A.alloc(4, F32, "thn128")
    g128 = A.alloc(16, F32, "g128")
    gi128 = A.alloc(16, F32, "gi128")
    w128 = A.alloc(16, F32, "w128")
    sq_sb = A.alloc(4, BF16, "sq_sb")
    normstb = A.alloc(64, BF16, "normstb")
    nsb = A.alloc(4, BF16, "nsb")
    nf4 = A.alloc(4, F32, "nf4")
    relP = A.alloc(16, BF16, "relP")
    r_sc = A.alloc(4 * 32, F32, "r_sc")
    Sst = A.alloc(2 * 132, F32, "Sst")
    Sst_b = A.alloc(2 * 132, BF16, "Sst_b")
    Scur_b = A.alloc(2 * 132, BF16, "Scur_b")
    kpf = A.alloc(2 * 128, BF16, "kpf")
    scm = A.alloc(2 * 128, BF16, "scm")
    denb = A.alloc(2 * 2, F32, "denb")
    zacc = A.alloc(2 * 2, F32, "zacc")
    elut = A.alloc(2 * 512, F32, "elut")
    sgb = A.alloc(2 * 512, BF16, "sgb")
    memo = A.alloc(2 * 1024, BF16, "memo")
    rows = A.alloc(512, F32, "rows")
    rows2 = A.alloc(58, F32, "rows2")
    rowsb = A.alloc(16, BF16, "rowsb")
    assert A.off <= SB_BYTES, A.off

    kp = R1.alloc(16 * 512, BF16, "kp")
    vt = R1.alloc(16 * 512, BF16, "vt")
    sjq = R1.alloc(4 * 1024, BF16, "sq")
    comb = R1.alloc(8 * 2048, BF16, "comb")
    gbuf = [R1.alloc(1408, F32, "gbuf0"), R1.alloc(1408, F32, "gbuf1")]
    scur32 = R1.alloc(2 * 128, F32, "scur32")
    hidden = Arena(R1ap, R1.nbytes).alloc(NHC * 1024, BF16, "hidden")
    comb_fm = Arena(R1ap, 40 * 1024).alloc(16 * 1024, BF16, "comb_fm")
    attn_fm = xn_pre

    ones_b = cb[:, 0:1]
    id_b = cb[:, 1:129]
    onesrow_b = cb[0:1, 257:385]
    id_f = cf[:, 0:128]
    onesrow_f = cf[0:1, 128:256]
    mask_f = cf[:, 257:258]
    lneps_f = cf[0:1, 258:259]
    half_f = cf[:, 259:260]
    tril_f = cf[:, 260:388]

    P = Prog(nc)
    qS = P.chan("qS", dma=True); qG = P.chan("qG", dma=True)
    qO = [P.chan("qO0", dma=True), P.chan("qO1", dma=True)]
    qCb = P.chan("qCb", dma=True); qCf = P.chan("qCf", dma=True); qGt = P.chan("qGt", dma=True)
    qNrm = P.chan("qNrm", dma=True)
    qGm = [P.chan("qGm0", dma=True), P.chan("qGm1", dma=True)]
    qXb = [P.chan(f"qXb{i}", dma=True) for i in range(4)]
    qWk = [P.chan("qWk0", dma=True), P.chan("qWk1", dma=True)]
    qMb = [P.chan("qMb0", dma=True), P.chan("qMb1", dma=True)]
    qWo = [P.chan("qWo0", dma=True), P.chan("qWo1", dma=True)]
    qWf = [[P.chan(f"qWf{m}{i}", dma=True) for i in range(2)] for m in range(2)]
    qW3 = [P.chan("qW30", dma=True), P.chan("qW31", dma=True)]
    qXo = [P.chan("qXo0", dma=True), P.chan("qXo1", dma=True)]
    tC = P.chan("tC"); vC = P.chan("vC"); aC = P.chan("aC"); gC = P.chan("gC")

    t_cb = P.dma("sync", cb, cstbd, inc=qCb)
    t_cf = P.dma("sync", cf, cstfd, inc=qCf)
    t_nrm = P.dma("gpsimd", normstb, normd, inc=qNrm)
    t_gate = P.dma("sync", rows2[:, 0:16], gated, inc=qGt)
    t_g = P.emit("scalar", lambda s: s.activation(g128, rows2[:, 0:16], AF.Sigmoid), waits=[t_gate], inc=aC)
    t_gi = P.emit("vector", lambda v: v.tensor_scalar(gi128, g128, -1.0, 1.0, ALU.mult, ALU.add), waits=[t_g], inc=vC)
    t_gz = P.emit("gpsimd", lambda g: g.memset(gsum, 0.0), inc=gC)

    GT_PER = 64
    gamma_tiles = [(j, wd, t) for j, wd in enumerate((w1d, w2d, w3d)) for t in range(GT_PER)]
    gst = {"i": 0, "last_rd": [None, None], "last_ac": [None, None], "tok": None}

    def gamma_pump(n):
        for _ in range(n):
            i = gst["i"]
            if i >= len(gamma_tiles):
                return
            gst["i"] = i + 1
            j, wd, t = gamma_tiles[i]
            bi = i % 2
            gb = gbuf[bi]
            off = t * 128 * 1408
            t_ld = P.dma("sync", gb, wd[off:off + 128 * 1408].rearrange("(p c) -> p c", p=128),
                         waits=[gst["last_rd"][bi]], inc=qGm[bi])
            t_rd = P.emit("scalar", lambda s, gb=gb, bi=bi: s.activation(gb, gb, AF.Abs, accum_out=gacc[:, bi:bi+1]),
                          waits=[t_ld, t_gz, gst["last_ac"][bi]], inc=aC)
            t_ac = P.emit("gpsimd", lambda g, j=j, bi=bi: g.tensor_tensor(gsum[:, j:j+1], gsum[:, j:j+1], gacc[:, bi:bi+1], ALU.add),
                          waits=[t_rd], fwaits=[gst["tok"]], inc=gC)
            gst["last_rd"][bi] = t_rd
            gst["last_ac"][bi] = t_ac
            gst["tok"] = t_ac

    def xview(dc):
        return xctx[dc * 128 * T:(dc + 1) * 128 * T].rearrange("(p t) -> p t", p=128)

    xpark = Arena(R1ap, 64 * 1024).alloc(NDC * 2048, BF16, "xpark")
    SA = Arena(SAap, 24 * 1024)
    xb = [SA.alloc(2048, BF16, "xb0"), SA.alloc(2048, BF16, "xb1")]
    x2b = [SA.alloc(2048, BF16, "x2b0"), SA.alloc(2048, BF16, "x2b1")]
    arow_b = SA.alloc(2048, BF16, "arow_b")
    brow_b = SA.alloc(2048, BF16, "brow_b")
    assert SA.off <= 24 * 1024

    last_mm = [None, None]
    stat_last = None
    xpark_tok = {}
    xp_prev = [None] * 4
    for dc in range(NDC):
        bi = dc % 4
        xp = xpark[:, 2048*dc:2048*(dc+1)]
        t_x = P.dma("gpsimd", xp, xview(dc), waits=[xp_prev[bi]], inc=qXb[bi])
        xp_prev[bi] = t_x
        xpark_tok[dc] = t_x
        b2 = dc % 2
        t_sq = P.emit("vector", lambda v, b2=b2, xp=xp: v.tensor_mul(x2b[b2], xp, xp),
                      waits=[t_x, last_mm[b2]], inc=vC)
        for tt in range(4):
            st, sp = dc == 0, dc == NDC - 1
            P.emit("tensor", lambda t, xp=xp, tt=tt, st=st, sp=sp: t.matmul(
                B[tt].ap[0:1, :], ones_b, xp[:, 512*tt:512*(tt+1)], start=st, stop=sp),
                waits=[t_x, t_cb], inc=tC)
            stat_last = P.emit("tensor", lambda t, b2=b2, tt=tt, st=st, sp=sp: t.matmul(
                B[4+tt].ap[0:1, :], ones_b, x2b[b2][:, 512*tt:512*(tt+1)], start=st, stop=sp),
                waits=[t_sq], inc=tC)
        last_mm[b2] = stat_last
        gamma_pump(1)

    t_fin = None
    for tt in range(4):
        t_mn = P.emit("vector", lambda v, tt=tt: v.tensor_scalar(
            B[tt].ap[0:1, :], B[tt].ap[0:1, :], 1.0/D, None, ALU.mult),
            waits=[stat_last], inc=vC)
        t_e2 = P.emit("vector", lambda v, tt=tt: v.tensor_scalar(
            B[4+tt].ap[0:1, :], B[4+tt].ap[0:1, :], 1.0/D, None, ALU.mult), inc=vC)
        t_ms = P.emit("scalar", lambda s, tt=tt: s.activation(
            rows[0:1, 0:512], B[tt].ap[0:1, :], AF.Square), waits=[t_mn], inc=aC)
        t_vr = P.emit("vector", lambda v, tt=tt: v.tensor_sub(
            B[4+tt].ap[0:1, :], B[4+tt].ap[0:1, :], rows[0:1, 0:512]), waits=[t_ms], fwaits=[t_e2], inc=vC)
        t_sq2 = P.emit("scalar", lambda s, tt=tt: s.activation(
            B[4+tt].ap[0:1, :], B[4+tt].ap[0:1, :], AF.Sqrt, bias=lneps_f),
            waits=[t_vr, t_cf], inc=aC)
        t_rs = P.emit("vector", lambda v, tt=tt: v.reciprocal(
            B[4+tt].ap[0:1, :], B[4+tt].ap[0:1, :]), waits=[t_sq2], inc=vC)
        t_a = P.emit("vector", lambda v, tt=tt: v.tensor_copy(
            arow_b[0:1, 512*tt:512*(tt+1)], B[4+tt].ap[0:1, :]), fwaits=[t_rs], inc=vC)
        t_mc = P.emit("vector", lambda v, tt=tt: v.tensor_copy(rows[0:1, 0:512], B[tt].ap[0:1, :]), inc=vC)
        t_b = P.emit("vector", lambda v, tt=tt: v.scalar_tensor_tensor(
            brow_b[0:1, 512*tt:512*(tt+1)], rows[0:1, 0:512], -1.0, B[4+tt].ap[0:1, :],
            ALU.mult, ALU.mult), waits=[t_mc], inc=vC)
        t_fin = t_b
        B[tt].rd = t_b
        B[4+tt].rd = t_b
    t_ab = t_fin
    t_bbb = t_fin
    bc_last = None
    for tt in range(4):
        t1 = P.emit("tensor", lambda t, tt=tt: t.matmul(
            B[tt].ap, onesrow_b, arow_b[0:1, 512*tt:512*(tt+1)], start=True, stop=True),
            waits=[t_ab, B[tt].rd], inc=tC)
        t2 = P.emit("tensor", lambda t, tt=tt: t.matmul(
            B[4+tt].ap, onesrow_b, brow_b[0:1, 512*tt:512*(tt+1)], start=True, stop=True),
            waits=[t_bbb, B[4+tt].rd], inc=tC)
        e1 = P.emit("vector", lambda v, tt=tt: v.tensor_copy(A128[:, 512*tt:512*(tt+1)], B[tt].ap), waits=[t1], inc=vC)
        e2 = P.emit("vector", lambda v, tt=tt: v.tensor_copy(B128[:, 512*tt:512*(tt+1)], B[4+tt].ap), waits=[t2], inc=vC)
        B[tt].rd, B[tt].wr = e1, t1
        B[4+tt].rd, B[4+tt].wr = e2, t2
        bc_last = e2
    gamma_pump(2)

    last_use = [None, None]
    sxn_toks = []
    xn_tok = {}
    for dc in range(NDC):
        bi = dc % 2
        xp = xpark[:, 2048*dc:2048*(dc+1)]
        hts = []
        for hfi, dst in ((0, xn_pre), (1, xn_own)):
            sl = slice(1024*hfi, 1024*(hfi+1))
            t_p1 = P.emit("vector", lambda v, bi=bi, sl=sl, xp=xp: v.tensor_tensor(
                x2b[bi][:, sl], xp[:, sl], A128[:, sl], ALU.mult), waits=[xpark_tok[dc], bc_last, last_use[bi]], inc=vC)
            t_p2 = P.emit("vector", lambda v, bi=bi, sl=sl, dst=dst, dc=dc: v.tensor_tensor(
                dst[:, 1024*dc:1024*(dc+1)], x2b[bi][:, sl], B128[:, sl], ALU.add),
                waits=[t_p1], inc=vC)
            hts.append(t_p2)
        xn_tok[dc] = hts
        t_r1 = P.emit("vector", lambda v, dc=dc: v.tensor_reduce(rtmp[:, 1:2], xn_pre[:, 1024*dc:1024*(dc+1)], AXX, ALU.add), waits=[hts[0]], inc=vC)
        t_r2 = P.emit("vector", lambda v, dc=dc: v.tensor_reduce(rtmp[:, 2:3], xn_own[:, 1024*dc:1024*(dc+1)], AXX, ALU.add), waits=[hts[1]], inc=vC)
        t_r3 = P.emit("vector", lambda v, dc=dc: v.tensor_tensor(sxn[:, dc:dc+1], rtmp[:, 1:2], rtmp[:, 2:3], ALU.add), fwaits=[t_r2], inc=vC)
        sxn_toks.append(t_r3)
        last_use[bi] = hts[1]
        gamma_pump(1)
    t_sxnb = P.emit("vector", lambda v: v.tensor_copy(sxn_b, sxn), waits=[sxn_toks[-1]], inc=vC)

    dbq = P.chan("dbq", dma=True)
    dbg_xn = want_dbg("xn", [NDC * 128, T])
    if dbg_xn is not None:
        for dc in range(NDC):
            P.dma("gpsimd", dbg_xn[128*dc:128*(dc+1), 0:1024], xn_pre[:, 1024*dc:1024*(dc+1)], waits=xn_tok[dc], inc=dbq)
            P.dma("gpsimd", dbg_xn[128*dc:128*(dc+1), 1024:2048], xn_own[:, 1024*dc:1024*(dc+1)], waits=xn_tok[dc], inc=dbq)
    dbg_sxn = want_dbg("sxn", [128, 16])
    if dbg_sxn is not None:
        P.dma("gpsimd", dbg_sxn, sxn, waits=sxn_toks, inc=dbq)
    dbg_a = want_dbg("a128", [128, 2048])
    if dbg_a is not None:
        P.dma("gpsimd", dbg_a, A128, waits=[bc_last], inc=dbq)
    dbg_bb = want_dbg("b128", [128, 2048])
    if dbg_bb is not None:
        P.dma("gpsimd", dbg_bb, B128, waits=[bc_last], inc=dbq)
    dbg_ar = want_dbg("arow", [1, 2048])
    if dbg_ar is not None:
        P.dma("gpsimd", dbg_ar, arow_b[0:1, :], waits=[t_fin], inc=dbq)
    dbg_br = want_dbg("brow", [1, 2048])
    if dbg_br is not None:
        P.dma("gpsimd", dbg_br, brow_b[0:1, :], waits=[t_fin], inc=dbq)
    if stop_after < 1:
        return _fin(nc, P, es, outd, dbg_d, locals())

    SC = Arena(SAap, 24 * 1024)
    wkb = [SC.alloc(4*512, BF16, f"wkb{i}") for i in range(4)]
    membuf = [SC.alloc(512, BF16, "mem0"), SC.alloc(512, BF16, "mem1")]
    assert SC.off <= 24 * 1024
    wkb_tok = [(vC, vC.n), (vC, vC.n), (vC, vC.n), (vC, vC.n)]
    qWk4 = [P.chan(f"qWk4_{i}", dma=True) for i in range(4)]

    def wslice(wd, g, dc0, ndc):
        off = (g * NDC + dc0) * 128 * 512
        return wd[off:off + ndc * 128 * 512].rearrange("(a p c) -> p a c", a=ndc, p=128)

    def elu_evac(bk, dst, par, extra=()):
        t_min = P.emit("vector", lambda v, bk=bk, par=par: v.tensor_scalar(
            elut[:, 512*par:512*(par+1)], bk.ap, 0.0, None, ALU.min),
            waits=[bk.wr] + list(extra), inc=vC)
        t_exp = P.emit("scalar", lambda s, par=par: s.activation(
            elut[:, 512*par:512*(par+1)], elut[:, 512*par:512*(par+1)], AF.Exp),
            waits=[t_min], inc=aC)
        t_f = P.emit("vector", lambda v, bk=bk, dst=dst, par=par: v.scalar_tensor_tensor(
            dst, bk.ap, 1.0, elut[:, 512*par:512*(par+1)], ALU.add, ALU.max),
            waits=[t_exp], inc=vC)
        bk.rd = t_f
        return t_f

    kp_rdy = [None] * 16
    vt_rdy = [None] * 16
    sq_rdy = [None] * 4
    group_end = None
    t_er = t_ier = None
    e_row = rows2[0:1, 16:20]
    ie_row = rows2[0:1, 20:24]
    t_w128 = None
    t_sqs = None

    dbg_kp = want_dbg("kp", [16*128, 512])
    dbg_vt = want_dbg("vt", [16*128, 512])
    dbg_sq = want_dbg("sigq", [4*128, 1024])
    dbg_w = want_dbg("wmh", [128, 16])
    dbg_comb = want_dbg("comb", [8*128, 2048])
    dbg_st = want_dbg("sst", [4*128, 129])

    ngroups = NG if stop_after >= 3 else 1
    for g in range(ngroups):
        prev_end = group_end
        for wd, dstbuf, is_k, rdy in ((wkd, kp, True, kp_rdy), (wvd, vt, False, vt_rdy)):
            for tcb in range(4):
                banks = [B[(tcb % 2) * 4 + i] for i in range(4)]
                first = [True] * 4
                mm_last = None
                for dcb in range(4):
                    bi = dcb % 2 if False else (tcb * 4 + dcb) % 4
                    t_w = P.dma("gpsimd", wkb[bi][:].rearrange("p (a c) -> p a c", a=4),
                                wslice(wd, g, 4*dcb, 4), waits=[wkb_tok[bi]], inc=qWk4[bi])
                    for dci in range(4):
                        dc = 4*dcb + dci
                        for i4 in range(4):
                            tcn = 4*tcb + i4
                            xsrc = xn_pre if tcn < 8 else xn_own
                            xcol = 1024*dc + 128*(tcn % 8)
                            bkk = banks[i4]
                            w_ = [t_w, xn_tok[dc][0 if tcn < 8 else 1]]
                            if first[i4]:
                                w_.append(bkk.rd)
                            mm_last = P.emit("tensor", lambda t, bkk=bkk, xsrc=xsrc, xcol=xcol, bi=bi, dci=dci, st=first[i4], sp=(dc == 15): t.matmul(
                                bkk.ap, xsrc[:, xcol:xcol+128], wkb[bi][:, 512*dci:512*(dci+1)],
                                start=st, stop=sp), waits=w_, inc=tC)
                            first[i4] = False
                            bkk.wr = mm_last
                    wkb_tok[bi] = mm_last
                for i4 in range(4):
                    tcn = 4*tcb + i4
                    bkk = banks[i4]
                    dst = dstbuf[:, 512*tcn:512*(tcn+1)]
                    if is_k:
                        rdy[tcn] = elu_evac(bkk, dst, i4 % 2)
                    else:
                        rdy[tcn] = P.emit("vector", lambda v, bkk=bkk, dst=dst: v.tensor_copy(dst, bkk.ap), waits=[bkk.wr], inc=vC)
                        bkk.rd = rdy[tcn]
                gamma_pump(2)
        if g == 0 and dbg_kp is not None:
            for tcn in range(16):
                P.dma("gpsimd", dbg_kp[128*tcn:128*(tcn+1), :], kp[:, 512*tcn:512*(tcn+1)], waits=[kp_rdy[tcn]], inc=dbq)
        if g == 0 and dbg_vt is not None:
            for tcn in range(16):
                P.dma("gpsimd", dbg_vt[128*tcn:128*(tcn+1), :], vt[:, 512*tcn:512*(tcn+1)], waits=[vt_rdy[tcn]], inc=dbq)

        sqbk = B[4]
        for p2 in (0, 1):
            qbs = [B[0], B[1], B[2], B[3]] if p2 == 0 else [B[5], B[6], B[7], B[0]]
            firstq = [True] * 4
            firstsq = True
            for dcb in range(4):
                bi = (p2 * 4 + dcb) % 4
                t_w = P.dma("gpsimd", wkb[bi][:].rearrange("p (a c) -> p a c", a=4),
                            wslice(wqd, g, 4*dcb, 4), waits=[wkb_tok[bi]], inc=qWk4[bi])
                mm_last = None
                for dci in range(4):
                    dc = 4*dcb + dci
                    for ei in range(2):
                        e128 = 2*p2 + ei
                        for ct in range(2):
                            qi = 2*ei + ct
                            bkk = qbs[qi]
                            w_ = [t_w, xn_tok[dc][1]]
                            if firstq[qi]:
                                w_.append(bkk.rd)
                            ecol = 512*dci + 128*(2*p2 + ei)
                            mm_last = P.emit("tensor", lambda t, bkk=bkk, bi=bi, ecol=ecol, dc=dc, ct=ct, st=firstq[qi], sp=(dc == 15): t.matmul(
                                bkk.ap, wkb[bi][:, ecol:ecol+128],
                                xn_own[:, 1024*dc + 512*ct:1024*dc + 512*(ct+1)],
                                start=st, stop=sp), waits=w_, inc=tC)
                            firstq[qi] = False
                            bkk.wr = mm_last
                            wkb_tok[bi] = mm_last
                    if p2 == 0:
                        w_ = [t_sxnb]
                        if firstsq:
                            w_.append(sqbk.rd)
                        mm_last = P.emit("tensor", lambda t, bi=bi, dci=dci, dc=dc, st=firstsq, sp=(dc == 15): t.matmul(
                            sqbk.ap[0:1, :], sxn_b[:, dc:dc+1], wkb[bi][:, 512*dci:512*(dci+1)],
                            start=st, stop=sp), waits=w_, inc=tC)
                        firstsq = False
                        sqbk.wr = mm_last
                        wkb_tok[bi] = mm_last
            for qi in range(4):
                e128 = 2*p2 + (qi // 2)
                ct = qi % 2
                bkk = qbs[qi]
                sq_rdy[e128] = elu_evac(bkk, sjq[:, 1024*e128 + 512*ct:1024*e128 + 512*(ct+1)], qi % 2)
            if p2 == 0:
                t_cp = P.emit("vector", lambda v: v.tensor_copy(rows[0:1, 0:512], sqbk.ap[0:1, :]), waits=[sqbk.wr], inc=vC)
                sqbk.rd = t_cp
                tt_last = None
                for hh in range(4):
                    tt_last = P.emit("tensor", lambda t, hh=hh: t.transpose(
                        sqbk.ap[:, hh:hh+1], rows[0:1, 128*hh:128*(hh+1)], id_f[0:1, 0:1]),
                        waits=[t_cp, t_cf], inc=tC)
                sqbk.wr = tt_last
                t_sqs = P.emit("vector", lambda v: v.tensor_scalar(
                    sq_sb, sqbk.ap[:, 0:4], SCALE / T, None, ALU.mult), waits=[tt_last], inc=vC)
                sqbk.rd = t_sqs
        gamma_pump(4)
        if g == 0 and dbg_sq is not None:
            for hh in range(4):
                P.dma("gpsimd", dbg_sq[128*hh:128*(hh+1), :], sjq[:, 1024*hh:1024*(hh+1)], waits=[sq_rdy[hh]], inc=dbq)

        if g == 0:
            nst = []
            for m in range(M):
                nst.append(P.emit("vector", lambda v, m=m: v.tensor_reduce(
                    nf4[:, m:m+1], normstb[:, m:64:4], AXX, ALU.add), waits=[t_nrm], inc=vC))
            t_nsb = P.emit("vector", lambda v: v.tensor_copy(nsb, nf4), fwaits=[nst[-1]], inc=vC)
            t_nmm = P.emit("tensor", lambda t: t.matmul(B[1].ap[0:1, 0:4], ones_b, nsb, start=True, stop=True),
                           waits=[t_nsb, B[1].rd], inc=tC)
            B[1].wr = t_nmm
            t_er = P.emit("vector", lambda v: v.tensor_scalar(e_row, B[1].ap[0:1, 0:4], 1e-6, None, ALU.is_lt), waits=[t_nmm], inc=vC)
            B[1].rd = t_er
            t_ier = P.emit("vector", lambda v: v.tensor_scalar(ie_row, e_row, -1.0, 1.0, ALU.mult, ALU.add), waits=[t_er], inc=vC)
        t_P = P.emit("vector", lambda v, g=g: v.tensor_tensor(
            relP[:].rearrange("p (a b) -> p a b", a=4),
            normstb[:, 16*g:16*(g+1)].rearrange("p (a b) -> p a b", a=4),
            sq_sb[:].unsqueeze(2).broadcast_to([128, 4, 4]), ALU.mult), waits=[t_nrm], fwaits=[t_sqs], inc=vC)
        t_rmm = P.emit("tensor", lambda t: t.matmul(B[1].ap[0:1, 4:20], ones_b, relP, start=True, stop=True),
                       waits=[t_P, B[1].rd], inc=tC)
        B[1].wr = t_rmm
        RL = rows2[0:1, 32:48]
        RLv = RL.rearrange("p (a b) -> p a b", a=4)
        MX = rows2[0:1, 48:52]
        M2 = rows2[0:1, 52:56]
        t_rl = P.emit("vector", lambda v: v.tensor_tensor(
            RLv, B[1].ap[0:1, 4:20].rearrange("p (a b) -> p a b", a=4),
            ie_row.unsqueeze(1).broadcast_to([1, 4, 4]), ALU.mult), waits=[t_rmm, t_ier], inc=vC)
        B[1].rd = t_rl
        t_rl2 = P.emit("vector", lambda v: v.scalar_tensor_tensor(
            RLv, e_row.unsqueeze(1).broadcast_to([1, 4, 4]), -1e30, RLv, ALU.mult, ALU.add), fwaits=[t_rl], inc=vC)
        t_m1 = P.emit("vector", lambda v: v.tensor_tensor(MX, RL[:, 0:16:4], RL[:, 1:16:4], ALU.max), fwaits=[t_rl2], inc=vC)
        t_m2 = P.emit("vector", lambda v: v.tensor_tensor(M2, RL[:, 2:16:4], RL[:, 3:16:4], ALU.max), inc=vC)
        t_m3 = P.emit("vector", lambda v: v.tensor_tensor(MX, MX, M2, ALU.max), fwaits=[t_m2], inc=vC)
        t_sb = P.emit("vector", lambda v: v.tensor_tensor(
            RLv, RLv, MX.unsqueeze(2).broadcast_to([1, 4, 4]), ALU.subtract), fwaits=[t_m3], inc=vC)
        t_ex = P.emit("scalar", lambda s: s.activation(RL, RL, AF.Exp), waits=[t_sb], inc=aC)
        t_s1 = P.emit("vector", lambda v: v.tensor_tensor(MX, RL[:, 0:16:4], RL[:, 1:16:4], ALU.add), waits=[t_ex], inc=vC)
        t_s2 = P.emit("vector", lambda v: v.tensor_tensor(M2, RL[:, 2:16:4], RL[:, 3:16:4], ALU.add), inc=vC)
        t_s3 = P.emit("vector", lambda v: v.tensor_tensor(MX, MX, M2, ALU.add), fwaits=[t_s2], inc=vC)
        t_rc = P.emit("vector", lambda v: v.reciprocal(MX, MX), fwaits=[t_s3], inc=vC)
        t_wn = P.emit("vector", lambda v: v.tensor_tensor(
            RLv, RLv, MX.unsqueeze(2).broadcast_to([1, 4, 4]), ALU.mult), fwaits=[t_rc], inc=vC)
        t_wm = P.emit("vector", lambda v: v.tensor_tensor(
            rowsb[0:1, 0:16].rearrange("p (a b) -> p a b", a=4), RLv,
            ie_row.unsqueeze(1).broadcast_to([1, 4, 4]), ALU.mult), fwaits=[t_wn], inc=vC)
        t_wbc = P.emit("tensor", lambda t: t.matmul(B[1].ap[:, 32:48], onesrow_b, rowsb[0:1, 0:16], start=True, stop=True),
                       waits=[t_wm, B[1].rd], inc=tC)
        B[1].wr = t_wbc
        t_w128 = P.emit("vector", lambda v: v.tensor_copy(w128, B[1].ap[:, 32:48]), waits=[t_wbc], inc=vC)
        B[1].rd = t_w128
        if g == 0 and dbg_w is not None:
            P.dma("gpsimd", dbg_w, w128, waits=[t_w128], inc=dbq)

        memtok = [(tC, tC.n), (tC, tC.n)]
        t_sstb_all = None
        for pair in range(2):
            hpair = [2*pair, 2*pair+1]
            sst_tok = {}
            for hh in hpair:
                h = 4*g + hh
                par = hh % 2
                t_mem = P.dma("gpsimd", membuf[par], memd[h*128*512:(h+1)*128*512].rearrange("(p c) -> p c", p=128),
                              waits=[memtok[par]], inc=qMb[par])
                bS = B[6 + par]
                bZp = B[4 + par]
                mmz = None
                for j in range(8):
                    kcol = 512*j + 128*hh
                    w_ = [kp_rdy[j], vt_rdy[j]]
                    if j == 0:
                        w_.append(bS.rd)
                    P.emit("tensor", lambda t, bS=bS, kcol=kcol, st=(j == 0), sp=(j == 7): t.matmul(
                        bS.ap[:, 0:128], kp[:, kcol:kcol+128], vt[:, kcol:kcol+128],
                        start=st, stop=sp), waits=w_, inc=tC)
                    w2_ = [bZp.rd] if j == 0 else []
                    mmz = P.emit("tensor", lambda t, bZp=bZp, kcol=kcol, st=(j == 0), sp=(j == 7): t.matmul(
                        bZp.ap[:, 0:1], kp[:, kcol:kcol+128], ones_b,
                        start=st, stop=sp), waits=w2_, inc=tC)
                bS.wr = mmz
                bZp.wr = mmz
                t_sst = P.emit("scalar", lambda s, bS=bS, par=par: s.activation(
                    Sst[:, 132*par:132*par+128], bS.ap[:, 0:128], AF.Copy, scale=mask_f),
                    waits=[mmz, t_cf, prev_end], inc=aC)
                t_sstz = P.emit("scalar", lambda s, bZp=bZp, par=par: s.activation(
                    Sst[:, 132*par+128:132*par+129], bZp.ap[:, 0:1], AF.Copy, scale=mask_f),
                    inc=aC)
                bS.rd = t_sstz
                bZp.rd = t_sstz
                sst_tok[hh] = t_sstz
                t_sstb_all = P.emit("vector", lambda v, par=par: v.tensor_copy(
                    Sst_b[:, 132*par:132*par+129], Sst[:, 132*par:132*par+129]), waits=[t_sstz], inc=vC)
                if g == 0 and dbg_st is not None:
                    P.dma("gpsimd", dbg_st[128*hh:128*(hh+1), :], Sst[:, 132*par:132*par+129], waits=[t_sst], inc=dbq)

                bA = B[par]
                bN = B[2]
                nmm = None
                for sc in range(NSC):
                    w_ = [sq_rdy[hh], t_nrm]
                    if sc == 0:
                        w_.append(bN.rd)
                    nmm = P.emit("tensor", lambda t, sc=sc, hh=hh, g=g: t.matmul(
                        bN.ap[:, 32*hh + 4*sc:32*hh + 4*sc + 4],
                        sjq[:, 1024*hh + 128*sc:1024*hh + 128*sc + 128],
                        normstb[:, 4*(4*g+hh):4*(4*g+hh)+4], start=True, stop=True),
                        waits=w_, inc=tC)
                bN.wr = nmm
                t_mx = P.emit("vector", lambda v, hh=hh, bN=bN: v.tensor_scalar(
                    r_sc[:, 32*hh:32*(hh+1)], bN.ap[:, 32*hh:32*(hh+1)], EPS_ATT, None, ALU.max),
                    waits=[nmm], inc=vC)
                bN.rd = t_mx
                t_rp = P.emit("vector", lambda v, hh=hh: v.reciprocal(
                    r_sc[:, 32*hh:32*(hh+1)], r_sc[:, 32*hh:32*(hh+1)]), fwaits=[t_mx], inc=vC)
                t_rw = t_rp
                for m in range(M):
                    t_rw = P.emit("vector", lambda v, hh=hh, m=m: v.tensor_scalar(
                        r_sc[:, 32*hh+m:32*(hh+1):4], r_sc[:, 32*hh+m:32*(hh+1):4],
                        w128[:, 4*hh+m:4*hh+m+1], None, ALU.mult),
                        waits=[t_w128], fwaits=[t_rw] if m == 0 else [], inc=vC)
                t_drain = t_rw
                amm = None
                for sc in range(NSC):
                    for m in range(M):
                        w_ = [t_mem, sq_rdy[hh]]
                        if m == 0:
                            w_.append(bA.rd)
                        amm = P.emit("tensor", lambda t, sc=sc, m=m, hh=hh, bA=bA, par=par: t.matmul(
                            bA.ap[:, 128*m:128*(m+1)],
                            sjq[:, 1024*hh + 128*sc:1024*hh + 128*sc + 128],
                            membuf[par][:, 128*m:128*(m+1)], start=True, stop=True),
                            waits=w_, inc=tC)
                    bA.wr = amm
                    t_c = P.emit("vector", lambda v, sc=sc, hh=hh, par=par, bA=bA: v.tensor_scalar(
                        memo[:, 1024*par + 128*sc:1024*par + 128*(sc+1)], bA.ap[:, 0:128],
                        r_sc[:, 32*hh + 4*sc:32*hh + 4*sc + 1], None, ALU.mult),
                        waits=[amm], fwaits=[t_drain] if sc == 0 else [], inc=vC)
                    for m in range(1, M):
                        t_c = P.emit("vector", lambda v, sc=sc, hh=hh, par=par, m=m, bA=bA: v.scalar_tensor_tensor(
                            memo[:, 1024*par + 128*sc:1024*par + 128*(sc+1)], bA.ap[:, 128*m:128*(m+1)],
                            r_sc[:, 32*hh + 4*sc + m:32*hh + 4*sc + m + 1],
                            memo[:, 1024*par + 128*sc:1024*par + 128*(sc+1)], ALU.mult, ALU.add),
                            fwaits=[t_c], inc=vC)
                    bA.rd = t_c
                memtok[par] = amm

            state_tok = {hh: None for hh in hpair}
            zu_tok = {hh: None for hh in hpair}
            for j in range(NSC):
                for hh in hpair:
                    par = hh % 2
                    h = 4*g + hh
                    tcn = 8 + j
                    kcol = 512*tcn + 128*hh
                    qcol = 1024*hh + 128*j
                    bT = B[par]; bSc = B[2 + par]; bNm = B[4 + par]; bOw = B[6 + par]
                    t_t1 = P.emit("tensor", lambda t, bT=bT, kcol=kcol: t.transpose(
                        bT.ap.bitcast(BF16)[:, 0:128], kp[:, kcol:kcol+128], id_b),
                        waits=[bT.rd, kp_rdy[tcn]], inc=tC)
                    bT.wr = t_t1
                    t_kpf = P.emit("vector", lambda v, bT=bT, par=par: v.tensor_copy(
                        kpf[:, 128*par:128*(par+1)], bT.ap.bitcast(BF16)[:, 0:128]), waits=[t_t1], inc=vC)
                    bT.rd = t_kpf
                    t_t2 = P.emit("tensor", lambda t, bSc=bSc, par=par, qcol=qcol: t.matmul(
                        bSc.ap[:, 0:128], kpf[:, 128*par:128*(par+1)], sjq[:, qcol:qcol+128],
                        start=True, stop=True), waits=[t_kpf, bSc.rd, sq_rdy[hh]], inc=tC)
                    bSc.wr = t_t2
                    t_scm = P.emit("vector", lambda v, bSc=bSc, par=par: v.tensor_tensor(
                        scm[:, 128*par:128*(par+1)], bSc.ap[:, 0:128], tril_f, ALU.mult),
                        waits=[t_t2], inc=vC)
                    bSc.rd = t_scm
                    if state_tok[hh] is None:
                        Sop = Sst_b[:, 132*par:132*par+129]
                        s_tok = t_sstb_all
                    else:
                        Sop = Scur_b[:, 132*par:132*par+129]
                        s_tok = state_tok[hh]
                    t_t3 = P.emit("tensor", lambda t, bNm=bNm, qcol=qcol, Sop=Sop: t.matmul(
                        bNm.ap[:, 0:128], sjq[:, qcol:qcol+128], Sop[:, 0:128],
                        start=True, stop=False), waits=[s_tok, bNm.rd], inc=tC)
                    t_t4 = P.emit("tensor", lambda t, bNm=bNm, par=par, kcol=kcol: t.matmul(
                        bNm.ap[:, 0:128], scm[:, 128*par:128*(par+1)], vt[:, kcol:kcol+128],
                        start=False, stop=True), waits=[t_scm, vt_rdy[tcn]], inc=tC)
                    t_t5 = P.emit("tensor", lambda t, bNm=bNm, qcol=qcol, Sop=Sop: t.matmul(
                        bNm.ap[:, 128:129], sjq[:, qcol:qcol+128], Sop[:, 128:129],
                        start=True, stop=False), inc=tC)
                    t_t6 = P.emit("tensor", lambda t, bNm=bNm, par=par: t.matmul(
                        bNm.ap[:, 128:129], scm[:, 128*par:128*(par+1)], ones_b,
                        start=False, stop=True), inc=tC)
                    bNm.wr = t_t6
                    w_ = [kp_rdy[tcn], vt_rdy[tcn]]
                    if j == 0:
                        w_.append(bOw.rd)
                    t_t7 = P.emit("tensor", lambda t, bOw=bOw, kcol=kcol: t.matmul(
                        bOw.ap[:, 0:128], kp[:, kcol:kcol+128], vt[:, kcol:kcol+128],
                        start=True, stop=True), waits=w_, inc=tC)
                    bOw.wr = t_t7
                    if j == 0:
                        t_zu = P.emit("vector", lambda v, par=par: v.tensor_reduce(
                            zacc[:, 2*par:2*par+1], kpf[:, 128*par:128*(par+1)], AXX, ALU.add),
                            fwaits=[t_kpf], inc=vC)
                    else:
                        t_zr = P.emit("vector", lambda v, par=par: v.tensor_reduce(
                            denb[:, 2*par+1:2*par+2], kpf[:, 128*par:128*(par+1)], AXX, ALU.add),
                            fwaits=[t_kpf, zu_tok[hh]], inc=vC)
                        t_zu = P.emit("vector", lambda v, par=par: v.tensor_tensor(
                            zacc[:, 2*par:2*par+1], zacc[:, 2*par:2*par+1], denb[:, 2*par+1:2*par+2], ALU.add),
                            fwaits=[t_zr], inc=vC)
                    zu_tok[hh] = t_zu
                    if j < NSC - 1:
                        if j == 0:
                            t_su = P.emit("vector", lambda v, par=par, bOw=bOw: v.tensor_tensor(
                                scur32[:, 128*par:128*(par+1)], Sst[:, 132*par:132*par+128], bOw.ap[:, 0:128], ALU.add),
                                waits=[t_t7, sst_tok[hh]], inc=vC)
                        else:
                            t_su = P.emit("vector", lambda v, par=par, bOw=bOw: v.tensor_tensor(
                                scur32[:, 128*par:128*(par+1)], scur32[:, 128*par:128*(par+1)], bOw.ap[:, 0:128], ALU.add),
                                waits=[t_t7], fwaits=[state_tok[hh]], inc=vC)
                        t_sub = P.emit("vector", lambda v, par=par: v.tensor_copy(
                            Scur_b[:, 132*par:132*par+128], scur32[:, 128*par:128*(par+1)]), fwaits=[t_su], inc=vC)
                        t_suz = P.emit("vector", lambda v, par=par: v.tensor_tensor(
                            Scur_b[:, 132*par+128:132*par+129], Sst[:, 132*par+128:132*par+129], zacc[:, 2*par:2*par+1], ALU.add),
                            fwaits=[t_zu, t_sub], inc=vC)
                        state_tok[hh] = t_suz
                        bOw.rd = t_su
                    t_d1 = P.emit("vector", lambda v, bNm=bNm, par=par: v.tensor_scalar(
                        denb[:, 2*par:2*par+1], bNm.ap[:, 128:129], EPS_ATT, None, ALU.max),
                        waits=[t_t6], inc=vC)
                    t_d2 = P.emit("vector", lambda v, par=par: v.reciprocal(
                        denb[:, 2*par:2*par+1], denb[:, 2*par:2*par+1]), fwaits=[t_d1], inc=vC)
                    t_d3 = P.emit("gpsimd", lambda g, par=par, h=h: g.tensor_scalar(
                        denb[:, 2*par:2*par+1], denb[:, 2*par:2*par+1], gi128[:, h:h+1], None, ALU.mult),
                        waits=[t_gi, t_d2], inc=gC)
                    ccol = 2048*j + 128*h
                    t_c1 = P.emit("vector", lambda v, bNm=bNm, par=par, ccol=ccol: v.tensor_scalar(
                        comb[:, ccol:ccol+128], bNm.ap[:, 0:128], denb[:, 2*par:2*par+1], None, ALU.mult),
                        waits=[t_d3], inc=vC)
                    t_c2 = P.emit("vector", lambda v, par=par, j=j, h=h, ccol=ccol: v.scalar_tensor_tensor(
                        comb[:, ccol:ccol+128], memo[:, 1024*par + 128*j:1024*par + 128*(j+1)],
                        g128[:, h:h+1], comb[:, ccol:ccol+128], ALU.mult, ALU.add),
                        waits=[t_g], fwaits=[t_c1], inc=vC)
                    bNm.rd = t_c2
                gamma_pump(1)
        group_end = P.emit("vector", lambda v: v.tensor_copy(rtmp[:, 3:4], cf[:, 0:1]), waits=[t_cf], fwaits=[group_end], inc=vC)

    dbg_can = want_dbg("can", [128, 16])
    P.barrier("gpsimd", [group_end])
    if dbg_can is not None:
        P.dma("gpsimd", dbg_can[:, 0:4], rtmp, waits=[group_end], inc=dbq)
    if dbg_comb is not None:
        for sc in range(NSC):
            P.dma("gpsimd", dbg_comb[128*sc:128*(sc+1), :], comb[:, 2048*sc:2048*(sc+1)], waits=[group_end], inc=dbq)
    if stop_after < 4:
        return _fin(nc, P, es, outd, dbg_d, locals())

    SD = Arena(SAap, 24 * 1024)
    wob = [SD.alloc(2048, BF16, "wob0"), SD.alloc(2048, BF16, "wob1")]
    for sc in range(NSC):
        for ec in range(16):
            par = (sc * 16 + ec) % 2
            bT = B[par]
            t_tp = P.emit("tensor", lambda t, bT=bT, sc=sc, ec=ec: t.transpose(
                bT.ap.bitcast(BF16)[:, 0:128], comb[:, 2048*sc + 128*ec:2048*sc + 128*(ec+1)], id_b),
                waits=[bT.rd, group_end], inc=tC)
            bT.wr = t_tp
            t_ev = P.emit("vector", lambda v, bT=bT, sc=sc, ec=ec: v.tensor_copy(
                comb_fm[:, 1024*ec + 128*sc:1024*ec + 128*(sc+1)], bT.ap.bitcast(BF16)[:, 0:128]),
                waits=[t_tp], inc=vC)
            bT.rd = t_ev
    cfm_all = P.emit("vector", lambda v: v.tensor_copy(rtmp[:, 3:4], cf[:, 0:1]), fwaits=[group_end], inc=vC)

    wo_tok = [(tC, tC.n), (tC, tC.n)]
    attn_rdy = {}
    for fc in range(16):
        bi = fc % 2
        t_w = P.dma("gpsimd", wob[bi], wod[fc*128*2048:(fc+1)*128*2048].rearrange("(p c) -> p c", p=128),
                    waits=[wo_tok[bi]], inc=qWo[bi])
        banks = [B[2 + 2*(fc % 2)], B[3 + 2*(fc % 2)]]
        mm_last = None
        for ct in range(2):
            first = True
            for ec in range(16):
                w_ = [t_w, cfm_all]
                if first:
                    w_.append(banks[ct].rd)
                mm_last = P.emit("tensor", lambda t, bkk=banks[ct], bi=bi, ec=ec, ct=ct, st=first, sp=(ec == 15): t.matmul(
                    bkk.ap, wob[bi][:, 128*ec:128*(ec+1)],
                    comb_fm[:, 1024*ec + 512*ct:1024*ec + 512*(ct+1)],
                    start=st, stop=sp), waits=w_, inc=tC)
                first = False
            banks[ct].wr = mm_last
        wo_tok[bi] = mm_last
        for ct in range(2):
            t_ev = P.emit("vector", lambda v, bkk=banks[ct], fc=fc, ct=ct: v.tensor_copy(
                attn_fm[:, 1024*fc + 512*ct:1024*fc + 512*(ct+1)], bkk.ap), waits=[banks[ct].wr], inc=vC)
            banks[ct].rd = t_ev
            attn_rdy[fc] = t_ev
    P.barrier("gpsimd", [attn_rdy[15]])
    if dbg_can is not None:
        P.dma("gpsimd", dbg_can[:, 4:8], rtmp, waits=[attn_rdy[15]], inc=dbq)
    dbg_attn = want_dbg("attn", [16*128, 1024])
    if dbg_attn is not None:
        for fc in range(16):
            P.dma("gpsimd", dbg_attn[128*fc:128*(fc+1), :], attn_fm[:, 1024*fc:1024*(fc+1)], waits=[attn_rdy[fc]], inc=dbq)
    if stop_after < 5:
        return _fin(nc, P, es, outd, dbg_d, locals())

    gamma_pump(10**6)
    t_tg = P.emit("tensor", lambda t: t.transpose(B[6].ap[0:4, 0:128], gsum, id_f),
                  waits=[gst["tok"], B[6].rd, t_cf], inc=tC)
    B[6].wr = t_tg
    t_gr = P.emit("vector", lambda v: v.tensor_reduce(rows2[0:4, 24:25], B[6].ap[0:4, 0:128], AXX, ALU.add),
                  waits=[t_tg], inc=vC)
    B[6].rd = t_gr
    t_th = P.emit("vector", lambda v: v.tensor_scalar(rows2[0:4, 25:26], rows2[0:4, 24:25], 0.5/NFFN, 0.5e-8, ALU.mult, ALU.add),
                  fwaits=[t_gr], inc=vC)
    t_tb = P.emit("tensor", lambda t: t.transpose(B[6].ap[0:1, 4:8], rows2[0:4, 25:26], id_f[0:4, 0:4]),
                  waits=[t_th, B[6].rd], inc=tC)
    B[6].wr = t_tb
    t_cpr = P.emit("vector", lambda v: v.tensor_copy(rows2[0:1, 26:30], B[6].ap[0:1, 4:8]), waits=[t_tb], inc=vC)
    B[6].rd = t_cpr
    t_bcm = P.emit("tensor", lambda t: t.matmul(B[6].ap[:, 8:12], onesrow_f, rows2[0:1, 26:30], start=True, stop=True),
                   waits=[t_cpr, B[6].rd], inc=tC)
    B[6].wr = t_bcm
    t_th128 = P.emit("vector", lambda v: v.tensor_copy(th128, B[6].ap[:, 8:12]), waits=[t_bcm], inc=vC)
    B[6].rd = t_th128
    t_thn = P.emit("vector", lambda v: v.tensor_scalar(thn128, th128, -1.0, None, ALU.mult), fwaits=[t_th128], inc=vC)
    dbg_th = want_dbg("th", [128, 4])
    if dbg_th is not None:
        P.dma("gpsimd", dbg_th, th128, waits=[t_th128], inc=dbq)

    SE = Arena(SAap, 24 * 1024)
    wf = [[SE.alloc(512, F32, f"wf{m}{i}") for i in range(2)] for m in range(2)]
    qbb = [[SE.alloc(2048, BF16, f"qb{m}{i}") for i in range(2)] for m in range(2)]
    ab_q = [[memo[:, 1024*m + 512*i:1024*m + 512*(i+1)] for i in range(2)] for m in range(2)]
    ab_q2 = [[elut.bitcast(BF16)[:, 1024*m + 512*i:1024*m + 512*(i+1)] for i in range(2)] for m in range(2)]
    assert SE.off <= 24 * 1024

    wo_done = wo_tok[0] if wo_tok[0][1] > wo_tok[1][1] else wo_tok[1]
    qhz = {"lastQ": [None] * 8}
    def quantize(matj, wfb, qdst, tok_ld, abuf, abuf2, guard=None, lane=0):
        t_a = P.emit("scalar", lambda s: s.activation(abuf, wfb, AF.Sign, bias=thn128[:, matj:matj+1]),
                     waits=[tok_ld, t_thn, qhz["lastQ"][lane]], inc=aC)
        t_b = P.emit("scalar", lambda s: s.activation(abuf2, wfb, AF.Sign, bias=th128[:, matj:matj+1]),
                     waits=[], inc=aC)
        t_q = P.emit("vector", lambda v: v.tensor_tensor(qdst, abuf, abuf2, ALU.add),
                     waits=[t_b, guard], inc=vC)
        qhz["lastQ"][lane] = t_q
        return t_q
    hid_rdy = {}
    wf_tok = [[(tC, tC.n), (tC, tC.n)], [(tC, tC.n), (tC, tC.n)]]
    q_tok = [[None, None], [None, None]]
    dbg_q1 = want_dbg("q1", [256, 2048])
    qb_guard = [[(tC, tC.n), (tC, tC.n)], [(tC, tC.n), (tC, tC.n)]]
    sg_guard = [None, None]
    for hc in range(NHC):
        x1b = [B[0], B[2]]
        x2bk = [B[1], B[3]]
        qtoks = [[], []]
        qb = [qbb[0][hc % 2], qbb[1][hc % 2]]
        for matj, (wd, qbm) in enumerate(((w1d, qb[0]), (w2d, qb[1]))):
            for qt in range(4):
                bi = qt % 2
                t_ld = P.dma("sync", wf[matj][bi],
                             wd[hc*16*128*128:(hc+1)*16*128*128].rearrange("(p c) -> p c", p=128)[:, 512*qt:512*(qt+1)],
                             waits=[wf_tok[matj][bi]], inc=qWf[matj][bi])
                abp = ab_q[matj] if qt % 2 == 0 else ab_q2[matj]
                t_q = quantize(matj, wf[matj][bi], qbm[:, 512*qt:512*(qt+1)], t_ld, abp[0], abp[1], guard=qb_guard[matj][hc % 2], lane=matj * 2 + qt % 2)
                wf_tok[matj][bi] = t_q
                qtoks[matj].append(t_q)
        mm_last = None
        for ct in range(2):
            for matj in range(2):
                bkk = x1b[ct] if matj == 0 else x2bk[ct]
                first = True
                for dc in range(NDC):
                    w_ = [qtoks[matj][dc // 4], xn_tok[dc][1]]
                    if first:
                        w_.append(bkk.rd)
                    mm_last = P.emit("tensor", lambda t, bkk=bkk, qbm_=qb[matj], dc=dc, ct=ct, st=first, sp=(dc == 15): t.matmul(
                        bkk.ap, qbm_[:, 128*dc:128*(dc+1)],
                        xn_own[:, 1024*dc + 512*ct:1024*dc + 512*(ct+1)],
                        start=st, stop=sp), waits=w_, inc=tC)
                    first = False
                    bkk.wr = mm_last
        for _p in range(2):
            qb_guard[0][_p] = mm_last
            qb_guard[1][_p] = mm_last
        for ct in range(2):
            t_sg = P.emit("scalar", lambda s, ct=ct: s.activation(
                sgb[:, 512*ct:512*(ct+1)], x1b[ct].ap, AF.Sigmoid, scale=half_f), waits=[x1b[ct].wr, t_cf, sg_guard[ct]], inc=aC)
            x1b[ct].rd = t_sg
            t_h = P.emit("vector", lambda v, ct=ct, hc=hc: v.scalar_tensor_tensor(
                hidden[:, 1024*hc + 512*ct:1024*hc + 512*(ct+1)], sgb[:, 512*ct:512*(ct+1)],
                0.5, x2bk[ct].ap, ALU.mult, ALU.mult), waits=[t_sg, x2bk[ct].wr, wo_done], inc=vC)
            x2bk[ct].rd = t_h
            sg_guard[ct] = t_h
            hid_rdy[(hc, ct)] = t_h
    P.barrier("gpsimd", [hid_rdy[(NHC-1, 1)]])
    if dbg_can is not None:
        P.dma("gpsimd", dbg_can[:, 8:12], rtmp, waits=[hid_rdy[(NHC-1, 1)]], inc=dbq)
    dbg_hid = want_dbg("hid", [44*128, 1024])
    if dbg_hid is not None:
        for hc in range(NHC):
            P.dma("gpsimd", dbg_hid[128*hc:128*(hc+1), :], hidden[:, 1024*hc:1024*(hc+1)],
                  waits=[hid_rdy[(hc, 0)], hid_rdy[(hc, 1)]], inc=dbq)
    if stop_after < 6:
        return _fin(nc, P, es, outd, dbg_d, locals())

    SF = Arena(ar_t[:, 32 * 1024:64 * 1024], 32 * 1024)
    q3b = [SF.alloc(5632, BF16, "q3a"), SF.alloc(5632, BF16, "q3b")]
    assert SF.off <= 32 * 1024
    SFO = Arena(SAap, 24 * 1024)
    ost = [SFO.alloc(1024, F32, "ost0"), SFO.alloc(1024, F32, "ost1")]
    xot = [SFO.alloc(1024, BF16, "xot0"), SFO.alloc(1024, BF16, "xot1")]
    w3f = [SFO.alloc(704, F32, "w3f0"), SFO.alloc(704, F32, "w3f1")]
    ab3p = [[SFO.alloc(704, BF16, f"ab3{i}{j}") for j in range(2)] for i in range(2)]
    assert SFO.off <= 24 * 1024
    gE = (gC, gC.n)
    vE = (vC, vC.n)

    w3_tok = [(tC, tC.n), (tC, tC.n)]
    xo_tok = [None, None]
    ost_tok = [None, None]
    last_out = None
    phaseE_done = mm_last
    q3_guard = [(tC, tC.n), (tC, tC.n)]
    for fc in range(16):
        q3 = q3b[fc % 2]
        qq_toks = []
        for qt in range(8):
            bi = qt % 2
            t_ld = P.dma("sync", w3f[bi],
                         w3d[fc*44*128*128:(fc+1)*44*128*128].rearrange("(p c) -> p c", p=128)[:, 704*qt:704*(qt+1)],
                         waits=[w3_tok[bi], phaseE_done], inc=qW3[bi])
            t_q = quantize(2, w3f[bi], q3[:, 704*qt:704*(qt+1)], t_ld, ab3p[bi][0], ab3p[bi][1], guard=q3_guard[fc % 2], lane=4 + bi)
            w3_tok[bi] = t_q
            qq_toks.append(t_q)
        banks = [B[4 + 2*(fc % 2)], B[5 + 2*(fc % 2)]]
        mm3 = None
        for ct in range(2):
            first = True
            for hcc in range(NHC):
                w_ = [qq_toks[(128 * hcc + 127) // 704], hid_rdy[(hcc, ct)]]
                if first:
                    w_.append(banks[ct].rd)
                mm3 = P.emit("tensor", lambda t, bkk=banks[ct], q3_=q3, hcc=hcc, ct=ct, st=first, sp=(hcc == NHC-1): t.matmul(
                    bkk.ap, q3_[:, 128*hcc:128*(hcc+1)],
                    hidden[:, 1024*hcc + 512*ct:1024*hcc + 512*(ct+1)],
                    start=st, stop=sp), waits=w_, inc=tC)
                first = False
            banks[ct].wr = mm3
        q3_guard[fc % 2] = mm3
        bi = fc % 2
        t_xo = P.dma("gpsimd", xot[bi], xview(fc)[:, 1024:2048], waits=[xo_tok[bi], phaseE_done, gE, vE], inc=qXo[bi])
        for ct in range(2):
            t_a1 = P.emit("vector", lambda v, bkk=banks[ct], fc=fc, ct=ct, bi=bi: v.scalar_tensor_tensor(
                ost[bi][:, 512*ct:512*(ct+1)], bkk.ap, 0.5, attn_fm[:, 1024*fc + 512*ct:1024*fc + 512*(ct+1)],
                ALU.mult, ALU.add),
                waits=[banks[ct].wr, attn_rdy[fc], ost_tok[bi], gE], inc=vC)
            banks[ct].rd = t_a1
            t_a2 = P.emit("gpsimd", lambda g, fc=fc, ct=ct, bi=bi: g.tensor_tensor(
                ost[bi][:, 512*ct:512*(ct+1)], ost[bi][:, 512*ct:512*(ct+1)],
                xot[bi][:, 512*ct:512*(ct+1)], ALU.add), waits=[t_a1, t_xo], inc=gC)
        xo_tok[bi] = t_a2
        last_out = P.dma("sync", outd[fc*128*TO:(fc+1)*128*TO].rearrange("(p c) -> p c", p=128),
                         ost[bi], waits=[t_a2], inc=qO[bi])
        ost_tok[bi] = last_out

    if dbg_can is not None:
        P.dma("gpsimd", dbg_can[:, 12:16], rtmp, waits=[last_out], inc=dbq)
    P.barrier("sync", [(qO[0], qO[0].n), (qO[1], qO[1].n)])
    return _fin(nc, P, es, outd, dbg_d, locals())


def _fin(nc, P, es, outd, dbg_d, lcl):
    es2 = ExitStack()
    for ch in P.chans:
        ch.sem = es2.enter_context(nc.semaphore(ch.name))
    with es2:
        with nc.Block() as block:
            P.finalize(block)
    es.close()
    return nc, dbg_d


def host_inputs(full, core):
    b, half = core // 2, core % 2
    x = np.asarray(full["L0_attn_hidden_states"])
    own = x[b, 1024*half:1024*half+1024]
    pre = x[b, 1024*(1-half):1024*(1-half)+1024]
    ctx = np.concatenate([pre, own], axis=0)
    xctx = np.ascontiguousarray(ctx.T).reshape(-1)
    def wtile(w):
        return np.ascontiguousarray(np.asarray(w).T.reshape(16, 128, 4, 512).transpose(2, 0, 1, 3)).reshape(-1)
    wo = np.asarray(full["L0_attn_w_o"]).T.reshape(16, 128, 16, 128).transpose(2, 1, 0, 3)
    w1 = np.asarray(full["L0_ffn_w1"]).T.reshape(16, 128, 44, 128).transpose(2, 1, 0, 3)
    w2 = np.asarray(full["L0_ffn_w2"]).T.reshape(16, 128, 44, 128).transpose(2, 1, 0, 3)
    w3 = np.asarray(full["L0_ffn_w3"]).T.reshape(44, 128, 16, 128).transpose(2, 1, 0, 3)
    mems = np.asarray(full["L0_attn_memories"]).transpose(1, 2, 0, 3)
    normst = np.ascontiguousarray(np.asarray(full["L0_attn_memory_norms"]).transpose(2, 1, 0).reshape(128, 64))
    nt = np.asarray(full["L0_attn_memory_norms"]).transpose(2, 1, 0)
    normst = np.ascontiguousarray(nt.reshape(128, 64))
    gater = np.tile(np.asarray(full["L0_attn_gate"]).reshape(1, 16), (128, 1)).astype(np.float32)
    cstf = np.zeros((128, 390), np.float32)
    cstf[:, 0:128] = np.eye(128)
    cstf[:, 128:256] = 1.0
    cstf[:, 257] = float(half)
    cstf[0, 258] = LN_EPS
    cstf[:, 259] = 0.5
    jj, ii = np.mgrid[0:128, 0:128]
    cstf[:, 260:388] = (jj <= ii).astype(np.float32)
    cstb = np.zeros((128, 386), np.float32)
    cstb[:, 0] = 1.0
    cstb[:, 1:129] = np.eye(128)
    cstb[:, 129:257] = (jj <= ii)
    cstb[:, 257:385] = 1.0
    import ml_dtypes
    return {
        "xctx": xctx.astype(np.float32),
        "wq": wtile(full["L0_attn_w_q"]).astype(np.float32),
        "wk": wtile(full["L0_attn_w_k"]).astype(np.float32),
        "wv": wtile(full["L0_attn_w_v"]).astype(np.float32),
        "wo": np.ascontiguousarray(wo).reshape(-1).astype(np.float32),
        "w1": np.ascontiguousarray(w1).reshape(-1).astype(np.float32),
        "w2": np.ascontiguousarray(w2).reshape(-1).astype(np.float32),
        "w3": np.ascontiguousarray(w3).reshape(-1).astype(np.float32),
        "mems": np.ascontiguousarray(mems).reshape(-1).astype(np.float32),
        "normst": normst.astype(np.float32),
        "gater": gater,
        "cstf": cstf,
        "cstb": cstb.astype(ml_dtypes.bfloat16),
    }


def kernel(**inputs):
    from concourse.bass_utils import run_bass_kernel_spmd
    nc, _ = build(dbg_names=(), stop_after=99)
    in_maps = [host_inputs(inputs, c) for c in range(8)]
    res = run_bass_kernel_spmd(nc, in_maps, core_ids=list(range(8)))
    out = np.zeros((4, 2048, 2048), dtype=np.float32)
    for c in range(8):
        b, half = c // 2, c % 2
        o = np.asarray(res.results[c]["out"]).reshape(2048, 1024)
        out[b, 1024*half:1024*(half+1), :] = o.T
    return out


# version 09
# speedup vs baseline: 1.1077x; 1.0376x over previous
import numpy as np
import concourse.bass as bass
import concourse.mybir as mybir
from contextlib import ExitStack

F32 = mybir.dt.float32
BF16 = mybir.dt.bfloat16
U8 = mybir.dt.uint8
AF = mybir.ActivationFunctionType
ALU = mybir.AluOpType
AXX = mybir.AxisListType.X

ENGINES = ("sync", "tensor", "vector", "scalar", "gpsimd")

D = 2048; T = 2048; TO = 1024; NDC = 16
H = 16; HD = 128; NG = 4; M = 4; CH = 128; NSC = 8
HF = 5632; NHC = 44
SCALE = float(HD) ** -0.5
LN_EPS = 1e-5
EPS_ATT = 1e-6
NFFN = float(HF * D)


class Ch:
    __slots__ = ("sem", "n", "name", "dma")
    def __init__(self, name, dma=False):
        self.sem = None; self.n = 0; self.name = name; self.dma = dma


class Prog:
    def __init__(self, nc):
        self.nc = nc
        self.streams = {e: [] for e in ENGINES}
        self.chans = []
        self.seen = {e: {} for e in ENGINES}

    def chan(self, name, dma=False):
        ch = Ch(name, dma); self.chans.append(ch); return ch

    def emit(self, eng, fn, waits=(), inc=None, inc_amt=1, fwaits=()):
        waits = [w for w in waits if w is not None]
        fwaits = [w for w in fwaits if w is not None]
        seen = self.seen[eng]
        needed = list(fwaits)
        for (ch, v) in fwaits:
            if seen.get(ch, 0) < v:
                seen[ch] = v
        for (ch, v) in waits:
            if seen.get(ch, 0) < v:
                needed.append((ch, v)); seen[ch] = v
        tok = None
        if inc is not None:
            inc.n += inc_amt
            tok = (inc, inc.n)
        self.streams[eng].append((needed, fn, (inc, inc_amt) if inc else None))
        return tok

    def dma(self, eng, out, in_, waits=(), inc=None):
        assert inc is not None
        w = list(waits) + ([(inc, inc.n)] if inc.n else [])
        return self.emit(eng, lambda e: e.dma_start(out=out, in_=in_),
                         waits=w, inc=inc, inc_amt=16)

    def barrier(self, eng, waits):
        self.emit(eng, None, waits=waits)

    def finalize(self, block):
        def mk(eng_name):
            stream = self.streams[eng_name]
            def body(e):
                for needed, fn, inc in stream:
                    for (ch, v) in needed:
                        e.wait_ge(ch.sem, v)
                    if fn is None:
                        continue
                    ins = fn(e)
                    if inc is not None:
                        ch, amt = inc
                        ins.then_inc(ch.sem, amt)
            return body
        for eng_name in ENGINES:
            if self.streams[eng_name]:
                getattr(block, eng_name)(mk(eng_name))


class Arena:
    def __init__(self, ap, nbytes):
        self.ap = ap; self.nbytes = nbytes; self.off = 0

    def alloc(self, cols, dtype, name=""):
        isz = 2 if dtype == BF16 else 4
        self.off = (self.off + 3) & ~3
        nb = cols * isz
        assert self.off + nb <= self.nbytes, f"arena overflow {name}: {self.off+nb}>{self.nbytes}"
        v = self.ap[:, self.off:self.off + nb].bitcast(dtype)
        self.off += nb
        return v


class Bank:
    __slots__ = ("ap", "rd", "wr")
    def __init__(self, ap):
        self.ap = ap; self.rd = None; self.wr = None


def build(dbg_names=(), stop_after=99):
    nc = bass.Bass("TRN2", target_bir_lowering=False)

    def din(name, shape, dt=F32):
        return nc.dram_tensor(name, shape, dt, kind="ExternalInput").ap()

    xctx = din("xctx", [D * T])
    wqd = din("wq", [NG * NDC * 128 * 512])
    wkd = din("wk", [NG * NDC * 128 * 512])
    wvd = din("wv", [NG * NDC * 128 * 512])
    wod = din("wo", [16 * 128 * 16 * 128])
    w1d = din("w1", [NHC * 128 * NDC * 128])
    w2d = din("w2", [NHC * 128 * NDC * 128])
    w3d = din("w3", [16 * 128 * NHC * 128])
    memd = din("mems", [H * 128 * M * 128])
    normd = din("normst", [128, 64])
    gated = din("gater", [128, 16])
    cstfd = din("cstf", [128, 390])
    cstbd = din("cstb", [128, 386], BF16)

    outd = nc.dram_tensor("out", [NDC * 128 * TO], F32, kind="ExternalOutput").ap()

    dbg_d = {}
    def want_dbg(name, shape):
        if name in dbg_names:
            dbg_d[name] = nc.dram_tensor("dbg_" + name, shape, F32, kind="ExternalOutput").ap()
            return dbg_d[name]
        return None

    es = ExitStack()
    SB_BYTES = 206 * 1024
    ar_t = es.enter_context(nc.sbuf_tensor("arena", [128, SB_BYTES], U8))
    pb_t = [es.enter_context(nc.psum_tensor(f"pb{i}", [128, 512], F32)) for i in range(8)]
    B = [Bank(t[:]) for t in pb_t]

    A = Arena(ar_t[:], SB_BYTES)
    xn_pre = A.alloc(NDC * 1024, BF16, "xn_pre")
    xn_own = A.alloc(NDC * 1024, BF16, "xn_own")
    R1ap = ar_t[:, A.off:A.off + 91 * 1024]
    R1 = Arena(R1ap, 91 * 1024)
    A.off += 91 * 1024
    SAap = ar_t[:, A.off:A.off + 24 * 1024]
    A.off += 24 * 1024
    A128 = A.alloc(2048, BF16, "A128")
    B128 = A.alloc(2048, BF16, "B128")
    sxn = A.alloc(16, F32, "sxn")
    sxn_b = A.alloc(16, BF16, "sxn_b")
    cb = A.alloc(386, BF16, "cb")
    cf = A.alloc(390, F32, "cf")
    gsum = A.alloc(4, F32, "gsum")
    rtmp = A.alloc(4, F32, "rtmp")
    gacc = A.alloc(2, F32, "gacc")
    th128 = A.alloc(4, F32, "th128")
    thn128 = A.alloc(4, F32, "thn128")
    g128 = A.alloc(16, F32, "g128")
    gi128 = A.alloc(16, F32, "gi128")
    w128 = A.alloc(16, F32, "w128")
    sq_sb = A.alloc(4, BF16, "sq_sb")
    normstb = A.alloc(64, BF16, "normstb")
    nsb = A.alloc(4, BF16, "nsb")
    nf4 = A.alloc(4, F32, "nf4")
    relP = A.alloc(16, BF16, "relP")
    r_sc = A.alloc(4 * 32, F32, "r_sc")
    Sst = A.alloc(2 * 132, F32, "Sst")
    Sst_b = A.alloc(2 * 132, BF16, "Sst_b")
    Scur_b = A.alloc(2 * 132, BF16, "Scur_b")
    kpf = A.alloc(2 * 128, BF16, "kpf")
    scm = A.alloc(2 * 128, BF16, "scm")
    denb = A.alloc(2 * 2, F32, "denb")
    zacc = A.alloc(2 * 2, F32, "zacc")
    elut = A.alloc(2 * 512, F32, "elut")
    sgb = A.alloc(2 * 512, BF16, "sgb")
    memo = A.alloc(2 * 1024, BF16, "memo")
    rows = A.alloc(512, F32, "rows")
    rows2 = A.alloc(58, F32, "rows2")
    rowsb = A.alloc(16, BF16, "rowsb")
    assert A.off <= SB_BYTES, A.off

    kp = R1.alloc(16 * 512, BF16, "kp")
    vt = R1.alloc(16 * 512, BF16, "vt")
    sjq = R1.alloc(4 * 1024, BF16, "sq")
    comb = R1.alloc(8 * 2048, BF16, "comb")
    gbuf = [R1.alloc(1408, F32, "gbuf0"), R1.alloc(1408, F32, "gbuf1")]
    scur32 = R1.alloc(2 * 128, F32, "scur32")
    hidden = Arena(R1ap, R1.nbytes).alloc(NHC * 1024, BF16, "hidden")
    comb_fm = Arena(R1ap, 40 * 1024).alloc(16 * 1024, BF16, "comb_fm")
    attn_fm = xn_pre

    ones_b = cb[:, 0:1]
    id_b = cb[:, 1:129]
    onesrow_b = cb[0:1, 257:385]
    id_f = cf[:, 0:128]
    onesrow_f = cf[0:1, 128:256]
    mask_f = cf[:, 257:258]
    lneps_f = cf[0:1, 258:259]
    half_f = cf[:, 259:260]
    tril_f = cf[:, 260:388]

    P = Prog(nc)
    qS = P.chan("qS", dma=True); qG = P.chan("qG", dma=True)
    qO = [P.chan("qO0", dma=True), P.chan("qO1", dma=True)]
    qCb = P.chan("qCb", dma=True); qCf = P.chan("qCf", dma=True); qGt = P.chan("qGt", dma=True)
    qNrm = P.chan("qNrm", dma=True)
    qGm = [P.chan("qGm0", dma=True), P.chan("qGm1", dma=True)]
    qXb = [P.chan(f"qXb{i}", dma=True) for i in range(4)]
    qWk = [P.chan("qWk0", dma=True), P.chan("qWk1", dma=True)]
    qMb = [P.chan("qMb0", dma=True), P.chan("qMb1", dma=True)]
    qWo = [P.chan("qWo0", dma=True), P.chan("qWo1", dma=True)]
    qWf = [[P.chan(f"qWf{m}{i}", dma=True) for i in range(2)] for m in range(2)]
    qW3 = [P.chan("qW30", dma=True), P.chan("qW31", dma=True)]
    qXo = [P.chan("qXo0", dma=True), P.chan("qXo1", dma=True)]
    tC = P.chan("tC"); vC = P.chan("vC"); aC = P.chan("aC"); gC = P.chan("gC")

    t_cb = P.dma("sync", cb, cstbd, inc=qCb)
    t_cf = P.dma("sync", cf, cstfd, inc=qCf)
    t_nrm = P.dma("gpsimd", normstb, normd, inc=qNrm)
    t_gate = P.dma("sync", rows2[:, 0:16], gated, inc=qGt)
    t_g = P.emit("scalar", lambda s: s.activation(g128, rows2[:, 0:16], AF.Sigmoid), waits=[t_gate], inc=aC)
    t_gi = P.emit("vector", lambda v: v.tensor_scalar(gi128, g128, -1.0, 1.0, ALU.mult, ALU.add), waits=[t_g], inc=vC)
    t_gz = P.emit("gpsimd", lambda g: g.memset(gsum, 0.0), inc=gC)

    GT_PER = 64
    gamma_tiles = [(j, wd, t) for j, wd in enumerate((w1d, w2d, w3d)) for t in range(GT_PER)]
    gst = {"i": 0, "last_rd": [None, None], "last_ac": [None, None], "tok": None}

    def gamma_pump(n):
        for _ in range(n):
            i = gst["i"]
            if i >= len(gamma_tiles):
                return
            gst["i"] = i + 1
            j, wd, t = gamma_tiles[i]
            bi = i % 2
            gb = gbuf[bi]
            off = t * 128 * 1408
            t_ld = P.dma("sync", gb, wd[off:off + 128 * 1408].rearrange("(p c) -> p c", p=128),
                         waits=[gst["last_rd"][bi]], inc=qGm[bi])
            t_rd = P.emit("scalar", lambda s, gb=gb, bi=bi: s.activation(gb, gb, AF.Abs, accum_out=gacc[:, bi:bi+1]),
                          waits=[t_ld, t_gz, gst["last_ac"][bi]], inc=aC)
            t_ac = P.emit("gpsimd", lambda g, j=j, bi=bi: g.tensor_tensor(gsum[:, j:j+1], gsum[:, j:j+1], gacc[:, bi:bi+1], ALU.add),
                          waits=[t_rd], fwaits=[gst["tok"]], inc=gC)
            gst["last_rd"][bi] = t_rd
            gst["last_ac"][bi] = t_ac
            gst["tok"] = t_ac

    def xview(dc):
        return xctx[dc * 128 * T:(dc + 1) * 128 * T].rearrange("(p t) -> p t", p=128)

    xpark = Arena(R1ap, 64 * 1024).alloc(NDC * 2048, BF16, "xpark")
    SA = Arena(SAap, 24 * 1024)
    xb = [SA.alloc(2048, BF16, "xb0"), SA.alloc(2048, BF16, "xb1")]
    x2b = [SA.alloc(2048, BF16, "x2b0"), SA.alloc(2048, BF16, "x2b1")]
    arow_b = SA.alloc(2048, BF16, "arow_b")
    brow_b = SA.alloc(2048, BF16, "brow_b")
    assert SA.off <= 24 * 1024

    last_mm = [None, None]
    stat_last = None
    xpark_tok = {}
    xp_prev = [None] * 4
    for dc in range(NDC):
        bi = dc % 4
        xp = xpark[:, 2048*dc:2048*(dc+1)]
        t_x = P.dma("gpsimd", xp, xview(dc), waits=[xp_prev[bi]], inc=qXb[bi])
        xp_prev[bi] = t_x
        xpark_tok[dc] = t_x
        b2 = dc % 2
        t_sq = P.emit("vector", lambda v, b2=b2, xp=xp: v.tensor_mul(x2b[b2], xp, xp),
                      waits=[t_x, last_mm[b2]], inc=vC)
        for tt in range(4):
            st, sp = dc == 0, dc == NDC - 1
            P.emit("tensor", lambda t, xp=xp, tt=tt, st=st, sp=sp: t.matmul(
                B[tt].ap[0:1, :], ones_b, xp[:, 512*tt:512*(tt+1)], start=st, stop=sp),
                waits=[t_x, t_cb], inc=tC)
            stat_last = P.emit("tensor", lambda t, b2=b2, tt=tt, st=st, sp=sp: t.matmul(
                B[4+tt].ap[0:1, :], ones_b, x2b[b2][:, 512*tt:512*(tt+1)], start=st, stop=sp),
                waits=[t_sq], inc=tC)
        last_mm[b2] = stat_last
        gamma_pump(1)

    t_fin = None
    for tt in range(4):
        t_mn = P.emit("vector", lambda v, tt=tt: v.tensor_scalar(
            B[tt].ap[0:1, :], B[tt].ap[0:1, :], 1.0/D, None, ALU.mult),
            waits=[stat_last], inc=vC)
        t_e2 = P.emit("vector", lambda v, tt=tt: v.tensor_scalar(
            B[4+tt].ap[0:1, :], B[4+tt].ap[0:1, :], 1.0/D, None, ALU.mult), inc=vC)
        t_ms = P.emit("scalar", lambda s, tt=tt: s.activation(
            rows[0:1, 0:512], B[tt].ap[0:1, :], AF.Square), waits=[t_mn], inc=aC)
        t_vr = P.emit("vector", lambda v, tt=tt: v.tensor_sub(
            B[4+tt].ap[0:1, :], B[4+tt].ap[0:1, :], rows[0:1, 0:512]), waits=[t_ms], fwaits=[t_e2], inc=vC)
        t_sq2 = P.emit("scalar", lambda s, tt=tt: s.activation(
            B[4+tt].ap[0:1, :], B[4+tt].ap[0:1, :], AF.Sqrt, bias=lneps_f),
            waits=[t_vr, t_cf], inc=aC)
        t_rs = P.emit("vector", lambda v, tt=tt: v.reciprocal(
            B[4+tt].ap[0:1, :], B[4+tt].ap[0:1, :]), waits=[t_sq2], inc=vC)
        t_a = P.emit("vector", lambda v, tt=tt: v.tensor_copy(
            arow_b[0:1, 512*tt:512*(tt+1)], B[4+tt].ap[0:1, :]), fwaits=[t_rs], inc=vC)
        t_mc = P.emit("vector", lambda v, tt=tt: v.tensor_copy(rows[0:1, 0:512], B[tt].ap[0:1, :]), inc=vC)
        t_b = P.emit("vector", lambda v, tt=tt: v.scalar_tensor_tensor(
            brow_b[0:1, 512*tt:512*(tt+1)], rows[0:1, 0:512], -1.0, B[4+tt].ap[0:1, :],
            ALU.mult, ALU.mult), waits=[t_mc], inc=vC)
        t_fin = t_b
        B[tt].rd = t_b
        B[4+tt].rd = t_b
    t_ab = t_fin
    t_bbb = t_fin
    bc_last = None
    for tt in range(4):
        t1 = P.emit("tensor", lambda t, tt=tt: t.matmul(
            B[tt].ap, onesrow_b, arow_b[0:1, 512*tt:512*(tt+1)], start=True, stop=True),
            waits=[t_ab, B[tt].rd], inc=tC)
        t2 = P.emit("tensor", lambda t, tt=tt: t.matmul(
            B[4+tt].ap, onesrow_b, brow_b[0:1, 512*tt:512*(tt+1)], start=True, stop=True),
            waits=[t_bbb, B[4+tt].rd], inc=tC)
        e1 = P.emit("vector", lambda v, tt=tt: v.tensor_copy(A128[:, 512*tt:512*(tt+1)], B[tt].ap), waits=[t1], inc=vC)
        e2 = P.emit("vector", lambda v, tt=tt: v.tensor_copy(B128[:, 512*tt:512*(tt+1)], B[4+tt].ap), waits=[t2], inc=vC)
        B[tt].rd, B[tt].wr = e1, t1
        B[4+tt].rd, B[4+tt].wr = e2, t2
        bc_last = e2
    gamma_pump(2)

    last_use = [None, None]
    sxn_toks = []
    xn_tok = {}
    for dc in range(NDC):
        bi = dc % 2
        xp = xpark[:, 2048*dc:2048*(dc+1)]
        hts = []
        for hfi, dst in ((0, xn_pre), (1, xn_own)):
            sl = slice(1024*hfi, 1024*(hfi+1))
            t_p1 = P.emit("vector", lambda v, bi=bi, sl=sl, xp=xp: v.tensor_tensor(
                x2b[bi][:, sl], xp[:, sl], A128[:, sl], ALU.mult), waits=[xpark_tok[dc], bc_last, last_use[bi]], inc=vC)
            t_p2 = P.emit("vector", lambda v, bi=bi, sl=sl, dst=dst, dc=dc: v.tensor_tensor(
                dst[:, 1024*dc:1024*(dc+1)], x2b[bi][:, sl], B128[:, sl], ALU.add),
                waits=[t_p1], inc=vC)
            hts.append(t_p2)
        xn_tok[dc] = hts
        t_r1 = P.emit("vector", lambda v, dc=dc: v.tensor_reduce(rtmp[:, 1:2], xn_pre[:, 1024*dc:1024*(dc+1)], AXX, ALU.add), waits=[hts[0]], inc=vC)
        t_r2 = P.emit("vector", lambda v, dc=dc: v.tensor_reduce(rtmp[:, 2:3], xn_own[:, 1024*dc:1024*(dc+1)], AXX, ALU.add), waits=[hts[1]], inc=vC)
        t_r3 = P.emit("vector", lambda v, dc=dc: v.tensor_tensor(sxn[:, dc:dc+1], rtmp[:, 1:2], rtmp[:, 2:3], ALU.add), fwaits=[t_r2], inc=vC)
        sxn_toks.append(t_r3)
        last_use[bi] = hts[1]
        gamma_pump(1)
    t_sxnb = P.emit("vector", lambda v: v.tensor_copy(sxn_b, sxn), waits=[sxn_toks[-1]], inc=vC)

    dbq = P.chan("dbq", dma=True)
    dbg_xn = want_dbg("xn", [NDC * 128, T])
    if dbg_xn is not None:
        for dc in range(NDC):
            P.dma("gpsimd", dbg_xn[128*dc:128*(dc+1), 0:1024], xn_pre[:, 1024*dc:1024*(dc+1)], waits=xn_tok[dc], inc=dbq)
            P.dma("gpsimd", dbg_xn[128*dc:128*(dc+1), 1024:2048], xn_own[:, 1024*dc:1024*(dc+1)], waits=xn_tok[dc], inc=dbq)
    dbg_sxn = want_dbg("sxn", [128, 16])
    if dbg_sxn is not None:
        P.dma("gpsimd", dbg_sxn, sxn, waits=sxn_toks, inc=dbq)
    dbg_a = want_dbg("a128", [128, 2048])
    if dbg_a is not None:
        P.dma("gpsimd", dbg_a, A128, waits=[bc_last], inc=dbq)
    dbg_bb = want_dbg("b128", [128, 2048])
    if dbg_bb is not None:
        P.dma("gpsimd", dbg_bb, B128, waits=[bc_last], inc=dbq)
    dbg_ar = want_dbg("arow", [1, 2048])
    if dbg_ar is not None:
        P.dma("gpsimd", dbg_ar, arow_b[0:1, :], waits=[t_fin], inc=dbq)
    dbg_br = want_dbg("brow", [1, 2048])
    if dbg_br is not None:
        P.dma("gpsimd", dbg_br, brow_b[0:1, :], waits=[t_fin], inc=dbq)
    if stop_after < 1:
        return _fin(nc, P, es, outd, dbg_d, locals())

    SC = Arena(SAap, 24 * 1024)
    wkb = [SC.alloc(2*512, BF16, f"wkb{i}") for i in range(8)]
    membuf = [SC.alloc(512, BF16, "mem0"), SC.alloc(512, BF16, "mem1")]
    assert SC.off <= 24 * 1024
    wkb_tok = [(vC, vC.n)] * 8
    qWk4 = [P.chan(f"qWk4_{i}", dma=True) for i in range(8)]

    def wslice(wd, g, dc0, ndc):
        off = (g * NDC + dc0) * 128 * 512
        return wd[off:off + ndc * 128 * 512].rearrange("(a p c) -> p a c", a=ndc, p=128)

    elu_state = {"last": [None, None]}
    def elu_evac(bk, dst, par, extra=()):
        t_min = P.emit("vector", lambda v, bk=bk, par=par: v.tensor_scalar(
            elut[:, 512*par:512*(par+1)], bk.ap, 0.0, None, ALU.min),
            waits=[bk.wr, elu_state["last"][par]] + list(extra), inc=vC)
        t_exp = P.emit("scalar", lambda s, par=par: s.activation(
            elut[:, 512*par:512*(par+1)], elut[:, 512*par:512*(par+1)], AF.Exp),
            waits=[t_min], inc=aC)
        t_f = P.emit("vector", lambda v, bk=bk, dst=dst, par=par: v.scalar_tensor_tensor(
            dst, bk.ap, 1.0, elut[:, 512*par:512*(par+1)], ALU.add, ALU.max),
            waits=[t_exp], inc=vC)
        elu_state["last"][par] = t_f
        bk.rd = t_f
        return t_f

    kp_rdy = [None] * 16
    vt_rdy = [None] * 16
    sq_rdy = [None] * 4
    group_end = None
    t_er = t_ier = None
    e_row = rows2[0:1, 16:20]
    ie_row = rows2[0:1, 20:24]
    t_w128 = None
    t_sqs = None

    dbg_kp = want_dbg("kp", [16*128, 512])
    dbg_vt = want_dbg("vt", [16*128, 512])
    dbg_sq = want_dbg("sigq", [4*128, 1024])
    dbg_w = want_dbg("wmh", [128, 16])
    dbg_comb = want_dbg("comb", [8*128, 2048])
    dbg_st = want_dbg("sst", [4*128, 129])

    ngroups = NG if stop_after >= 3 else 1
    for g in range(ngroups):
        prev_end = group_end
        for wd, dstbuf, is_k, rdy in ((wkd, kp, True, kp_rdy), (wvd, vt, False, vt_rdy)):
            for tcb in range(4):
                banks = [B[(tcb % 2) * 4 + i] for i in range(4)]
                first = [True] * 4
                mm_last = None
                for dcb in range(8):
                    bi = (tcb * 8 + dcb) % 8
                    t_w = P.dma("gpsimd", wkb[bi][:].rearrange("p (a c) -> p a c", a=2),
                                wslice(wd, g, 2*dcb, 2), waits=[wkb_tok[bi]], inc=qWk4[bi])
                    for dci in range(2):
                        dc = 2*dcb + dci
                        for i4 in range(4):
                            tcn = 4*tcb + i4
                            xsrc = xn_pre if tcn < 8 else xn_own
                            xcol = 1024*dc + 128*(tcn % 8)
                            bkk = banks[i4]
                            w_ = [t_w, xn_tok[dc][0 if tcn < 8 else 1]]
                            if first[i4]:
                                w_.append(bkk.rd)
                            mm_last = P.emit("tensor", lambda t, bkk=bkk, xsrc=xsrc, xcol=xcol, bi=bi, dci=dci, st=first[i4], sp=(dc == 15): t.matmul(
                                bkk.ap, xsrc[:, xcol:xcol+128], wkb[bi][:, 512*dci:512*(dci+1)],
                                start=st, stop=sp), waits=w_, inc=tC)
                            first[i4] = False
                            bkk.wr = mm_last
                    wkb_tok[bi] = mm_last
                for i4 in range(4):
                    tcn = 4*tcb + i4
                    bkk = banks[i4]
                    dst = dstbuf[:, 512*tcn:512*(tcn+1)]
                    if is_k:
                        rdy[tcn] = elu_evac(bkk, dst, i4 % 2)
                    else:
                        rdy[tcn] = P.emit("vector", lambda v, bkk=bkk, dst=dst: v.tensor_copy(dst, bkk.ap), waits=[bkk.wr], inc=vC)
                        bkk.rd = rdy[tcn]
                gamma_pump(2)
        if g == 0 and dbg_kp is not None:
            for tcn in range(16):
                P.dma("gpsimd", dbg_kp[128*tcn:128*(tcn+1), :], kp[:, 512*tcn:512*(tcn+1)], waits=[kp_rdy[tcn]], inc=dbq)
        if g == 0 and dbg_vt is not None:
            for tcn in range(16):
                P.dma("gpsimd", dbg_vt[128*tcn:128*(tcn+1), :], vt[:, 512*tcn:512*(tcn+1)], waits=[vt_rdy[tcn]], inc=dbq)

        sqbk = B[4]
        for p2 in (0, 1):
            qbs = [B[0], B[1], B[2], B[3]] if p2 == 0 else [B[5], B[6], B[7], B[0]]
            firstq = [True] * 4
            firstsq = True
            for dcb in range(8):
                bi = (p2 * 8 + dcb) % 8
                t_w = P.dma("gpsimd", wkb[bi][:].rearrange("p (a c) -> p a c", a=2),
                            wslice(wqd, g, 2*dcb, 2), waits=[wkb_tok[bi]], inc=qWk4[bi])
                mm_last = None
                for dci in range(2):
                    dc = 2*dcb + dci
                    for ei in range(2):
                        e128 = 2*p2 + ei
                        for ct in range(2):
                            qi = 2*ei + ct
                            bkk = qbs[qi]
                            w_ = [t_w, xn_tok[dc][1]]
                            if firstq[qi]:
                                w_.append(bkk.rd)
                            ecol = 512*dci + 128*(2*p2 + ei)
                            mm_last = P.emit("tensor", lambda t, bkk=bkk, bi=bi, ecol=ecol, dc=dc, ct=ct, st=firstq[qi], sp=(dc == 15): t.matmul(
                                bkk.ap, wkb[bi][:, ecol:ecol+128],
                                xn_own[:, 1024*dc + 512*ct:1024*dc + 512*(ct+1)],
                                start=st, stop=sp), waits=w_, inc=tC)
                            firstq[qi] = False
                            bkk.wr = mm_last
                            wkb_tok[bi] = mm_last
                    if p2 == 0:
                        w_ = [t_sxnb]
                        if firstsq:
                            w_.append(sqbk.rd)
                        mm_last = P.emit("tensor", lambda t, bi=bi, dci=dci, dc=dc, st=firstsq, sp=(dc == 15): t.matmul(
                            sqbk.ap[0:1, :], sxn_b[:, dc:dc+1], wkb[bi][:, 512*dci:512*(dci+1)],
                            start=st, stop=sp), waits=w_, inc=tC)
                        firstsq = False
                        sqbk.wr = mm_last
                        wkb_tok[bi] = mm_last
            for qi in range(4):
                e128 = 2*p2 + (qi // 2)
                ct = qi % 2
                bkk = qbs[qi]
                sq_rdy[e128] = elu_evac(bkk, sjq[:, 1024*e128 + 512*ct:1024*e128 + 512*(ct+1)], qi % 2)
            if p2 == 0:
                t_cp = P.emit("vector", lambda v: v.tensor_copy(rows[0:1, 0:512], sqbk.ap[0:1, :]), waits=[sqbk.wr], inc=vC)
                sqbk.rd = t_cp
                tt_last = None
                for hh in range(4):
                    tt_last = P.emit("tensor", lambda t, hh=hh: t.transpose(
                        sqbk.ap[:, hh:hh+1], rows[0:1, 128*hh:128*(hh+1)], id_f[0:1, 0:1]),
                        waits=[t_cp, t_cf], inc=tC)
                sqbk.wr = tt_last
                t_sqs = P.emit("vector", lambda v: v.tensor_scalar(
                    sq_sb, sqbk.ap[:, 0:4], SCALE / T, None, ALU.mult), waits=[tt_last], inc=vC)
                sqbk.rd = t_sqs
        gamma_pump(4)
        if g == 0 and dbg_sq is not None:
            for hh in range(4):
                P.dma("gpsimd", dbg_sq[128*hh:128*(hh+1), :], sjq[:, 1024*hh:1024*(hh+1)], waits=[sq_rdy[hh]], inc=dbq)

        if g == 0:
            nst = []
            for m in range(M):
                nst.append(P.emit("vector", lambda v, m=m: v.tensor_reduce(
                    nf4[:, m:m+1], normstb[:, m:64:4], AXX, ALU.add), waits=[t_nrm], inc=vC))
            t_nsb = P.emit("vector", lambda v: v.tensor_copy(nsb, nf4), fwaits=[nst[-1]], inc=vC)
            t_nmm = P.emit("tensor", lambda t: t.matmul(B[1].ap[0:1, 0:4], ones_b, nsb, start=True, stop=True),
                           waits=[t_nsb, B[1].rd], inc=tC)
            B[1].wr = t_nmm
            t_er = P.emit("vector", lambda v: v.tensor_scalar(e_row, B[1].ap[0:1, 0:4], 1e-6, None, ALU.is_lt), waits=[t_nmm], inc=vC)
            B[1].rd = t_er
            t_ier = P.emit("vector", lambda v: v.tensor_scalar(ie_row, e_row, -1.0, 1.0, ALU.mult, ALU.add), waits=[t_er], inc=vC)
        t_P = P.emit("vector", lambda v, g=g: v.tensor_tensor(
            relP[:].rearrange("p (a b) -> p a b", a=4),
            normstb[:, 16*g:16*(g+1)].rearrange("p (a b) -> p a b", a=4),
            sq_sb[:].unsqueeze(2).broadcast_to([128, 4, 4]), ALU.mult), waits=[t_nrm], fwaits=[t_sqs], inc=vC)
        t_rmm = P.emit("tensor", lambda t: t.matmul(B[1].ap[0:1, 4:20], ones_b, relP, start=True, stop=True),
                       waits=[t_P, B[1].rd], inc=tC)
        B[1].wr = t_rmm
        RL = rows2[0:1, 32:48]
        RLv = RL.rearrange("p (a b) -> p a b", a=4)
        MX = rows2[0:1, 48:52]
        M2 = rows2[0:1, 52:56]
        t_rl = P.emit("vector", lambda v: v.tensor_tensor(
            RLv, B[1].ap[0:1, 4:20].rearrange("p (a b) -> p a b", a=4),
            ie_row.unsqueeze(1).broadcast_to([1, 4, 4]), ALU.mult), waits=[t_rmm, t_ier], inc=vC)
        B[1].rd = t_rl
        t_rl2 = P.emit("vector", lambda v: v.scalar_tensor_tensor(
            RLv, e_row.unsqueeze(1).broadcast_to([1, 4, 4]), -1e30, RLv, ALU.mult, ALU.add), fwaits=[t_rl], inc=vC)
        t_m1 = P.emit("vector", lambda v: v.tensor_tensor(MX, RL[:, 0:16:4], RL[:, 1:16:4], ALU.max), fwaits=[t_rl2], inc=vC)
        t_m2 = P.emit("vector", lambda v: v.tensor_tensor(M2, RL[:, 2:16:4], RL[:, 3:16:4], ALU.max), inc=vC)
        t_m3 = P.emit("vector", lambda v: v.tensor_tensor(MX, MX, M2, ALU.max), fwaits=[t_m2], inc=vC)
        t_sb = P.emit("vector", lambda v: v.tensor_tensor(
            RLv, RLv, MX.unsqueeze(2).broadcast_to([1, 4, 4]), ALU.subtract), fwaits=[t_m3], inc=vC)
        t_ex = P.emit("scalar", lambda s: s.activation(RL, RL, AF.Exp), waits=[t_sb], inc=aC)
        t_s1 = P.emit("vector", lambda v: v.tensor_tensor(MX, RL[:, 0:16:4], RL[:, 1:16:4], ALU.add), waits=[t_ex], inc=vC)
        t_s2 = P.emit("vector", lambda v: v.tensor_tensor(M2, RL[:, 2:16:4], RL[:, 3:16:4], ALU.add), inc=vC)
        t_s3 = P.emit("vector", lambda v: v.tensor_tensor(MX, MX, M2, ALU.add), fwaits=[t_s2], inc=vC)
        t_rc = P.emit("vector", lambda v: v.reciprocal(MX, MX), fwaits=[t_s3], inc=vC)
        t_wn = P.emit("vector", lambda v: v.tensor_tensor(
            RLv, RLv, MX.unsqueeze(2).broadcast_to([1, 4, 4]), ALU.mult), fwaits=[t_rc], inc=vC)
        t_wm = P.emit("vector", lambda v: v.tensor_tensor(
            rowsb[0:1, 0:16].rearrange("p (a b) -> p a b", a=4), RLv,
            ie_row.unsqueeze(1).broadcast_to([1, 4, 4]), ALU.mult), fwaits=[t_wn], inc=vC)
        t_wbc = P.emit("tensor", lambda t: t.matmul(B[1].ap[:, 32:48], onesrow_b, rowsb[0:1, 0:16], start=True, stop=True),
                       waits=[t_wm, B[1].rd], inc=tC)
        B[1].wr = t_wbc
        t_w128 = P.emit("vector", lambda v: v.tensor_copy(w128, B[1].ap[:, 32:48]), waits=[t_wbc], inc=vC)
        B[1].rd = t_w128
        if g == 0 and dbg_w is not None:
            P.dma("gpsimd", dbg_w, w128, waits=[t_w128], inc=dbq)

        memtok = [(tC, tC.n), (tC, tC.n)]
        t_sstb_all = None
        for pair in range(2):
            hpair = [2*pair, 2*pair+1]
            sst_tok = {}
            for hh in hpair:
                h = 4*g + hh
                par = hh % 2
                t_mem = P.dma("gpsimd", membuf[par], memd[h*128*512:(h+1)*128*512].rearrange("(p c) -> p c", p=128),
                              waits=[memtok[par]], inc=qMb[par])
                bS = B[6 + par]
                bZp = B[4 + par]
                mmz = None
                for j in range(8):
                    kcol = 512*j + 128*hh
                    w_ = [kp_rdy[j], vt_rdy[j]]
                    if j == 0:
                        w_.append(bS.rd)
                    P.emit("tensor", lambda t, bS=bS, kcol=kcol, st=(j == 0), sp=(j == 7): t.matmul(
                        bS.ap[:, 0:128], kp[:, kcol:kcol+128], vt[:, kcol:kcol+128],
                        start=st, stop=sp), waits=w_, inc=tC)
                    w2_ = [bZp.rd] if j == 0 else []
                    mmz = P.emit("tensor", lambda t, bZp=bZp, kcol=kcol, st=(j == 0), sp=(j == 7): t.matmul(
                        bZp.ap[:, 0:1], kp[:, kcol:kcol+128], ones_b,
                        start=st, stop=sp), waits=w2_, inc=tC)
                bS.wr = mmz
                bZp.wr = mmz
                t_sst = P.emit("scalar", lambda s, bS=bS, par=par: s.activation(
                    Sst[:, 132*par:132*par+128], bS.ap[:, 0:128], AF.Copy, scale=mask_f),
                    waits=[mmz, t_cf, prev_end], inc=aC)
                t_sstz = P.emit("scalar", lambda s, bZp=bZp, par=par: s.activation(
                    Sst[:, 132*par+128:132*par+129], bZp.ap[:, 0:1], AF.Copy, scale=mask_f),
                    inc=aC)
                bS.rd = t_sstz
                bZp.rd = t_sstz
                sst_tok[hh] = t_sstz
                t_sstb_all = P.emit("vector", lambda v, par=par: v.tensor_copy(
                    Sst_b[:, 132*par:132*par+129], Sst[:, 132*par:132*par+129]), waits=[t_sstz], inc=vC)
                if g == 0 and dbg_st is not None:
                    P.dma("gpsimd", dbg_st[128*hh:128*(hh+1), :], Sst[:, 132*par:132*par+129], waits=[t_sst], inc=dbq)

                bA = B[par]
                bN = B[2]
                nmm = None
                for sc in range(NSC):
                    w_ = [sq_rdy[hh], t_nrm]
                    if sc == 0:
                        w_.append(bN.rd)
                    nmm = P.emit("tensor", lambda t, sc=sc, hh=hh, g=g: t.matmul(
                        bN.ap[:, 32*hh + 4*sc:32*hh + 4*sc + 4],
                        sjq[:, 1024*hh + 128*sc:1024*hh + 128*sc + 128],
                        normstb[:, 4*(4*g+hh):4*(4*g+hh)+4], start=True, stop=True),
                        waits=w_, inc=tC)
                bN.wr = nmm
                t_mx = P.emit("vector", lambda v, hh=hh, bN=bN: v.tensor_scalar(
                    r_sc[:, 32*hh:32*(hh+1)], bN.ap[:, 32*hh:32*(hh+1)], EPS_ATT, None, ALU.max),
                    waits=[nmm], inc=vC)
                bN.rd = t_mx
                t_rp = P.emit("vector", lambda v, hh=hh: v.reciprocal(
                    r_sc[:, 32*hh:32*(hh+1)], r_sc[:, 32*hh:32*(hh+1)]), fwaits=[t_mx], inc=vC)
                t_rw = t_rp
                for m in range(M):
                    t_rw = P.emit("vector", lambda v, hh=hh, m=m: v.tensor_scalar(
                        r_sc[:, 32*hh+m:32*(hh+1):4], r_sc[:, 32*hh+m:32*(hh+1):4],
                        w128[:, 4*hh+m:4*hh+m+1], None, ALU.mult),
                        waits=[t_w128], fwaits=[t_rw] if m == 0 else [], inc=vC)
                t_drain = t_rw
                amm = None
                for sc in range(NSC):
                    for m in range(M):
                        w_ = [t_mem, sq_rdy[hh]]
                        if m == 0:
                            w_.append(bA.rd)
                        amm = P.emit("tensor", lambda t, sc=sc, m=m, hh=hh, bA=bA, par=par: t.matmul(
                            bA.ap[:, 128*m:128*(m+1)],
                            sjq[:, 1024*hh + 128*sc:1024*hh + 128*sc + 128],
                            membuf[par][:, 128*m:128*(m+1)], start=True, stop=True),
                            waits=w_, inc=tC)
                    bA.wr = amm
                    t_c = P.emit("vector", lambda v, sc=sc, hh=hh, par=par, bA=bA: v.tensor_scalar(
                        memo[:, 1024*par + 128*sc:1024*par + 128*(sc+1)], bA.ap[:, 0:128],
                        r_sc[:, 32*hh + 4*sc:32*hh + 4*sc + 1], None, ALU.mult),
                        waits=[amm], fwaits=[t_drain] if sc == 0 else [], inc=vC)
                    for m in range(1, M):
                        t_c = P.emit("vector", lambda v, sc=sc, hh=hh, par=par, m=m, bA=bA: v.scalar_tensor_tensor(
                            memo[:, 1024*par + 128*sc:1024*par + 128*(sc+1)], bA.ap[:, 128*m:128*(m+1)],
                            r_sc[:, 32*hh + 4*sc + m:32*hh + 4*sc + m + 1],
                            memo[:, 1024*par + 128*sc:1024*par + 128*(sc+1)], ALU.mult, ALU.add),
                            fwaits=[t_c], inc=vC)
                    bA.rd = t_c
                memtok[par] = amm

            state_tok = {hh: None for hh in hpair}
            zu_tok = {hh: None for hh in hpair}
            for j in range(NSC):
                for hh in hpair:
                    par = hh % 2
                    h = 4*g + hh
                    tcn = 8 + j
                    kcol = 512*tcn + 128*hh
                    qcol = 1024*hh + 128*j
                    bT = B[par]; bSc = B[2 + par]; bNm = B[4 + par]; bOw = B[6 + par]
                    t_t1 = P.emit("tensor", lambda t, bT=bT, kcol=kcol: t.transpose(
                        bT.ap.bitcast(BF16)[:, 0:128], kp[:, kcol:kcol+128], id_b),
                        waits=[bT.rd, kp_rdy[tcn]], inc=tC)
                    bT.wr = t_t1
                    t_kpf = P.emit("vector", lambda v, bT=bT, par=par: v.tensor_copy(
                        kpf[:, 128*par:128*(par+1)], bT.ap.bitcast(BF16)[:, 0:128]), waits=[t_t1], inc=vC)
                    bT.rd = t_kpf
                    t_t2 = P.emit("tensor", lambda t, bSc=bSc, par=par, qcol=qcol: t.matmul(
                        bSc.ap[:, 0:128], kpf[:, 128*par:128*(par+1)], sjq[:, qcol:qcol+128],
                        start=True, stop=True), waits=[t_kpf, bSc.rd, sq_rdy[hh]], inc=tC)
                    bSc.wr = t_t2
                    t_scm = P.emit("vector", lambda v, bSc=bSc, par=par: v.tensor_tensor(
                        scm[:, 128*par:128*(par+1)], bSc.ap[:, 0:128], tril_f, ALU.mult),
                        waits=[t_t2], inc=vC)
                    bSc.rd = t_scm
                    if state_tok[hh] is None:
                        Sop = Sst_b[:, 132*par:132*par+129]
                        s_tok = t_sstb_all
                    else:
                        Sop = Scur_b[:, 132*par:132*par+129]
                        s_tok = state_tok[hh]
                    t_t3 = P.emit("tensor", lambda t, bNm=bNm, qcol=qcol, Sop=Sop: t.matmul(
                        bNm.ap[:, 0:128], sjq[:, qcol:qcol+128], Sop[:, 0:128],
                        start=True, stop=False), waits=[s_tok, bNm.rd], inc=tC)
                    t_t4 = P.emit("tensor", lambda t, bNm=bNm, par=par, kcol=kcol: t.matmul(
                        bNm.ap[:, 0:128], scm[:, 128*par:128*(par+1)], vt[:, kcol:kcol+128],
                        start=False, stop=True), waits=[t_scm, vt_rdy[tcn]], inc=tC)
                    t_t5 = P.emit("tensor", lambda t, bNm=bNm, qcol=qcol, Sop=Sop: t.matmul(
                        bNm.ap[:, 128:129], sjq[:, qcol:qcol+128], Sop[:, 128:129],
                        start=True, stop=False), inc=tC)
                    t_t6 = P.emit("tensor", lambda t, bNm=bNm, par=par: t.matmul(
                        bNm.ap[:, 128:129], scm[:, 128*par:128*(par+1)], ones_b,
                        start=False, stop=True), inc=tC)
                    bNm.wr = t_t6
                    w_ = [kp_rdy[tcn], vt_rdy[tcn]]
                    if j == 0:
                        w_.append(bOw.rd)
                    t_t7 = P.emit("tensor", lambda t, bOw=bOw, kcol=kcol: t.matmul(
                        bOw.ap[:, 0:128], kp[:, kcol:kcol+128], vt[:, kcol:kcol+128],
                        start=True, stop=True), waits=w_, inc=tC)
                    bOw.wr = t_t7
                    if j == 0:
                        t_zu = P.emit("vector", lambda v, par=par: v.tensor_reduce(
                            zacc[:, 2*par:2*par+1], kpf[:, 128*par:128*(par+1)], AXX, ALU.add),
                            fwaits=[t_kpf], inc=vC)
                    else:
                        t_zr = P.emit("vector", lambda v, par=par: v.tensor_reduce(
                            denb[:, 2*par+1:2*par+2], kpf[:, 128*par:128*(par+1)], AXX, ALU.add),
                            fwaits=[t_kpf, zu_tok[hh]], inc=vC)
                        t_zu = P.emit("vector", lambda v, par=par: v.tensor_tensor(
                            zacc[:, 2*par:2*par+1], zacc[:, 2*par:2*par+1], denb[:, 2*par+1:2*par+2], ALU.add),
                            fwaits=[t_zr], inc=vC)
                    zu_tok[hh] = t_zu
                    if j < NSC - 1:
                        if j == 0:
                            t_su = P.emit("vector", lambda v, par=par, bOw=bOw: v.tensor_tensor(
                                scur32[:, 128*par:128*(par+1)], Sst[:, 132*par:132*par+128], bOw.ap[:, 0:128], ALU.add),
                                waits=[t_t7, sst_tok[hh]], inc=vC)
                        else:
                            t_su = P.emit("vector", lambda v, par=par, bOw=bOw: v.tensor_tensor(
                                scur32[:, 128*par:128*(par+1)], scur32[:, 128*par:128*(par+1)], bOw.ap[:, 0:128], ALU.add),
                                waits=[t_t7], fwaits=[state_tok[hh]], inc=vC)
                        t_sub = P.emit("vector", lambda v, par=par: v.tensor_copy(
                            Scur_b[:, 132*par:132*par+128], scur32[:, 128*par:128*(par+1)]), fwaits=[t_su], inc=vC)
                        t_suz = P.emit("vector", lambda v, par=par: v.tensor_tensor(
                            Scur_b[:, 132*par+128:132*par+129], Sst[:, 132*par+128:132*par+129], zacc[:, 2*par:2*par+1], ALU.add),
                            fwaits=[t_zu, t_sub], inc=vC)
                        state_tok[hh] = t_suz
                        bOw.rd = t_su
                    t_d1 = P.emit("vector", lambda v, bNm=bNm, par=par: v.tensor_scalar(
                        denb[:, 2*par:2*par+1], bNm.ap[:, 128:129], EPS_ATT, None, ALU.max),
                        waits=[t_t6], inc=vC)
                    t_d2 = P.emit("vector", lambda v, par=par: v.reciprocal(
                        denb[:, 2*par:2*par+1], denb[:, 2*par:2*par+1]), fwaits=[t_d1], inc=vC)
                    t_d3 = P.emit("gpsimd", lambda g, par=par, h=h: g.tensor_scalar(
                        denb[:, 2*par:2*par+1], denb[:, 2*par:2*par+1], gi128[:, h:h+1], None, ALU.mult),
                        waits=[t_gi, t_d2], inc=gC)
                    ccol = 2048*j + 128*h
                    t_c1 = P.emit("vector", lambda v, bNm=bNm, par=par, ccol=ccol: v.tensor_scalar(
                        comb[:, ccol:ccol+128], bNm.ap[:, 0:128], denb[:, 2*par:2*par+1], None, ALU.mult),
                        waits=[t_d3], inc=vC)
                    t_c2 = P.emit("vector", lambda v, par=par, j=j, h=h, ccol=ccol: v.scalar_tensor_tensor(
                        comb[:, ccol:ccol+128], memo[:, 1024*par + 128*j:1024*par + 128*(j+1)],
                        g128[:, h:h+1], comb[:, ccol:ccol+128], ALU.mult, ALU.add),
                        waits=[t_g], fwaits=[t_c1], inc=vC)
                    bNm.rd = t_c2
                gamma_pump(1)
        group_end = P.emit("vector", lambda v: v.tensor_copy(rtmp[:, 3:4], cf[:, 0:1]), waits=[t_cf], fwaits=[group_end], inc=vC)

    dbg_can = want_dbg("can", [128, 16])
    P.barrier("gpsimd", [group_end])
    if dbg_can is not None:
        P.dma("gpsimd", dbg_can[:, 0:4], rtmp, waits=[group_end], inc=dbq)
    if dbg_comb is not None:
        for sc in range(NSC):
            P.dma("gpsimd", dbg_comb[128*sc:128*(sc+1), :], comb[:, 2048*sc:2048*(sc+1)], waits=[group_end], inc=dbq)
    if stop_after < 4:
        return _fin(nc, P, es, outd, dbg_d, locals())

    SD = Arena(SAap, 24 * 1024)
    wob = [SD.alloc(2048, BF16, "wob0"), SD.alloc(2048, BF16, "wob1")]
    for sc in range(NSC):
        for ec in range(16):
            par = (sc * 16 + ec) % 2
            bT = B[par]
            t_tp = P.emit("tensor", lambda t, bT=bT, sc=sc, ec=ec: t.transpose(
                bT.ap.bitcast(BF16)[:, 0:128], comb[:, 2048*sc + 128*ec:2048*sc + 128*(ec+1)], id_b),
                waits=[bT.rd, group_end], inc=tC)
            bT.wr = t_tp
            t_ev = P.emit("vector", lambda v, bT=bT, sc=sc, ec=ec: v.tensor_copy(
                comb_fm[:, 1024*ec + 128*sc:1024*ec + 128*(sc+1)], bT.ap.bitcast(BF16)[:, 0:128]),
                waits=[t_tp], inc=vC)
            bT.rd = t_ev
    cfm_all = P.emit("vector", lambda v: v.tensor_copy(rtmp[:, 3:4], cf[:, 0:1]), fwaits=[group_end], inc=vC)

    wo_tok = [(tC, tC.n), (tC, tC.n)]
    attn_rdy = {}
    for fc in range(16):
        bi = fc % 2
        t_w = P.dma("gpsimd", wob[bi], wod[fc*128*2048:(fc+1)*128*2048].rearrange("(p c) -> p c", p=128),
                    waits=[wo_tok[bi]], inc=qWo[bi])
        banks = [B[2 + 2*(fc % 2)], B[3 + 2*(fc % 2)]]
        mm_last = None
        for ct in range(2):
            first = True
            for ec in range(16):
                w_ = [t_w, cfm_all]
                if first:
                    w_.append(banks[ct].rd)
                mm_last = P.emit("tensor", lambda t, bkk=banks[ct], bi=bi, ec=ec, ct=ct, st=first, sp=(ec == 15): t.matmul(
                    bkk.ap, wob[bi][:, 128*ec:128*(ec+1)],
                    comb_fm[:, 1024*ec + 512*ct:1024*ec + 512*(ct+1)],
                    start=st, stop=sp), waits=w_, inc=tC)
                first = False
            banks[ct].wr = mm_last
        wo_tok[bi] = mm_last
        for ct in range(2):
            t_ev = P.emit("vector", lambda v, bkk=banks[ct], fc=fc, ct=ct: v.tensor_copy(
                attn_fm[:, 1024*fc + 512*ct:1024*fc + 512*(ct+1)], bkk.ap), waits=[banks[ct].wr], inc=vC)
            banks[ct].rd = t_ev
            attn_rdy[fc] = t_ev
    P.barrier("gpsimd", [attn_rdy[15]])
    if dbg_can is not None:
        P.dma("gpsimd", dbg_can[:, 4:8], rtmp, waits=[attn_rdy[15]], inc=dbq)
    dbg_attn = want_dbg("attn", [16*128, 1024])
    if dbg_attn is not None:
        for fc in range(16):
            P.dma("gpsimd", dbg_attn[128*fc:128*(fc+1), :], attn_fm[:, 1024*fc:1024*(fc+1)], waits=[attn_rdy[fc]], inc=dbq)
    if stop_after < 5:
        return _fin(nc, P, es, outd, dbg_d, locals())

    gamma_pump(10**6)
    t_tg = P.emit("tensor", lambda t: t.transpose(B[6].ap[0:4, 0:128], gsum, id_f),
                  waits=[gst["tok"], B[6].rd, t_cf], inc=tC)
    B[6].wr = t_tg
    t_gr = P.emit("vector", lambda v: v.tensor_reduce(rows2[0:4, 24:25], B[6].ap[0:4, 0:128], AXX, ALU.add),
                  waits=[t_tg], inc=vC)
    B[6].rd = t_gr
    t_th = P.emit("vector", lambda v: v.tensor_scalar(rows2[0:4, 25:26], rows2[0:4, 24:25], 0.5/NFFN, 0.5e-8, ALU.mult, ALU.add),
                  fwaits=[t_gr], inc=vC)
    t_tb = P.emit("tensor", lambda t: t.transpose(B[6].ap[0:1, 4:8], rows2[0:4, 25:26], id_f[0:4, 0:4]),
                  waits=[t_th, B[6].rd], inc=tC)
    B[6].wr = t_tb
    t_cpr = P.emit("vector", lambda v: v.tensor_copy(rows2[0:1, 26:30], B[6].ap[0:1, 4:8]), waits=[t_tb], inc=vC)
    B[6].rd = t_cpr
    t_bcm = P.emit("tensor", lambda t: t.matmul(B[6].ap[:, 8:12], onesrow_f, rows2[0:1, 26:30], start=True, stop=True),
                   waits=[t_cpr, B[6].rd], inc=tC)
    B[6].wr = t_bcm
    t_th128 = P.emit("vector", lambda v: v.tensor_copy(th128, B[6].ap[:, 8:12]), waits=[t_bcm], inc=vC)
    B[6].rd = t_th128
    t_thn = P.emit("vector", lambda v: v.tensor_scalar(thn128, th128, -1.0, None, ALU.mult), fwaits=[t_th128], inc=vC)
    dbg_th = want_dbg("th", [128, 4])
    if dbg_th is not None:
        P.dma("gpsimd", dbg_th, th128, waits=[t_th128], inc=dbq)

    SE = Arena(SAap, 24 * 1024)
    wf = [[SE.alloc(512, F32, f"wf{m}{i}") for i in range(2)] for m in range(2)]
    qbb = [[SE.alloc(2048, BF16, f"qb{m}{i}") for i in range(2)] for m in range(2)]
    ab_q = [[memo[:, 1024*m + 512*i:1024*m + 512*(i+1)] for i in range(2)] for m in range(2)]
    ab_q2 = [[elut.bitcast(BF16)[:, 1024*m + 512*i:1024*m + 512*(i+1)] for i in range(2)] for m in range(2)]
    assert SE.off <= 24 * 1024

    wo_done = wo_tok[0] if wo_tok[0][1] > wo_tok[1][1] else wo_tok[1]
    qhz = {"lastQ": [None] * 8}
    def quantize(matj, wfb, qdst, tok_ld, abuf, abuf2, guard=None, lane=0):
        t_a = P.emit("scalar", lambda s: s.activation(abuf, wfb, AF.Sign, bias=thn128[:, matj:matj+1]),
                     waits=[tok_ld, t_thn, qhz["lastQ"][lane]], inc=aC)
        t_b = P.emit("scalar", lambda s: s.activation(abuf2, wfb, AF.Sign, bias=th128[:, matj:matj+1]),
                     waits=[], inc=aC)
        t_q = P.emit("vector", lambda v: v.tensor_tensor(qdst, abuf, abuf2, ALU.add),
                     waits=[t_b, guard], inc=vC)
        qhz["lastQ"][lane] = t_q
        return t_q
    hid_rdy = {}
    wf_tok = [[(tC, tC.n), (tC, tC.n)], [(tC, tC.n), (tC, tC.n)]]
    q_tok = [[None, None], [None, None]]
    dbg_q1 = want_dbg("q1", [256, 2048])
    qb_guard = [[(tC, tC.n), (tC, tC.n)], [(tC, tC.n), (tC, tC.n)]]
    sg_guard = [None, None]
    for hc in range(NHC):
        x1b = [B[0], B[2]]
        x2bk = [B[1], B[3]]
        qtoks = [[], []]
        qb = [qbb[0][hc % 2], qbb[1][hc % 2]]
        for matj, (wd, qbm) in enumerate(((w1d, qb[0]), (w2d, qb[1]))):
            for qt in range(4):
                bi = qt % 2
                t_ld = P.dma("sync", wf[matj][bi],
                             wd[hc*16*128*128:(hc+1)*16*128*128].rearrange("(p c) -> p c", p=128)[:, 512*qt:512*(qt+1)],
                             waits=[wf_tok[matj][bi]], inc=qWf[matj][bi])
                abp = ab_q[matj] if qt % 2 == 0 else ab_q2[matj]
                t_q = quantize(matj, wf[matj][bi], qbm[:, 512*qt:512*(qt+1)], t_ld, abp[0], abp[1], guard=qb_guard[matj][hc % 2], lane=matj * 2 + qt % 2)
                wf_tok[matj][bi] = t_q
                qtoks[matj].append(t_q)
        mm_last = None
        for ct in range(2):
            for matj in range(2):
                bkk = x1b[ct] if matj == 0 else x2bk[ct]
                first = True
                for dc in range(NDC):
                    w_ = [qtoks[matj][dc // 4], xn_tok[dc][1]]
                    if first:
                        w_.append(bkk.rd)
                    mm_last = P.emit("tensor", lambda t, bkk=bkk, qbm_=qb[matj], dc=dc, ct=ct, st=first, sp=(dc == 15): t.matmul(
                        bkk.ap, qbm_[:, 128*dc:128*(dc+1)],
                        xn_own[:, 1024*dc + 512*ct:1024*dc + 512*(ct+1)],
                        start=st, stop=sp), waits=w_, inc=tC)
                    first = False
                    bkk.wr = mm_last
        for _p in range(2):
            qb_guard[0][_p] = mm_last
            qb_guard[1][_p] = mm_last
        for ct in range(2):
            t_sg = P.emit("scalar", lambda s, ct=ct: s.activation(
                sgb[:, 512*ct:512*(ct+1)], x1b[ct].ap, AF.Sigmoid, scale=half_f), waits=[x1b[ct].wr, t_cf, sg_guard[ct]], inc=aC)
            x1b[ct].rd = t_sg
            t_h = P.emit("vector", lambda v, ct=ct, hc=hc: v.scalar_tensor_tensor(
                hidden[:, 1024*hc + 512*ct:1024*hc + 512*(ct+1)], sgb[:, 512*ct:512*(ct+1)],
                0.5, x2bk[ct].ap, ALU.mult, ALU.mult), waits=[t_sg, x2bk[ct].wr, wo_done], inc=vC)
            x2bk[ct].rd = t_h
            sg_guard[ct] = t_h
            hid_rdy[(hc, ct)] = t_h
    P.barrier("gpsimd", [hid_rdy[(NHC-1, 1)]])
    if dbg_can is not None:
        P.dma("gpsimd", dbg_can[:, 8:12], rtmp, waits=[hid_rdy[(NHC-1, 1)]], inc=dbq)
    dbg_hid = want_dbg("hid", [44*128, 1024])
    if dbg_hid is not None:
        for hc in range(NHC):
            P.dma("gpsimd", dbg_hid[128*hc:128*(hc+1), :], hidden[:, 1024*hc:1024*(hc+1)],
                  waits=[hid_rdy[(hc, 0)], hid_rdy[(hc, 1)]], inc=dbq)
    if stop_after < 6:
        return _fin(nc, P, es, outd, dbg_d, locals())

    SF = Arena(ar_t[:, 32 * 1024:64 * 1024], 32 * 1024)
    q3b = [SF.alloc(5632, BF16, "q3a"), SF.alloc(5632, BF16, "q3b")]
    assert SF.off <= 32 * 1024
    SFO = Arena(SAap, 24 * 1024)
    ost = [SFO.alloc(1024, F32, "ost0"), SFO.alloc(1024, F32, "ost1")]
    xot = [SFO.alloc(1024, BF16, "xot0"), SFO.alloc(1024, BF16, "xot1")]
    w3f = [SFO.alloc(704, F32, "w3f0"), SFO.alloc(704, F32, "w3f1")]
    ab3p = [[SFO.alloc(704, BF16, f"ab3{i}{j}") for j in range(2)] for i in range(2)]
    assert SFO.off <= 24 * 1024
    gE = (gC, gC.n)
    vE = (vC, vC.n)

    w3_tok = [(tC, tC.n), (tC, tC.n)]
    xo_tok = [None, None]
    ost_tok = [None, None]
    last_out = None
    phaseE_done = mm_last
    q3_guard = [(tC, tC.n), (tC, tC.n)]
    for fc in range(16):
        q3 = q3b[fc % 2]
        qq_toks = []
        for qt in range(8):
            bi = qt % 2
            t_ld = P.dma("sync", w3f[bi],
                         w3d[fc*44*128*128:(fc+1)*44*128*128].rearrange("(p c) -> p c", p=128)[:, 704*qt:704*(qt+1)],
                         waits=[w3_tok[bi], phaseE_done], inc=qW3[bi])
            t_q = quantize(2, w3f[bi], q3[:, 704*qt:704*(qt+1)], t_ld, ab3p[bi][0], ab3p[bi][1], guard=q3_guard[fc % 2], lane=4 + bi)
            w3_tok[bi] = t_q
            qq_toks.append(t_q)
        banks = [B[4 + 2*(fc % 2)], B[5 + 2*(fc % 2)]]
        mm3 = None
        for ct in range(2):
            first = True
            for hcc in range(NHC):
                w_ = [qq_toks[(128 * hcc + 127) // 704], hid_rdy[(hcc, ct)]]
                if first:
                    w_.append(banks[ct].rd)
                mm3 = P.emit("tensor", lambda t, bkk=banks[ct], q3_=q3, hcc=hcc, ct=ct, st=first, sp=(hcc == NHC-1): t.matmul(
                    bkk.ap, q3_[:, 128*hcc:128*(hcc+1)],
                    hidden[:, 1024*hcc + 512*ct:1024*hcc + 512*(ct+1)],
                    start=st, stop=sp), waits=w_, inc=tC)
                first = False
            banks[ct].wr = mm3
        q3_guard[fc % 2] = mm3
        bi = fc % 2
        t_xo = P.dma("gpsimd", xot[bi], xview(fc)[:, 1024:2048], waits=[xo_tok[bi], phaseE_done, gE, vE], inc=qXo[bi])
        for ct in range(2):
            t_a1 = P.emit("vector", lambda v, bkk=banks[ct], fc=fc, ct=ct, bi=bi: v.scalar_tensor_tensor(
                ost[bi][:, 512*ct:512*(ct+1)], bkk.ap, 0.5, attn_fm[:, 1024*fc + 512*ct:1024*fc + 512*(ct+1)],
                ALU.mult, ALU.add),
                waits=[banks[ct].wr, attn_rdy[fc], ost_tok[bi], gE], inc=vC)
            banks[ct].rd = t_a1
            t_a2 = P.emit("gpsimd", lambda g, fc=fc, ct=ct, bi=bi: g.tensor_tensor(
                ost[bi][:, 512*ct:512*(ct+1)], ost[bi][:, 512*ct:512*(ct+1)],
                xot[bi][:, 512*ct:512*(ct+1)], ALU.add), waits=[t_a1, t_xo], inc=gC)
        xo_tok[bi] = t_a2
        last_out = P.dma("sync", outd[fc*128*TO:(fc+1)*128*TO].rearrange("(p c) -> p c", p=128),
                         ost[bi], waits=[t_a2], inc=qO[bi])
        ost_tok[bi] = last_out

    if dbg_can is not None:
        P.dma("gpsimd", dbg_can[:, 12:16], rtmp, waits=[last_out], inc=dbq)
    P.barrier("sync", [(qO[0], qO[0].n), (qO[1], qO[1].n)])
    return _fin(nc, P, es, outd, dbg_d, locals())


def _fin(nc, P, es, outd, dbg_d, lcl):
    es2 = ExitStack()
    for ch in P.chans:
        ch.sem = es2.enter_context(nc.semaphore(ch.name))
    with es2:
        with nc.Block() as block:
            P.finalize(block)
    es.close()
    return nc, dbg_d


def host_inputs(full, core):
    b, half = core // 2, core % 2
    x = np.asarray(full["L0_attn_hidden_states"])
    own = x[b, 1024*half:1024*half+1024]
    pre = x[b, 1024*(1-half):1024*(1-half)+1024]
    ctx = np.concatenate([pre, own], axis=0)
    xctx = np.ascontiguousarray(ctx.T).reshape(-1)
    def wtile(w):
        return np.ascontiguousarray(np.asarray(w).T.reshape(16, 128, 4, 512).transpose(2, 0, 1, 3)).reshape(-1)
    wo = np.asarray(full["L0_attn_w_o"]).T.reshape(16, 128, 16, 128).transpose(2, 1, 0, 3)
    w1 = np.asarray(full["L0_ffn_w1"]).T.reshape(16, 128, 44, 128).transpose(2, 1, 0, 3)
    w2 = np.asarray(full["L0_ffn_w2"]).T.reshape(16, 128, 44, 128).transpose(2, 1, 0, 3)
    w3 = np.asarray(full["L0_ffn_w3"]).T.reshape(44, 128, 16, 128).transpose(2, 1, 0, 3)
    mems = np.asarray(full["L0_attn_memories"]).transpose(1, 2, 0, 3)
    normst = np.ascontiguousarray(np.asarray(full["L0_attn_memory_norms"]).transpose(2, 1, 0).reshape(128, 64))
    nt = np.asarray(full["L0_attn_memory_norms"]).transpose(2, 1, 0)
    normst = np.ascontiguousarray(nt.reshape(128, 64))
    gater = np.tile(np.asarray(full["L0_attn_gate"]).reshape(1, 16), (128, 1)).astype(np.float32)
    cstf = np.zeros((128, 390), np.float32)
    cstf[:, 0:128] = np.eye(128)
    cstf[:, 128:256] = 1.0
    cstf[:, 257] = float(half)
    cstf[0, 258] = LN_EPS
    cstf[:, 259] = 0.5
    jj, ii = np.mgrid[0:128, 0:128]
    cstf[:, 260:388] = (jj <= ii).astype(np.float32)
    cstb = np.zeros((128, 386), np.float32)
    cstb[:, 0] = 1.0
    cstb[:, 1:129] = np.eye(128)
    cstb[:, 129:257] = (jj <= ii)
    cstb[:, 257:385] = 1.0
    import ml_dtypes
    return {
        "xctx": xctx.astype(np.float32),
        "wq": wtile(full["L0_attn_w_q"]).astype(np.float32),
        "wk": wtile(full["L0_attn_w_k"]).astype(np.float32),
        "wv": wtile(full["L0_attn_w_v"]).astype(np.float32),
        "wo": np.ascontiguousarray(wo).reshape(-1).astype(np.float32),
        "w1": np.ascontiguousarray(w1).reshape(-1).astype(np.float32),
        "w2": np.ascontiguousarray(w2).reshape(-1).astype(np.float32),
        "w3": np.ascontiguousarray(w3).reshape(-1).astype(np.float32),
        "mems": np.ascontiguousarray(mems).reshape(-1).astype(np.float32),
        "normst": normst.astype(np.float32),
        "gater": gater,
        "cstf": cstf,
        "cstb": cstb.astype(ml_dtypes.bfloat16),
    }


def kernel(**inputs):
    from concourse.bass_utils import run_bass_kernel_spmd
    nc, _ = build(dbg_names=(), stop_after=99)
    in_maps = [host_inputs(inputs, c) for c in range(8)]
    res = run_bass_kernel_spmd(nc, in_maps, core_ids=list(range(8)))
    out = np.zeros((4, 2048, 2048), dtype=np.float32)
    for c in range(8):
        b, half = c // 2, c % 2
        o = np.asarray(res.results[c]["out"]).reshape(2048, 1024)
        out[b, 1024*half:1024*(half+1), :] = o.T
    return out


# version 10 (final)
# speedup vs baseline: 1.2300x; 1.1105x over previous
import numpy as np
import concourse.bass as bass
import concourse.mybir as mybir
from contextlib import ExitStack

F32 = mybir.dt.float32
BF16 = mybir.dt.bfloat16
U8 = mybir.dt.uint8
AF = mybir.ActivationFunctionType
ALU = mybir.AluOpType
AXX = mybir.AxisListType.X

ENGINES = ("sync", "tensor", "vector", "scalar", "gpsimd")

D = 2048; T = 2048; TO = 1024; NDC = 16
H = 16; HD = 128; NG = 4; M = 4; CH = 128; NSC = 8
HF = 5632; NHC = 44
SCALE = float(HD) ** -0.5
LN_EPS = 1e-5
EPS_ATT = 1e-6
NFFN = float(HF * D)


class Ch:
    __slots__ = ("sem", "n", "name", "dma")
    def __init__(self, name, dma=False):
        self.sem = None; self.n = 0; self.name = name; self.dma = dma


class Prog:
    def __init__(self, nc):
        self.nc = nc
        self.streams = {e: [] for e in ENGINES}
        self.chans = []
        self.seen = {e: {} for e in ENGINES}

    def chan(self, name, dma=False):
        ch = Ch(name, dma); self.chans.append(ch); return ch

    def emit(self, eng, fn, waits=(), inc=None, inc_amt=1, fwaits=()):
        waits = [w for w in waits if w is not None]
        fwaits = [w for w in fwaits if w is not None]
        seen = self.seen[eng]
        needed = list(fwaits)
        for (ch, v) in fwaits:
            if seen.get(ch, 0) < v:
                seen[ch] = v
        for (ch, v) in waits:
            if seen.get(ch, 0) < v:
                needed.append((ch, v)); seen[ch] = v
        tok = None
        if inc is not None:
            inc.n += inc_amt
            tok = (inc, inc.n)
        self.streams[eng].append((needed, fn, (inc, inc_amt) if inc else None))
        return tok

    def dma(self, eng, out, in_, waits=(), inc=None):
        assert inc is not None
        w = list(waits) + ([(inc, inc.n)] if inc.n else [])
        return self.emit(eng, lambda e: e.dma_start(out=out, in_=in_),
                         waits=w, inc=inc, inc_amt=16)

    def barrier(self, eng, waits):
        self.emit(eng, None, waits=waits)

    def finalize(self, block):
        def mk(eng_name):
            stream = self.streams[eng_name]
            def body(e):
                for needed, fn, inc in stream:
                    for (ch, v) in needed:
                        e.wait_ge(ch.sem, v)
                    if fn is None:
                        continue
                    ins = fn(e)
                    if inc is not None:
                        ch, amt = inc
                        ins.then_inc(ch.sem, amt)
            return body
        for eng_name in ENGINES:
            if self.streams[eng_name]:
                getattr(block, eng_name)(mk(eng_name))


class Arena:
    def __init__(self, ap, nbytes):
        self.ap = ap; self.nbytes = nbytes; self.off = 0

    def alloc(self, cols, dtype, name=""):
        isz = 2 if dtype == BF16 else 4
        self.off = (self.off + 3) & ~3
        nb = cols * isz
        assert self.off + nb <= self.nbytes, f"arena overflow {name}: {self.off+nb}>{self.nbytes}"
        v = self.ap[:, self.off:self.off + nb].bitcast(dtype)
        self.off += nb
        return v


class Bank:
    __slots__ = ("ap", "rd", "wr")
    def __init__(self, ap):
        self.ap = ap; self.rd = None; self.wr = None


def build(dbg_names=(), stop_after=99):
    nc = bass.Bass("TRN2", target_bir_lowering=False)

    def din(name, shape, dt=F32):
        return nc.dram_tensor(name, shape, dt, kind="ExternalInput").ap()

    xctx = din("xctx", [D * T])
    wqd = din("wq", [NG * NDC * 128 * 512])
    wkd = din("wk", [NG * NDC * 128 * 512])
    wvd = din("wv", [NG * NDC * 128 * 512])
    wod = din("wo", [16 * 128 * 16 * 128])
    w1d = din("w1", [NHC * 128 * NDC * 128])
    w2d = din("w2", [NHC * 128 * NDC * 128])
    w3d = din("w3", [16 * 128 * NHC * 128])
    memd = din("mems", [H * 128 * M * 128])
    normd = din("normst", [128, 64])
    gated = din("gater", [128, 16])
    cstfd = din("cstf", [128, 390])
    cstbd = din("cstb", [128, 386], BF16)

    outd = nc.dram_tensor("out", [NDC * 128 * TO], F32, kind="ExternalOutput").ap()

    dbg_d = {}
    def want_dbg(name, shape):
        if name in dbg_names:
            dbg_d[name] = nc.dram_tensor("dbg_" + name, shape, F32, kind="ExternalOutput").ap()
            return dbg_d[name]
        return None

    es = ExitStack()
    SB_BYTES = 206 * 1024
    ar_t = es.enter_context(nc.sbuf_tensor("arena", [128, SB_BYTES], U8))
    pb_t = [es.enter_context(nc.psum_tensor(f"pb{i}", [128, 512], F32)) for i in range(8)]
    B = [Bank(t[:]) for t in pb_t]

    A = Arena(ar_t[:], SB_BYTES)
    xn_pre = A.alloc(NDC * 1024, BF16, "xn_pre")
    xn_own = A.alloc(NDC * 1024, BF16, "xn_own")
    R1ap = ar_t[:, A.off:A.off + 91 * 1024]
    R1 = Arena(R1ap, 91 * 1024)
    A.off += 91 * 1024
    SAap = ar_t[:, A.off:A.off + 24 * 1024]
    A.off += 24 * 1024
    A128 = A.alloc(2048, BF16, "A128")
    B128 = A.alloc(2048, BF16, "B128")
    sxn = A.alloc(16, F32, "sxn")
    sxn_b = A.alloc(16, BF16, "sxn_b")
    cb = A.alloc(386, BF16, "cb")
    cf = A.alloc(390, F32, "cf")
    gsum = A.alloc(4, F32, "gsum")
    rtmp = A.alloc(4, F32, "rtmp")

    th128 = A.alloc(4, F32, "th128")
    thn128 = A.alloc(4, F32, "thn128")
    g128 = A.alloc(16, F32, "g128")
    gi128 = A.alloc(16, F32, "gi128")
    w128 = A.alloc(16, F32, "w128")
    sq_sb = A.alloc(4, BF16, "sq_sb")
    normstb = A.alloc(64, BF16, "normstb")
    nsb = A.alloc(4, BF16, "nsb")
    nf4 = A.alloc(4, F32, "nf4")
    relP = A.alloc(16, BF16, "relP")
    r_sc = A.alloc(4 * 32, F32, "r_sc")
    Sst = A.alloc(2 * 132, F32, "Sst")
    Sst_b = A.alloc(2 * 132, BF16, "Sst_b")
    Scur_b = A.alloc(2 * 132, BF16, "Scur_b")
    kpf = A.alloc(2 * 128, BF16, "kpf")
    scm = A.alloc(2 * 128, BF16, "scm")
    denb = A.alloc(2 * 2, F32, "denb")
    zacc = A.alloc(2 * 2, F32, "zacc")
    elut = A.alloc(4 * 512, BF16, "elut")
    sgb = A.alloc(2 * 512, BF16, "sgb")
    memo = A.alloc(2 * 1024, BF16, "memo")
    rows = A.alloc(512, F32, "rows")
    rows2 = A.alloc(58, F32, "rows2")
    rowsb = A.alloc(16, BF16, "rowsb")
    assert A.off <= SB_BYTES, A.off

    kp = R1.alloc(16 * 512, BF16, "kp")
    vt = R1.alloc(16 * 512, BF16, "vt")
    sjq = R1.alloc(4 * 1024, BF16, "sq")
    comb = R1.alloc(8 * 2048, BF16, "comb")
    gbuf = [R1.alloc(1408, F32, "gbuf0"), R1.alloc(1408, F32, "gbuf1")]
    scur32 = R1.alloc(2 * 128, F32, "scur32")
    gacc = R1.alloc(192, F32, "gacc")
    hidden = Arena(R1ap, R1.nbytes).alloc(NHC * 1024, BF16, "hidden")
    comb_fm = Arena(R1ap, 40 * 1024).alloc(16 * 1024, BF16, "comb_fm")
    attn_fm = xn_pre

    ones_b = cb[:, 0:1]
    id_b = cb[:, 1:129]
    onesrow_b = cb[0:1, 257:385]
    id_f = cf[:, 0:128]
    onesrow_f = cf[0:1, 128:256]
    mask_f = cf[:, 257:258]
    lneps_f = cf[0:1, 258:259]
    half_f = cf[:, 259:260]
    tril_f = cf[:, 260:388]

    P = Prog(nc)
    qS = P.chan("qS", dma=True); qG = P.chan("qG", dma=True)
    qO = [P.chan("qO0", dma=True), P.chan("qO1", dma=True)]
    qCb = P.chan("qCb", dma=True); qCf = P.chan("qCf", dma=True); qGt = P.chan("qGt", dma=True)
    qNrm = P.chan("qNrm", dma=True)
    qGm = [P.chan("qGm0", dma=True), P.chan("qGm1", dma=True)]
    qXb = [P.chan(f"qXb{i}", dma=True) for i in range(4)]
    qWk = [P.chan("qWk0", dma=True), P.chan("qWk1", dma=True)]
    qMb = [P.chan("qMb0", dma=True), P.chan("qMb1", dma=True)]
    qWo = [P.chan("qWo0", dma=True), P.chan("qWo1", dma=True)]
    qWf = [[P.chan(f"qWf{m}{i}", dma=True) for i in range(2)] for m in range(2)]
    qW3 = [P.chan("qW30", dma=True), P.chan("qW31", dma=True)]
    qXo = [P.chan("qXo0", dma=True), P.chan("qXo1", dma=True)]
    tC = P.chan("tC"); vC = P.chan("vC"); aC = P.chan("aC"); gC = P.chan("gC")

    t_cb = P.dma("sync", cb, cstbd, inc=qCb)
    t_cf = P.dma("sync", cf, cstfd, inc=qCf)
    t_nrm = P.dma("gpsimd", normstb, normd, inc=qNrm)
    t_gate = P.dma("sync", rows2[:, 0:16], gated, inc=qGt)
    t_g = P.emit("scalar", lambda s: s.activation(g128, rows2[:, 0:16], AF.Sigmoid), waits=[t_gate], inc=aC)
    t_gi = P.emit("vector", lambda v: v.tensor_scalar(gi128, g128, -1.0, 1.0, ALU.mult, ALU.add), waits=[t_g], inc=vC)
    t_gz = None

    GT_PER = 64
    gamma_tiles = [(j, wd, t) for j, wd in enumerate((w1d, w2d, w3d)) for t in range(GT_PER)]
    gst = {"i": 0, "last_rd": [None, None], "last_ac": [None, None], "tok": None}

    def gamma_pump(n):
        for _ in range(n):
            i = gst["i"]
            if i >= len(gamma_tiles):
                return
            gst["i"] = i + 1
            j, wd, t = gamma_tiles[i]
            bi = i % 2
            gb = gbuf[bi]
            off = t * 128 * 1408
            t_ld = P.dma("sync", gb, wd[off:off + 128 * 1408].rearrange("(p c) -> p c", p=128),
                         waits=[gst["last_rd"][bi]], inc=qGm[bi])
            t_rd = P.emit("scalar", lambda s, gb=gb, j=j, t=t: s.activation(gb, gb, AF.Abs, accum_out=gacc[:, 64*j+t:64*j+t+1]),
                          waits=[t_ld], inc=aC)
            gst["last_rd"][bi] = t_rd
            gst["tok"] = t_rd

    def xview(dc):
        return xctx[dc * 128 * T:(dc + 1) * 128 * T].rearrange("(p t) -> p t", p=128)

    xpark = Arena(R1ap, 64 * 1024).alloc(NDC * 2048, BF16, "xpark")
    SA = Arena(SAap, 24 * 1024)
    xb = [SA.alloc(2048, BF16, "xb0"), SA.alloc(2048, BF16, "xb1")]
    x2b = [SA.alloc(2048, BF16, "x2b0"), SA.alloc(2048, BF16, "x2b1")]
    arow_b = SA.alloc(2048, BF16, "arow_b")
    brow_b = SA.alloc(2048, BF16, "brow_b")
    assert SA.off <= 24 * 1024

    last_mm = [None, None]
    stat_last = None
    xpark_tok = {}
    xp_prev = [None] * 4
    for dc in range(NDC):
        bi = dc % 4
        xp = xpark[:, 2048*dc:2048*(dc+1)]
        t_x = P.dma("gpsimd", xp, xview(dc), waits=[xp_prev[bi]], inc=qXb[bi])
        xp_prev[bi] = t_x
        xpark_tok[dc] = t_x
        b2 = dc % 2
        t_sq = P.emit("vector", lambda v, b2=b2, xp=xp: v.tensor_mul(x2b[b2], xp, xp),
                      waits=[t_x, last_mm[b2]], inc=vC)
        for tt in range(4):
            st, sp = dc == 0, dc == NDC - 1
            P.emit("tensor", lambda t, xp=xp, tt=tt, st=st, sp=sp: t.matmul(
                B[tt].ap[0:1, :], ones_b, xp[:, 512*tt:512*(tt+1)], start=st, stop=sp),
                waits=[t_x, t_cb], inc=tC)
            stat_last = P.emit("tensor", lambda t, b2=b2, tt=tt, st=st, sp=sp: t.matmul(
                B[4+tt].ap[0:1, :], ones_b, x2b[b2][:, 512*tt:512*(tt+1)], start=st, stop=sp),
                waits=[t_sq], inc=tC)
        last_mm[b2] = stat_last
        gamma_pump(1)

    t_fin = None
    for tt in range(4):
        t_mn = P.emit("vector", lambda v, tt=tt: v.tensor_scalar(
            B[tt].ap[0:1, :], B[tt].ap[0:1, :], 1.0/D, None, ALU.mult),
            waits=[stat_last], inc=vC)
        t_e2 = P.emit("vector", lambda v, tt=tt: v.tensor_scalar(
            B[4+tt].ap[0:1, :], B[4+tt].ap[0:1, :], 1.0/D, None, ALU.mult), inc=vC)
        t_ms = P.emit("scalar", lambda s, tt=tt: s.activation(
            rows[0:1, 0:512], B[tt].ap[0:1, :], AF.Square), waits=[t_mn], inc=aC)
        t_vr = P.emit("vector", lambda v, tt=tt: v.tensor_sub(
            B[4+tt].ap[0:1, :], B[4+tt].ap[0:1, :], rows[0:1, 0:512]), waits=[t_ms], fwaits=[t_e2], inc=vC)
        t_sq2 = P.emit("scalar", lambda s, tt=tt: s.activation(
            B[4+tt].ap[0:1, :], B[4+tt].ap[0:1, :], AF.Sqrt, bias=lneps_f),
            waits=[t_vr, t_cf], inc=aC)
        t_rs = P.emit("vector", lambda v, tt=tt: v.reciprocal(
            B[4+tt].ap[0:1, :], B[4+tt].ap[0:1, :]), waits=[t_sq2], inc=vC)
        t_a = P.emit("vector", lambda v, tt=tt: v.tensor_copy(
            arow_b[0:1, 512*tt:512*(tt+1)], B[4+tt].ap[0:1, :]), fwaits=[t_rs], inc=vC)
        t_mc = P.emit("vector", lambda v, tt=tt: v.tensor_copy(rows[0:1, 0:512], B[tt].ap[0:1, :]), inc=vC)
        t_b = P.emit("vector", lambda v, tt=tt: v.scalar_tensor_tensor(
            brow_b[0:1, 512*tt:512*(tt+1)], rows[0:1, 0:512], -1.0, B[4+tt].ap[0:1, :],
            ALU.mult, ALU.mult), waits=[t_mc], inc=vC)
        t_fin = t_b
        B[tt].rd = t_b
        B[4+tt].rd = t_b
    t_ab = t_fin
    t_bbb = t_fin
    bc_last = None
    for tt in range(4):
        t1 = P.emit("tensor", lambda t, tt=tt: t.matmul(
            B[tt].ap, onesrow_b, arow_b[0:1, 512*tt:512*(tt+1)], start=True, stop=True),
            waits=[t_ab, B[tt].rd], inc=tC)
        t2 = P.emit("tensor", lambda t, tt=tt: t.matmul(
            B[4+tt].ap, onesrow_b, brow_b[0:1, 512*tt:512*(tt+1)], start=True, stop=True),
            waits=[t_bbb, B[4+tt].rd], inc=tC)
        e1 = P.emit("vector", lambda v, tt=tt: v.tensor_copy(A128[:, 512*tt:512*(tt+1)], B[tt].ap), waits=[t1], inc=vC)
        e2 = P.emit("vector", lambda v, tt=tt: v.tensor_copy(B128[:, 512*tt:512*(tt+1)], B[4+tt].ap), waits=[t2], inc=vC)
        B[tt].rd, B[tt].wr = e1, t1
        B[4+tt].rd, B[4+tt].wr = e2, t2
        bc_last = e2
    gamma_pump(2)

    last_use = [None, None]
    sxn_toks = []
    xn_tok = {}
    for dc in range(NDC):
        bi = dc % 2
        xp = xpark[:, 2048*dc:2048*(dc+1)]
        hts = []
        for hfi, dst in ((0, xn_pre), (1, xn_own)):
            sl = slice(1024*hfi, 1024*(hfi+1))
            t_p1 = P.emit("vector", lambda v, bi=bi, sl=sl, xp=xp: v.tensor_tensor(
                x2b[bi][:, sl], xp[:, sl], A128[:, sl], ALU.mult), waits=[xpark_tok[dc], bc_last, last_use[bi]], inc=vC)
            t_p2 = P.emit("vector", lambda v, bi=bi, sl=sl, dst=dst, dc=dc: v.tensor_tensor(
                dst[:, 1024*dc:1024*(dc+1)], x2b[bi][:, sl], B128[:, sl], ALU.add),
                waits=[t_p1], inc=vC)
            hts.append(t_p2)
        xn_tok[dc] = hts
        t_r1 = P.emit("vector", lambda v, dc=dc: v.tensor_reduce(rtmp[:, 1:2], xn_pre[:, 1024*dc:1024*(dc+1)], AXX, ALU.add), waits=[hts[0]], inc=vC)
        t_r2 = P.emit("vector", lambda v, dc=dc: v.tensor_reduce(rtmp[:, 2:3], xn_own[:, 1024*dc:1024*(dc+1)], AXX, ALU.add), waits=[hts[1]], inc=vC)
        t_r3 = P.emit("vector", lambda v, dc=dc: v.tensor_tensor(sxn[:, dc:dc+1], rtmp[:, 1:2], rtmp[:, 2:3], ALU.add), fwaits=[t_r2], inc=vC)
        sxn_toks.append(t_r3)
        last_use[bi] = hts[1]
        gamma_pump(1)
    t_sxnb = P.emit("vector", lambda v: v.tensor_copy(sxn_b, sxn), waits=[sxn_toks[-1]], inc=vC)

    dbq = P.chan("dbq", dma=True)
    dbg_xn = want_dbg("xn", [NDC * 128, T])
    if dbg_xn is not None:
        for dc in range(NDC):
            P.dma("gpsimd", dbg_xn[128*dc:128*(dc+1), 0:1024], xn_pre[:, 1024*dc:1024*(dc+1)], waits=xn_tok[dc], inc=dbq)
            P.dma("gpsimd", dbg_xn[128*dc:128*(dc+1), 1024:2048], xn_own[:, 1024*dc:1024*(dc+1)], waits=xn_tok[dc], inc=dbq)
    dbg_sxn = want_dbg("sxn", [128, 16])
    if dbg_sxn is not None:
        P.dma("gpsimd", dbg_sxn, sxn, waits=sxn_toks, inc=dbq)
    dbg_a = want_dbg("a128", [128, 2048])
    if dbg_a is not None:
        P.dma("gpsimd", dbg_a, A128, waits=[bc_last], inc=dbq)
    dbg_bb = want_dbg("b128", [128, 2048])
    if dbg_bb is not None:
        P.dma("gpsimd", dbg_bb, B128, waits=[bc_last], inc=dbq)
    dbg_ar = want_dbg("arow", [1, 2048])
    if dbg_ar is not None:
        P.dma("gpsimd", dbg_ar, arow_b[0:1, :], waits=[t_fin], inc=dbq)
    dbg_br = want_dbg("brow", [1, 2048])
    if dbg_br is not None:
        P.dma("gpsimd", dbg_br, brow_b[0:1, :], waits=[t_fin], inc=dbq)
    if stop_after < 1:
        return _fin(nc, P, es, outd, dbg_d, locals())

    SC = Arena(SAap, 24 * 1024)
    wkb = [SC.alloc(2*512, BF16, f"wkb{i}") for i in range(8)]
    membuf = [SC.alloc(512, BF16, "mem0"), SC.alloc(512, BF16, "mem1")]
    assert SC.off <= 24 * 1024
    wkb_tok = [(vC, vC.n)] * 8
    qWk4 = [P.chan(f"qWk4_{i}", dma=True) for i in range(8)]

    def wslice(wd, g, dc0, ndc):
        off = (g * NDC + dc0) * 128 * 512
        return wd[off:off + ndc * 128 * 512].rearrange("(a p c) -> p a c", a=ndc, p=128)

    elu_state = {"last": [None] * 4}
    def elu_min(bk, par, extra=()):
        t_min = P.emit("vector", lambda v, bk=bk, par=par: v.tensor_scalar(
            elut[:, 512*par:512*(par+1)], bk.ap, 0.0, None, ALU.min),
            waits=[bk.wr, elu_state["last"][par]] + list(extra), inc=vC)
        t_exp = P.emit("scalar", lambda s, par=par: s.activation(
            elut[:, 512*par:512*(par+1)], elut[:, 512*par:512*(par+1)], AF.Exp),
            waits=[t_min], inc=aC)
        return t_exp
    def elu_fin(bk, dst, par, t_exp):
        t_f = P.emit("vector", lambda v, bk=bk, dst=dst, par=par: v.scalar_tensor_tensor(
            dst, bk.ap, 1.0, elut[:, 512*par:512*(par+1)], ALU.add, ALU.max),
            waits=[t_exp], inc=vC)
        elu_state["last"][par] = t_f
        bk.rd = t_f
        return t_f
    def elu_evac(bk, dst, par, extra=()):
        return elu_fin(bk, dst, par, elu_min(bk, par, extra))

    kp_rdy = [None] * 16
    vt_rdy = [None] * 16
    sq_rdy = [None] * 4
    group_end = None
    t_er = t_ier = None
    e_row = rows2[0:1, 16:20]
    ie_row = rows2[0:1, 20:24]
    t_w128 = None
    t_sqs = None

    dbg_kp = want_dbg("kp", [16*128, 512])
    dbg_vt = want_dbg("vt", [16*128, 512])
    dbg_sq = want_dbg("sigq", [4*128, 1024])
    dbg_w = want_dbg("wmh", [128, 16])
    dbg_comb = want_dbg("comb", [8*128, 2048])
    dbg_st = want_dbg("sst", [4*128, 129])

    ngroups = NG if stop_after >= 3 else 1
    for g in range(ngroups):
        prev_end = group_end
        for wd, dstbuf, is_k, rdy in ((wkd, kp, True, kp_rdy), (wvd, vt, False, vt_rdy)):
            for tcb in range(4):
                banks = [B[(tcb % 2) * 4 + i] for i in range(4)]
                first = [True] * 4
                mm_last = None
                for dcb in range(8):
                    bi = (tcb * 8 + dcb) % 8
                    t_w = P.dma("gpsimd", wkb[bi][:].rearrange("p (a c) -> p a c", a=2),
                                wslice(wd, g, 2*dcb, 2), waits=[wkb_tok[bi]], inc=qWk4[bi])
                    for dci in range(2):
                        dc = 2*dcb + dci
                        for i4 in range(4):
                            tcn = 4*tcb + i4
                            xsrc = xn_pre if tcn < 8 else xn_own
                            xcol = 1024*dc + 128*(tcn % 8)
                            bkk = banks[i4]
                            w_ = [t_w, xn_tok[dc][0 if tcn < 8 else 1]]
                            if first[i4]:
                                w_.append(bkk.rd)
                            mm_last = P.emit("tensor", lambda t, bkk=bkk, xsrc=xsrc, xcol=xcol, bi=bi, dci=dci, st=first[i4], sp=(dc == 15): t.matmul(
                                bkk.ap, xsrc[:, xcol:xcol+128], wkb[bi][:, 512*dci:512*(dci+1)],
                                start=st, stop=sp), waits=w_, inc=tC)
                            first[i4] = False
                            bkk.wr = mm_last
                    wkb_tok[bi] = mm_last
                if is_k:
                    exps = [elu_min(banks[i4], i4) for i4 in range(4)]
                    for i4 in range(4):
                        tcn = 4*tcb + i4
                        rdy[tcn] = elu_fin(banks[i4], dstbuf[:, 512*tcn:512*(tcn+1)], i4, exps[i4])
                else:
                    for i4 in range(4):
                        tcn = 4*tcb + i4
                        bkk = banks[i4]
                        dst = dstbuf[:, 512*tcn:512*(tcn+1)]
                        rdy[tcn] = P.emit("vector", lambda v, bkk=bkk, dst=dst: v.tensor_copy(dst, bkk.ap), waits=[bkk.wr], inc=vC)
                        bkk.rd = rdy[tcn]
                gamma_pump(2)
        if g == 0 and dbg_kp is not None:
            for tcn in range(16):
                P.dma("gpsimd", dbg_kp[128*tcn:128*(tcn+1), :], kp[:, 512*tcn:512*(tcn+1)], waits=[kp_rdy[tcn]], inc=dbq)
        if g == 0 and dbg_vt is not None:
            for tcn in range(16):
                P.dma("gpsimd", dbg_vt[128*tcn:128*(tcn+1), :], vt[:, 512*tcn:512*(tcn+1)], waits=[vt_rdy[tcn]], inc=dbq)

        sqbk = B[4]
        for p2 in (0, 1):
            qbs = [B[0], B[1], B[2], B[3]] if p2 == 0 else [B[5], B[6], B[7], B[0]]
            firstq = [True] * 4
            firstsq = True
            for dcb in range(8):
                bi = (p2 * 8 + dcb) % 8
                t_w = P.dma("gpsimd", wkb[bi][:].rearrange("p (a c) -> p a c", a=2),
                            wslice(wqd, g, 2*dcb, 2), waits=[wkb_tok[bi]], inc=qWk4[bi])
                mm_last = None
                for dci in range(2):
                    dc = 2*dcb + dci
                    for ei in range(2):
                        e128 = 2*p2 + ei
                        for ct in range(2):
                            qi = 2*ei + ct
                            bkk = qbs[qi]
                            w_ = [t_w, xn_tok[dc][1]]
                            if firstq[qi]:
                                w_.append(bkk.rd)
                            ecol = 512*dci + 128*(2*p2 + ei)
                            mm_last = P.emit("tensor", lambda t, bkk=bkk, bi=bi, ecol=ecol, dc=dc, ct=ct, st=firstq[qi], sp=(dc == 15): t.matmul(
                                bkk.ap, wkb[bi][:, ecol:ecol+128],
                                xn_own[:, 1024*dc + 512*ct:1024*dc + 512*(ct+1)],
                                start=st, stop=sp), waits=w_, inc=tC)
                            firstq[qi] = False
                            bkk.wr = mm_last
                            wkb_tok[bi] = mm_last
                    if p2 == 0:
                        w_ = [t_sxnb]
                        if firstsq:
                            w_.append(sqbk.rd)
                        mm_last = P.emit("tensor", lambda t, bi=bi, dci=dci, dc=dc, st=firstsq, sp=(dc == 15): t.matmul(
                            sqbk.ap[0:1, :], sxn_b[:, dc:dc+1], wkb[bi][:, 512*dci:512*(dci+1)],
                            start=st, stop=sp), waits=w_, inc=tC)
                        firstsq = False
                        sqbk.wr = mm_last
                        wkb_tok[bi] = mm_last
            exps = [elu_min(qbs[qi], qi) for qi in range(4)]
            for qi in range(4):
                e128 = 2*p2 + (qi // 2)
                ct = qi % 2
                sq_rdy[e128] = elu_fin(qbs[qi], sjq[:, 1024*e128 + 512*ct:1024*e128 + 512*(ct+1)], qi, exps[qi])
            if p2 == 0:
                t_cp = P.emit("vector", lambda v: v.tensor_copy(rows[0:1, 0:512], sqbk.ap[0:1, :]), waits=[sqbk.wr], inc=vC)
                sqbk.rd = t_cp
                tt_last = None
                for hh in range(4):
                    tt_last = P.emit("tensor", lambda t, hh=hh: t.transpose(
                        sqbk.ap[:, hh:hh+1], rows[0:1, 128*hh:128*(hh+1)], id_f[0:1, 0:1]),
                        waits=[t_cp, t_cf], inc=tC)
                sqbk.wr = tt_last
                t_sqs = P.emit("vector", lambda v: v.tensor_scalar(
                    sq_sb, sqbk.ap[:, 0:4], SCALE / T, None, ALU.mult), waits=[tt_last], inc=vC)
                sqbk.rd = t_sqs
        gamma_pump(4)
        if g == 0 and dbg_sq is not None:
            for hh in range(4):
                P.dma("gpsimd", dbg_sq[128*hh:128*(hh+1), :], sjq[:, 1024*hh:1024*(hh+1)], waits=[sq_rdy[hh]], inc=dbq)

        if g == 0:
            nst = []
            for m in range(M):
                nst.append(P.emit("vector", lambda v, m=m: v.tensor_reduce(
                    nf4[:, m:m+1], normstb[:, m:64:4], AXX, ALU.add), waits=[t_nrm], inc=vC))
            t_nsb = P.emit("vector", lambda v: v.tensor_copy(nsb, nf4), fwaits=[nst[-1]], inc=vC)
            t_nmm = P.emit("tensor", lambda t: t.matmul(B[1].ap[0:1, 0:4], ones_b, nsb, start=True, stop=True),
                           waits=[t_nsb, B[1].rd], inc=tC)
            B[1].wr = t_nmm
            t_er = P.emit("vector", lambda v: v.tensor_scalar(e_row, B[1].ap[0:1, 0:4], 1e-6, None, ALU.is_lt), waits=[t_nmm], inc=vC)
            B[1].rd = t_er
            t_ier = P.emit("vector", lambda v: v.tensor_scalar(ie_row, e_row, -1.0, 1.0, ALU.mult, ALU.add), waits=[t_er], inc=vC)
        t_P = P.emit("vector", lambda v, g=g: v.tensor_tensor(
            relP[:].rearrange("p (a b) -> p a b", a=4),
            normstb[:, 16*g:16*(g+1)].rearrange("p (a b) -> p a b", a=4),
            sq_sb[:].unsqueeze(2).broadcast_to([128, 4, 4]), ALU.mult), waits=[t_nrm], fwaits=[t_sqs], inc=vC)
        t_rmm = P.emit("tensor", lambda t: t.matmul(B[1].ap[0:1, 4:20], ones_b, relP, start=True, stop=True),
                       waits=[t_P, B[1].rd], inc=tC)
        B[1].wr = t_rmm
        RL = rows2[0:1, 32:48]
        RLv = RL.rearrange("p (a b) -> p a b", a=4)
        MX = rows2[0:1, 48:52]
        M2 = rows2[0:1, 52:56]
        t_rl = P.emit("vector", lambda v: v.tensor_tensor(
            RLv, B[1].ap[0:1, 4:20].rearrange("p (a b) -> p a b", a=4),
            ie_row.unsqueeze(1).broadcast_to([1, 4, 4]), ALU.mult), waits=[t_rmm, t_ier], inc=vC)
        B[1].rd = t_rl
        t_rl2 = P.emit("vector", lambda v: v.scalar_tensor_tensor(
            RLv, e_row.unsqueeze(1).broadcast_to([1, 4, 4]), -1e30, RLv, ALU.mult, ALU.add), fwaits=[t_rl], inc=vC)
        t_m1 = P.emit("vector", lambda v: v.tensor_tensor(MX, RL[:, 0:16:4], RL[:, 1:16:4], ALU.max), fwaits=[t_rl2], inc=vC)
        t_m2 = P.emit("vector", lambda v: v.tensor_tensor(M2, RL[:, 2:16:4], RL[:, 3:16:4], ALU.max), inc=vC)
        t_m3 = P.emit("vector", lambda v: v.tensor_tensor(MX, MX, M2, ALU.max), fwaits=[t_m2], inc=vC)
        t_sb = P.emit("vector", lambda v: v.tensor_tensor(
            RLv, RLv, MX.unsqueeze(2).broadcast_to([1, 4, 4]), ALU.subtract), fwaits=[t_m3], inc=vC)
        t_ex = P.emit("scalar", lambda s: s.activation(RL, RL, AF.Exp), waits=[t_sb], inc=aC)
        t_s1 = P.emit("vector", lambda v: v.tensor_tensor(MX, RL[:, 0:16:4], RL[:, 1:16:4], ALU.add), waits=[t_ex], inc=vC)
        t_s2 = P.emit("vector", lambda v: v.tensor_tensor(M2, RL[:, 2:16:4], RL[:, 3:16:4], ALU.add), inc=vC)
        t_s3 = P.emit("vector", lambda v: v.tensor_tensor(MX, MX, M2, ALU.add), fwaits=[t_s2], inc=vC)
        t_rc = P.emit("vector", lambda v: v.reciprocal(MX, MX), fwaits=[t_s3], inc=vC)
        t_wn = P.emit("vector", lambda v: v.tensor_tensor(
            RLv, RLv, MX.unsqueeze(2).broadcast_to([1, 4, 4]), ALU.mult), fwaits=[t_rc], inc=vC)
        t_wm = P.emit("vector", lambda v: v.tensor_tensor(
            rowsb[0:1, 0:16].rearrange("p (a b) -> p a b", a=4), RLv,
            ie_row.unsqueeze(1).broadcast_to([1, 4, 4]), ALU.mult), fwaits=[t_wn], inc=vC)
        t_wbc = P.emit("tensor", lambda t: t.matmul(B[1].ap[:, 32:48], onesrow_b, rowsb[0:1, 0:16], start=True, stop=True),
                       waits=[t_wm, B[1].rd], inc=tC)
        B[1].wr = t_wbc
        t_w128 = P.emit("vector", lambda v: v.tensor_copy(w128, B[1].ap[:, 32:48]), waits=[t_wbc], inc=vC)
        B[1].rd = t_w128
        if g == 0 and dbg_w is not None:
            P.dma("gpsimd", dbg_w, w128, waits=[t_w128], inc=dbq)

        memtok = [(tC, tC.n), (tC, tC.n)]
        t_sstb_all = None
        for pair in range(2):
            hpair = [2*pair, 2*pair+1]
            sst_tok = {}
            for hh in hpair:
                h = 4*g + hh
                par = hh % 2
                t_mem = P.dma("gpsimd", membuf[par], memd[h*128*512:(h+1)*128*512].rearrange("(p c) -> p c", p=128),
                              waits=[memtok[par]], inc=qMb[par])
                bS = B[6 + par]
                bZp = B[4 + par]
                mmz = None
                for j in range(8):
                    kcol = 512*j + 128*hh
                    w_ = [kp_rdy[j], vt_rdy[j]]
                    if j == 0:
                        w_.append(bS.rd)
                    P.emit("tensor", lambda t, bS=bS, kcol=kcol, st=(j == 0), sp=(j == 7): t.matmul(
                        bS.ap[:, 0:128], kp[:, kcol:kcol+128], vt[:, kcol:kcol+128],
                        start=st, stop=sp), waits=w_, inc=tC)
                    w2_ = [bZp.rd] if j == 0 else []
                    mmz = P.emit("tensor", lambda t, bZp=bZp, kcol=kcol, st=(j == 0), sp=(j == 7): t.matmul(
                        bZp.ap[:, 0:1], kp[:, kcol:kcol+128], ones_b,
                        start=st, stop=sp), waits=w2_, inc=tC)
                bS.wr = mmz
                bZp.wr = mmz
                t_sst = P.emit("scalar", lambda s, bS=bS, par=par: s.activation(
                    Sst[:, 132*par:132*par+128], bS.ap[:, 0:128], AF.Copy, scale=mask_f),
                    waits=[mmz, t_cf, prev_end], inc=aC)
                t_sstz = P.emit("scalar", lambda s, bZp=bZp, par=par: s.activation(
                    Sst[:, 132*par+128:132*par+129], bZp.ap[:, 0:1], AF.Copy, scale=mask_f),
                    inc=aC)
                bS.rd = t_sstz
                bZp.rd = t_sstz
                sst_tok[hh] = t_sstz
                t_sstb_all = P.emit("vector", lambda v, par=par: v.tensor_copy(
                    Sst_b[:, 132*par:132*par+129], Sst[:, 132*par:132*par+129]), waits=[t_sstz], inc=vC)
                if g == 0 and dbg_st is not None:
                    P.dma("gpsimd", dbg_st[128*hh:128*(hh+1), :], Sst[:, 132*par:132*par+129], waits=[t_sst], inc=dbq)

                bA = B[par]
                bN = B[2]
                nmm = None
                for sc in range(NSC):
                    w_ = [sq_rdy[hh], t_nrm]
                    if sc == 0:
                        w_.append(bN.rd)
                    nmm = P.emit("tensor", lambda t, sc=sc, hh=hh, g=g: t.matmul(
                        bN.ap[:, 32*hh + 4*sc:32*hh + 4*sc + 4],
                        sjq[:, 1024*hh + 128*sc:1024*hh + 128*sc + 128],
                        normstb[:, 4*(4*g+hh):4*(4*g+hh)+4], start=True, stop=True),
                        waits=w_, inc=tC)
                bN.wr = nmm
                t_mx = P.emit("vector", lambda v, hh=hh, bN=bN: v.tensor_scalar(
                    r_sc[:, 32*hh:32*(hh+1)], bN.ap[:, 32*hh:32*(hh+1)], EPS_ATT, None, ALU.max),
                    waits=[nmm], inc=vC)
                bN.rd = t_mx
                t_rp = P.emit("vector", lambda v, hh=hh: v.reciprocal(
                    r_sc[:, 32*hh:32*(hh+1)], r_sc[:, 32*hh:32*(hh+1)]), fwaits=[t_mx], inc=vC)
                t_rw = t_rp
                for m in range(M):
                    t_rw = P.emit("vector", lambda v, hh=hh, m=m: v.tensor_scalar(
                        r_sc[:, 32*hh+m:32*(hh+1):4], r_sc[:, 32*hh+m:32*(hh+1):4],
                        w128[:, 4*hh+m:4*hh+m+1], None, ALU.mult),
                        waits=[t_w128], fwaits=[t_rw] if m == 0 else [], inc=vC)
                t_drain = t_rw
                amm = None
                for sc in range(NSC):
                    for m in range(M):
                        w_ = [t_mem, sq_rdy[hh]]
                        if m == 0:
                            w_.append(bA.rd)
                        amm = P.emit("tensor", lambda t, sc=sc, m=m, hh=hh, bA=bA, par=par: t.matmul(
                            bA.ap[:, 128*m:128*(m+1)],
                            sjq[:, 1024*hh + 128*sc:1024*hh + 128*sc + 128],
                            membuf[par][:, 128*m:128*(m+1)], start=True, stop=True),
                            waits=w_, inc=tC)
                    bA.wr = amm
                    t_c = P.emit("vector", lambda v, sc=sc, hh=hh, par=par, bA=bA: v.tensor_scalar(
                        memo[:, 1024*par + 128*sc:1024*par + 128*(sc+1)], bA.ap[:, 0:128],
                        r_sc[:, 32*hh + 4*sc:32*hh + 4*sc + 1], None, ALU.mult),
                        waits=[amm], fwaits=[t_drain] if sc == 0 else [], inc=vC)
                    for m in range(1, M):
                        t_c = P.emit("vector", lambda v, sc=sc, hh=hh, par=par, m=m, bA=bA: v.scalar_tensor_tensor(
                            memo[:, 1024*par + 128*sc:1024*par + 128*(sc+1)], bA.ap[:, 128*m:128*(m+1)],
                            r_sc[:, 32*hh + 4*sc + m:32*hh + 4*sc + m + 1],
                            memo[:, 1024*par + 128*sc:1024*par + 128*(sc+1)], ALU.mult, ALU.add),
                            fwaits=[t_c], inc=vC)
                    bA.rd = t_c
                memtok[par] = amm

            state_tok = {hh: None for hh in hpair}
            zu_tok = {hh: None for hh in hpair}
            for j in range(NSC):
                for hh in hpair:
                    par = hh % 2
                    h = 4*g + hh
                    tcn = 8 + j
                    kcol = 512*tcn + 128*hh
                    qcol = 1024*hh + 128*j
                    bT = B[par]; bSc = B[2 + par]; bNm = B[4 + par]; bOw = B[6 + par]
                    t_t1 = P.emit("tensor", lambda t, bT=bT, kcol=kcol: t.transpose(
                        bT.ap.bitcast(BF16)[:, 0:128], kp[:, kcol:kcol+128], id_b),
                        waits=[bT.rd, kp_rdy[tcn]], inc=tC)
                    bT.wr = t_t1
                    t_kpf = P.emit("vector", lambda v, bT=bT, par=par: v.tensor_copy(
                        kpf[:, 128*par:128*(par+1)], bT.ap.bitcast(BF16)[:, 0:128]), waits=[t_t1], inc=vC)
                    bT.rd = t_kpf
                    t_t2 = P.emit("tensor", lambda t, bSc=bSc, par=par, qcol=qcol: t.matmul(
                        bSc.ap[:, 0:128], kpf[:, 128*par:128*(par+1)], sjq[:, qcol:qcol+128],
                        start=True, stop=True), waits=[t_kpf, bSc.rd, sq_rdy[hh]], inc=tC)
                    bSc.wr = t_t2
                    t_scm = P.emit("vector", lambda v, bSc=bSc, par=par: v.tensor_tensor(
                        scm[:, 128*par:128*(par+1)], bSc.ap[:, 0:128], tril_f, ALU.mult),
                        waits=[t_t2], inc=vC)
                    bSc.rd = t_scm
                    if state_tok[hh] is None:
                        Sop = Sst_b[:, 132*par:132*par+129]
                        s_tok = t_sstb_all
                    else:
                        Sop = Scur_b[:, 132*par:132*par+129]
                        s_tok = state_tok[hh]
                    t_t3 = P.emit("tensor", lambda t, bNm=bNm, qcol=qcol, Sop=Sop: t.matmul(
                        bNm.ap[:, 0:128], sjq[:, qcol:qcol+128], Sop[:, 0:128],
                        start=True, stop=False), waits=[s_tok, bNm.rd], inc=tC)
                    t_t4 = P.emit("tensor", lambda t, bNm=bNm, par=par, kcol=kcol: t.matmul(
                        bNm.ap[:, 0:128], scm[:, 128*par:128*(par+1)], vt[:, kcol:kcol+128],
                        start=False, stop=True), waits=[t_scm, vt_rdy[tcn]], inc=tC)
                    t_t5 = P.emit("tensor", lambda t, bNm=bNm, qcol=qcol, Sop=Sop: t.matmul(
                        bNm.ap[:, 128:129], sjq[:, qcol:qcol+128], Sop[:, 128:129],
                        start=True, stop=False), inc=tC)
                    t_t6 = P.emit("tensor", lambda t, bNm=bNm, par=par: t.matmul(
                        bNm.ap[:, 128:129], scm[:, 128*par:128*(par+1)], ones_b,
                        start=False, stop=True), inc=tC)
                    bNm.wr = t_t6
                    w_ = [kp_rdy[tcn], vt_rdy[tcn]]
                    if j == 0:
                        w_.append(bOw.rd)
                    t_t7 = P.emit("tensor", lambda t, bOw=bOw, kcol=kcol: t.matmul(
                        bOw.ap[:, 0:128], kp[:, kcol:kcol+128], vt[:, kcol:kcol+128],
                        start=True, stop=True), waits=w_, inc=tC)
                    bOw.wr = t_t7
                    if j == 0:
                        t_zu = P.emit("vector", lambda v, par=par: v.tensor_reduce(
                            zacc[:, 2*par:2*par+1], kpf[:, 128*par:128*(par+1)], AXX, ALU.add),
                            fwaits=[t_kpf], inc=vC)
                    else:
                        t_zr = P.emit("vector", lambda v, par=par: v.tensor_reduce(
                            denb[:, 2*par+1:2*par+2], kpf[:, 128*par:128*(par+1)], AXX, ALU.add),
                            fwaits=[t_kpf, zu_tok[hh]], inc=vC)
                        t_zu = P.emit("vector", lambda v, par=par: v.tensor_tensor(
                            zacc[:, 2*par:2*par+1], zacc[:, 2*par:2*par+1], denb[:, 2*par+1:2*par+2], ALU.add),
                            fwaits=[t_zr], inc=vC)
                    zu_tok[hh] = t_zu
                    if j < NSC - 1:
                        if j == 0:
                            t_su = P.emit("vector", lambda v, par=par, bOw=bOw: v.tensor_tensor(
                                scur32[:, 128*par:128*(par+1)], Sst[:, 132*par:132*par+128], bOw.ap[:, 0:128], ALU.add),
                                waits=[t_t7, sst_tok[hh]], inc=vC)
                        else:
                            t_su = P.emit("vector", lambda v, par=par, bOw=bOw: v.tensor_tensor(
                                scur32[:, 128*par:128*(par+1)], scur32[:, 128*par:128*(par+1)], bOw.ap[:, 0:128], ALU.add),
                                waits=[t_t7], fwaits=[state_tok[hh]], inc=vC)
                        t_sub = P.emit("vector", lambda v, par=par: v.tensor_copy(
                            Scur_b[:, 132*par:132*par+128], scur32[:, 128*par:128*(par+1)]), fwaits=[t_su], inc=vC)
                        t_suz = P.emit("vector", lambda v, par=par: v.tensor_tensor(
                            Scur_b[:, 132*par+128:132*par+129], Sst[:, 132*par+128:132*par+129], zacc[:, 2*par:2*par+1], ALU.add),
                            fwaits=[t_zu, t_sub], inc=vC)
                        state_tok[hh] = t_suz
                        bOw.rd = t_su
                    t_d1 = P.emit("vector", lambda v, bNm=bNm, par=par: v.tensor_scalar(
                        denb[:, 2*par:2*par+1], bNm.ap[:, 128:129], EPS_ATT, None, ALU.max),
                        waits=[t_t6], inc=vC)
                    t_d2 = P.emit("vector", lambda v, par=par: v.reciprocal(
                        denb[:, 2*par:2*par+1], denb[:, 2*par:2*par+1]), fwaits=[t_d1], inc=vC)
                    t_d3 = P.emit("gpsimd", lambda g, par=par, h=h: g.tensor_scalar(
                        denb[:, 2*par:2*par+1], denb[:, 2*par:2*par+1], gi128[:, h:h+1], None, ALU.mult),
                        waits=[t_gi, t_d2], inc=gC)
                    ccol = 2048*j + 128*h
                    t_c1 = P.emit("vector", lambda v, bNm=bNm, par=par, ccol=ccol: v.tensor_scalar(
                        comb[:, ccol:ccol+128], bNm.ap[:, 0:128], denb[:, 2*par:2*par+1], None, ALU.mult),
                        waits=[t_d3], inc=vC)
                    t_c2 = P.emit("vector", lambda v, par=par, j=j, h=h, ccol=ccol: v.scalar_tensor_tensor(
                        comb[:, ccol:ccol+128], memo[:, 1024*par + 128*j:1024*par + 128*(j+1)],
                        g128[:, h:h+1], comb[:, ccol:ccol+128], ALU.mult, ALU.add),
                        waits=[t_g], fwaits=[t_c1], inc=vC)
                    bNm.rd = t_c2
                gamma_pump(1)
        group_end = P.emit("vector", lambda v: v.tensor_copy(rtmp[:, 3:4], cf[:, 0:1]), waits=[t_cf], fwaits=[group_end], inc=vC)

    dbg_can = want_dbg("can", [128, 16])
    P.barrier("gpsimd", [group_end])
    if dbg_can is not None:
        P.dma("gpsimd", dbg_can[:, 0:4], rtmp, waits=[group_end], inc=dbq)
    if dbg_comb is not None:
        for sc in range(NSC):
            P.dma("gpsimd", dbg_comb[128*sc:128*(sc+1), :], comb[:, 2048*sc:2048*(sc+1)], waits=[group_end], inc=dbq)
    if stop_after < 4:
        return _fin(nc, P, es, outd, dbg_d, locals())

    SD = Arena(SAap, 24 * 1024)
    wob = [SD.alloc(2048, BF16, "wob0"), SD.alloc(2048, BF16, "wob1")]
    for sc in range(NSC):
        for ec in range(16):
            par = (sc * 16 + ec) % 2
            bT = B[par]
            t_tp = P.emit("tensor", lambda t, bT=bT, sc=sc, ec=ec: t.transpose(
                bT.ap.bitcast(BF16)[:, 0:128], comb[:, 2048*sc + 128*ec:2048*sc + 128*(ec+1)], id_b),
                waits=[bT.rd, group_end], inc=tC)
            bT.wr = t_tp
            t_ev = P.emit("vector", lambda v, bT=bT, sc=sc, ec=ec: v.tensor_copy(
                comb_fm[:, 1024*ec + 128*sc:1024*ec + 128*(sc+1)], bT.ap.bitcast(BF16)[:, 0:128]),
                waits=[t_tp], inc=vC)
            bT.rd = t_ev
    cfm_all = P.emit("vector", lambda v: v.tensor_copy(rtmp[:, 3:4], cf[:, 0:1]), fwaits=[group_end], inc=vC)

    wo_tok = [(tC, tC.n), (tC, tC.n)]
    attn_rdy = {}
    for fc in range(16):
        bi = fc % 2
        t_w = P.dma("gpsimd", wob[bi], wod[fc*128*2048:(fc+1)*128*2048].rearrange("(p c) -> p c", p=128),
                    waits=[wo_tok[bi]], inc=qWo[bi])
        banks = [B[2 + 2*(fc % 2)], B[3 + 2*(fc % 2)]]
        mm_last = None
        for ct in range(2):
            first = True
            for ec in range(16):
                w_ = [t_w, cfm_all]
                if first:
                    w_.append(banks[ct].rd)
                mm_last = P.emit("tensor", lambda t, bkk=banks[ct], bi=bi, ec=ec, ct=ct, st=first, sp=(ec == 15): t.matmul(
                    bkk.ap, wob[bi][:, 128*ec:128*(ec+1)],
                    comb_fm[:, 1024*ec + 512*ct:1024*ec + 512*(ct+1)],
                    start=st, stop=sp), waits=w_, inc=tC)
                first = False
            banks[ct].wr = mm_last
        wo_tok[bi] = mm_last
        for ct in range(2):
            t_ev = P.emit("vector", lambda v, bkk=banks[ct], fc=fc, ct=ct: v.tensor_copy(
                attn_fm[:, 1024*fc + 512*ct:1024*fc + 512*(ct+1)], bkk.ap), waits=[banks[ct].wr], inc=vC)
            banks[ct].rd = t_ev
            attn_rdy[fc] = t_ev
    P.barrier("gpsimd", [attn_rdy[15]])
    if dbg_can is not None:
        P.dma("gpsimd", dbg_can[:, 4:8], rtmp, waits=[attn_rdy[15]], inc=dbq)
    dbg_attn = want_dbg("attn", [16*128, 1024])
    if dbg_attn is not None:
        for fc in range(16):
            P.dma("gpsimd", dbg_attn[128*fc:128*(fc+1), :], attn_fm[:, 1024*fc:1024*(fc+1)], waits=[attn_rdy[fc]], inc=dbq)
    if stop_after < 5:
        return _fin(nc, P, es, outd, dbg_d, locals())

    gamma_pump(10**6)
    t_gr3 = None
    for j in range(3):
        t_gr3 = P.emit("vector", lambda v, j=j: v.tensor_reduce(
            gsum[:, j:j+1], gacc[:, 64*j:64*(j+1)], AXX, ALU.add),
            waits=[(aC, gst["tok"][1])] if gst["tok"] else [], fwaits=[t_gr3], inc=vC)
    t_tg = P.emit("tensor", lambda t: t.transpose(B[6].ap[0:3, 0:128], gsum[:, 0:3], id_f),
                  waits=[t_gr3, B[6].rd, t_cf], inc=tC)
    B[6].wr = t_tg
    t_gr = P.emit("vector", lambda v: v.tensor_reduce(rows2[0:3, 24:25], B[6].ap[0:3, 0:128], AXX, ALU.add),
                  waits=[t_tg], inc=vC)
    B[6].rd = t_gr
    t_th = P.emit("vector", lambda v: v.tensor_scalar(rows2[0:3, 25:26], rows2[0:3, 24:25], 0.5/NFFN, 0.5e-8, ALU.mult, ALU.add),
                  fwaits=[t_gr], inc=vC)
    t_tb = P.emit("tensor", lambda t: t.transpose(B[6].ap[0:1, 4:7], rows2[0:3, 25:26], id_f[0:3, 0:3]),
                  waits=[t_th, B[6].rd], inc=tC)
    B[6].wr = t_tb
    t_cpr = P.emit("vector", lambda v: v.tensor_copy(rows2[0:1, 26:29], B[6].ap[0:1, 4:7]), waits=[t_tb], inc=vC)
    B[6].rd = t_cpr
    t_bcm = P.emit("tensor", lambda t: t.matmul(B[6].ap[:, 8:11], onesrow_f, rows2[0:1, 26:29], start=True, stop=True),
                   waits=[t_cpr, B[6].rd], inc=tC)
    B[6].wr = t_bcm
    t_th128 = P.emit("vector", lambda v: v.tensor_copy(th128[:, 0:3], B[6].ap[:, 8:11]), waits=[t_bcm], inc=vC)
    B[6].rd = t_th128
    t_thn = P.emit("vector", lambda v: v.tensor_scalar(thn128[:, 0:3], th128[:, 0:3], -1.0, None, ALU.mult), fwaits=[t_th128], inc=vC)
    dbg_th = want_dbg("th", [128, 4])
    if dbg_th is not None:
        P.dma("gpsimd", dbg_th, th128, waits=[t_th128], inc=dbq)

    SE = Arena(SAap, 24 * 1024)
    wf = [[SE.alloc(512, F32, f"wf{m}{i}") for i in range(2)] for m in range(2)]
    qbb = [[SE.alloc(2048, BF16, f"qb{m}{i}") for i in range(2)] for m in range(2)]
    ab_q = [[memo[:, 1024*m + 512*i:1024*m + 512*(i+1)] for i in range(2)] for m in range(2)]
    ab_q2 = [[elut[:, 1024*m + 512*i:1024*m + 512*(i+1)] for i in range(2)] for m in range(2)]
    assert SE.off <= 24 * 1024

    wo_done = wo_tok[0] if wo_tok[0][1] > wo_tok[1][1] else wo_tok[1]
    qhz = {"lastQ": [None] * 8}
    def quantize(matj, wfb, qdst, tok_ld, abuf, abuf2, guard=None, lane=0):
        t_a = P.emit("scalar", lambda s: s.activation(abuf, wfb, AF.Sign, bias=thn128[:, matj:matj+1]),
                     waits=[tok_ld, t_thn, qhz["lastQ"][lane]], inc=aC)
        t_b = P.emit("scalar", lambda s: s.activation(abuf2, wfb, AF.Sign, bias=th128[:, matj:matj+1]),
                     waits=[], inc=aC)
        t_q = P.emit("vector", lambda v: v.tensor_tensor(qdst, abuf, abuf2, ALU.add),
                     waits=[t_b, guard], inc=vC)
        qhz["lastQ"][lane] = t_q
        return t_q
    hid_rdy = {}
    wf_tok = [[(tC, tC.n), (tC, tC.n)], [(tC, tC.n), (tC, tC.n)]]
    q_tok = [[None, None], [None, None]]
    dbg_q1 = want_dbg("q1", [256, 2048])
    qb_guard = [[(tC, tC.n), (tC, tC.n)], [(tC, tC.n), (tC, tC.n)]]
    sg_guard = [None, None]
    for hc in range(NHC):
        x1b = [B[0], B[2]]
        x2bk = [B[1], B[3]]
        qtoks = [[], []]
        qb = [qbb[0][hc % 2], qbb[1][hc % 2]]
        for matj, (wd, qbm) in enumerate(((w1d, qb[0]), (w2d, qb[1]))):
            for qt in range(4):
                bi = qt % 2
                t_ld = P.dma("sync", wf[matj][bi],
                             wd[hc*16*128*128:(hc+1)*16*128*128].rearrange("(p c) -> p c", p=128)[:, 512*qt:512*(qt+1)],
                             waits=[wf_tok[matj][bi]], inc=qWf[matj][bi])
                abp = ab_q[matj] if qt % 2 == 0 else ab_q2[matj]
                t_q = quantize(matj, wf[matj][bi], qbm[:, 512*qt:512*(qt+1)], t_ld, abp[0], abp[1], guard=qb_guard[matj][hc % 2], lane=matj * 2 + qt % 2)
                wf_tok[matj][bi] = t_q
                qtoks[matj].append(t_q)
        mm_last = None
        for ct in range(2):
            for matj in range(2):
                bkk = x1b[ct] if matj == 0 else x2bk[ct]
                first = True
                for dc in range(NDC):
                    w_ = [qtoks[matj][dc // 4], xn_tok[dc][1]]
                    if first:
                        w_.append(bkk.rd)
                    mm_last = P.emit("tensor", lambda t, bkk=bkk, qbm_=qb[matj], dc=dc, ct=ct, st=first, sp=(dc == 15): t.matmul(
                        bkk.ap, qbm_[:, 128*dc:128*(dc+1)],
                        xn_own[:, 1024*dc + 512*ct:1024*dc + 512*(ct+1)],
                        start=st, stop=sp), waits=w_, inc=tC)
                    first = False
                    bkk.wr = mm_last
        for _p in range(2):
            qb_guard[0][_p] = mm_last
            qb_guard[1][_p] = mm_last
        for ct in range(2):
            t_sg = P.emit("scalar", lambda s, ct=ct: s.activation(
                sgb[:, 512*ct:512*(ct+1)], x1b[ct].ap, AF.Sigmoid, scale=half_f), waits=[x1b[ct].wr, t_cf, sg_guard[ct]], inc=aC)
            x1b[ct].rd = t_sg
            t_h = P.emit("vector", lambda v, ct=ct, hc=hc: v.scalar_tensor_tensor(
                hidden[:, 1024*hc + 512*ct:1024*hc + 512*(ct+1)], sgb[:, 512*ct:512*(ct+1)],
                0.5, x2bk[ct].ap, ALU.mult, ALU.mult), waits=[t_sg, x2bk[ct].wr, wo_done], inc=vC)
            x2bk[ct].rd = t_h
            sg_guard[ct] = t_h
            hid_rdy[(hc, ct)] = t_h
    P.barrier("gpsimd", [hid_rdy[(NHC-1, 1)]])
    if dbg_can is not None:
        P.dma("gpsimd", dbg_can[:, 8:12], rtmp, waits=[hid_rdy[(NHC-1, 1)]], inc=dbq)
    dbg_hid = want_dbg("hid", [44*128, 1024])
    if dbg_hid is not None:
        for hc in range(NHC):
            P.dma("gpsimd", dbg_hid[128*hc:128*(hc+1), :], hidden[:, 1024*hc:1024*(hc+1)],
                  waits=[hid_rdy[(hc, 0)], hid_rdy[(hc, 1)]], inc=dbq)
    if stop_after < 6:
        return _fin(nc, P, es, outd, dbg_d, locals())

    SF = Arena(ar_t[:, 32 * 1024:64 * 1024], 32 * 1024)
    q3b = [SF.alloc(5632, BF16, "q3a"), SF.alloc(5632, BF16, "q3b")]
    assert SF.off <= 32 * 1024
    SFO = Arena(SAap, 24 * 1024)
    ost = [SFO.alloc(1024, F32, "ost0"), SFO.alloc(1024, F32, "ost1")]
    xot = [SFO.alloc(1024, BF16, "xot0"), SFO.alloc(1024, BF16, "xot1")]
    w3f = [SFO.alloc(704, F32, "w3f0"), SFO.alloc(704, F32, "w3f1")]
    ab3p = [[SFO.alloc(704, BF16, f"ab3{i}{j}") for j in range(2)] for i in range(2)]
    assert SFO.off <= 24 * 1024
    gE = (gC, gC.n)
    vE = (vC, vC.n)

    w3_tok = [(tC, tC.n), (tC, tC.n)]
    xo_tok = [None, None]
    ost_tok = [None, None]
    last_out = None
    phaseE_done = mm_last
    q3_guard = [(tC, tC.n), (tC, tC.n)]
    for fc in range(16):
        q3 = q3b[fc % 2]
        qq_toks = []
        for qt in range(8):
            bi = qt % 2
            t_ld = P.dma("sync", w3f[bi],
                         w3d[fc*44*128*128:(fc+1)*44*128*128].rearrange("(p c) -> p c", p=128)[:, 704*qt:704*(qt+1)],
                         waits=[w3_tok[bi], phaseE_done], inc=qW3[bi])
            t_q = quantize(2, w3f[bi], q3[:, 704*qt:704*(qt+1)], t_ld, ab3p[bi][0], ab3p[bi][1], guard=q3_guard[fc % 2], lane=4 + bi)
            w3_tok[bi] = t_q
            qq_toks.append(t_q)
        banks = [B[4 + 2*(fc % 2)], B[5 + 2*(fc % 2)]]
        mm3 = None
        for ct in range(2):
            first = True
            for hcc in range(NHC):
                w_ = [qq_toks[(128 * hcc + 127) // 704], hid_rdy[(hcc, ct)]]
                if first:
                    w_.append(banks[ct].rd)
                mm3 = P.emit("tensor", lambda t, bkk=banks[ct], q3_=q3, hcc=hcc, ct=ct, st=first, sp=(hcc == NHC-1): t.matmul(
                    bkk.ap, q3_[:, 128*hcc:128*(hcc+1)],
                    hidden[:, 1024*hcc + 512*ct:1024*hcc + 512*(ct+1)],
                    start=st, stop=sp), waits=w_, inc=tC)
                first = False
            banks[ct].wr = mm3
        q3_guard[fc % 2] = mm3
        bi = fc % 2
        t_xo = P.dma("gpsimd", xot[bi], xview(fc)[:, 1024:2048], waits=[xo_tok[bi], phaseE_done, gE, vE], inc=qXo[bi])
        for ct in range(2):
            t_a1 = P.emit("vector", lambda v, bkk=banks[ct], fc=fc, ct=ct, bi=bi: v.scalar_tensor_tensor(
                ost[bi][:, 512*ct:512*(ct+1)], bkk.ap, 0.5, attn_fm[:, 1024*fc + 512*ct:1024*fc + 512*(ct+1)],
                ALU.mult, ALU.add),
                waits=[banks[ct].wr, attn_rdy[fc], ost_tok[bi], gE], inc=vC)
            banks[ct].rd = t_a1
            t_a2 = P.emit("gpsimd", lambda g, fc=fc, ct=ct, bi=bi: g.tensor_tensor(
                ost[bi][:, 512*ct:512*(ct+1)], ost[bi][:, 512*ct:512*(ct+1)],
                xot[bi][:, 512*ct:512*(ct+1)], ALU.add), waits=[t_a1, t_xo], inc=gC)
        xo_tok[bi] = t_a2
        last_out = P.dma("sync", outd[fc*128*TO:(fc+1)*128*TO].rearrange("(p c) -> p c", p=128),
                         ost[bi], waits=[t_a2], inc=qO[bi])
        ost_tok[bi] = last_out

    if dbg_can is not None:
        P.dma("gpsimd", dbg_can[:, 12:16], rtmp, waits=[last_out], inc=dbq)
    P.barrier("sync", [(qO[0], qO[0].n), (qO[1], qO[1].n)])
    return _fin(nc, P, es, outd, dbg_d, locals())


def _fin(nc, P, es, outd, dbg_d, lcl):
    es2 = ExitStack()
    for ch in P.chans:
        ch.sem = es2.enter_context(nc.semaphore(ch.name))
    with es2:
        with nc.Block() as block:
            P.finalize(block)
    es.close()
    return nc, dbg_d


def host_inputs(full, core):
    b, half = core // 2, core % 2
    x = np.asarray(full["L0_attn_hidden_states"])
    own = x[b, 1024*half:1024*half+1024]
    pre = x[b, 1024*(1-half):1024*(1-half)+1024]
    ctx = np.concatenate([pre, own], axis=0)
    xctx = np.ascontiguousarray(ctx.T).reshape(-1)
    def wtile(w):
        return np.ascontiguousarray(np.asarray(w).T.reshape(16, 128, 4, 512).transpose(2, 0, 1, 3)).reshape(-1)
    wo = np.asarray(full["L0_attn_w_o"]).T.reshape(16, 128, 16, 128).transpose(2, 1, 0, 3)
    w1 = np.asarray(full["L0_ffn_w1"]).T.reshape(16, 128, 44, 128).transpose(2, 1, 0, 3)
    w2 = np.asarray(full["L0_ffn_w2"]).T.reshape(16, 128, 44, 128).transpose(2, 1, 0, 3)
    w3 = np.asarray(full["L0_ffn_w3"]).T.reshape(44, 128, 16, 128).transpose(2, 1, 0, 3)
    mems = np.asarray(full["L0_attn_memories"]).transpose(1, 2, 0, 3)
    normst = np.ascontiguousarray(np.asarray(full["L0_attn_memory_norms"]).transpose(2, 1, 0).reshape(128, 64))
    nt = np.asarray(full["L0_attn_memory_norms"]).transpose(2, 1, 0)
    normst = np.ascontiguousarray(nt.reshape(128, 64))
    gater = np.tile(np.asarray(full["L0_attn_gate"]).reshape(1, 16), (128, 1)).astype(np.float32)
    cstf = np.zeros((128, 390), np.float32)
    cstf[:, 0:128] = np.eye(128)
    cstf[:, 128:256] = 1.0
    cstf[:, 257] = float(half)
    cstf[0, 258] = LN_EPS
    cstf[:, 259] = 0.5
    jj, ii = np.mgrid[0:128, 0:128]
    cstf[:, 260:388] = (jj <= ii).astype(np.float32)
    cstb = np.zeros((128, 386), np.float32)
    cstb[:, 0] = 1.0
    cstb[:, 1:129] = np.eye(128)
    cstb[:, 129:257] = (jj <= ii)
    cstb[:, 257:385] = 1.0
    import ml_dtypes
    return {
        "xctx": xctx.astype(np.float32),
        "wq": wtile(full["L0_attn_w_q"]).astype(np.float32),
        "wk": wtile(full["L0_attn_w_k"]).astype(np.float32),
        "wv": wtile(full["L0_attn_w_v"]).astype(np.float32),
        "wo": np.ascontiguousarray(wo).reshape(-1).astype(np.float32),
        "w1": np.ascontiguousarray(w1).reshape(-1).astype(np.float32),
        "w2": np.ascontiguousarray(w2).reshape(-1).astype(np.float32),
        "w3": np.ascontiguousarray(w3).reshape(-1).astype(np.float32),
        "mems": np.ascontiguousarray(mems).reshape(-1).astype(np.float32),
        "normst": normst.astype(np.float32),
        "gater": gater,
        "cstf": cstf,
        "cstb": cstb.astype(ml_dtypes.bfloat16),
    }


def kernel(**inputs):
    from concourse.bass_utils import run_bass_kernel_spmd
    nc, _ = build(dbg_names=(), stop_after=99)
    in_maps = [host_inputs(inputs, c) for c in range(8)]
    res = run_bass_kernel_spmd(nc, in_maps, core_ids=list(range(8)))
    out = np.zeros((4, 2048, 2048), dtype=np.float32)
    for c in range(8):
        b, half = c // 2, c % 2
        o = np.asarray(res.results[c]["out"]).reshape(2048, 1024)
        out[b, 1024*half:1024*(half+1), :] = o.T
    return out
